# version 04
# baseline (speedup 1.0000x reference)
import numpy as np
import contextlib
import concourse.bass as bass
import concourse.mybir as mybir

F32 = mybir.dt.float32
BF16 = mybir.dt.bfloat16
AF = mybir.ActivationFunctionType
ALU = mybir.AluOpType
AX = mybir.AxisListType

D = 1024
NB = 8
H = 16
HD = 64
NEG = -1.0e5
EPS = 1e-5
POOL = 68


def tsl(i, n):
    return slice(i * n, (i + 1) * n)


class Ctx:
    def __init__(self, nc, stack, S):
        self.nc = nc
        self.stack = stack
        self.S = S
        self.NT = S // 512
        self.NTT = S // 128
        self.NBLK = S // 256
        self.pool = [stack.enter_context(nc.semaphore(f"sp{i}")) for i in range(POOL)]
        self.pool_cnt = [0] * POOL
        self.next_slot = 1
        self.names = {}
        self.bar_cnt = 0
        self.names["bar"] = 0
        self.bar = self.pool[0]
        self.n_swdge = 0

    def begin_phase_sems(self):
        self.next_slot = 1

    def sem(self, name):
        i = self.next_slot
        self.next_slot += 1
        assert self.next_slot <= POOL, "sem pool exhausted"
        self.names[name] = i
        return self.pool[i]

    def swdge_sem(self, name):
        s = self.stack.enter_context(self.nc.semaphore(name))
        self.pool.append(s)
        self.pool_cnt.append(0)
        self.names[name] = len(self.pool) - 1
        self.n_swdge += 1
        assert self.n_swdge <= 35, "too many swdge sems"
        return s

    def s(self, name):
        return self.pool[self.names[name]]

    def inc(self, name, by=1):
        i = self.names[name]
        self.pool_cnt[i] += by
        return self.pool_cnt[i]

    def val(self, name):
        return self.pool_cnt[self.names[name]]

    def phase(self, name):
        return Phase(self, name)


class Phase:
    def __init__(self, ctx, name):
        self.ctx = ctx
        self.name = name
        self.fns = []

    def tensor(self, f): self.fns.append(("tensor", f))
    def vector(self, f): self.fns.append(("vector", f))
    def scalar(self, f): self.fns.append(("scalar", f))
    def gpsimd(self, f): self.fns.append(("gpsimd", f))
    def sync(self, f): self.fns.append(("sync", f))

    def emit(self):
        ctx = self.ctx
        nc = ctx.nc
        bar = ctx.bar
        enter = ctx.bar_cnt
        with nc.Block() as block:
            for eng_name, f in self.fns:
                def run(eng, f=f):
                    if enter:
                        eng.wait_ge(bar, enter)
                    last = f(eng)
                    if isinstance(last, tuple):
                        last = [last]
                    if isinstance(last, list):
                        for sname, v in last:
                            eng.wait_ge(ctx.s(sname), v)
                    eng.sem_inc(bar, 1)
                getattr(block, eng_name)(run)
        ctx.bar_cnt += len(self.fns)
        if not hasattr(ctx, "phase_log"):
            ctx.phase_log = []
        ctx.phase_log.append((self.name, {n: (ctx.names[n], ctx.pool_cnt[ctx.names[n]]) for n in list(ctx.names)}))
        ctx.begin_phase_sems()


class Rot:

    def __init__(self, ctx, tag, k, n_per=1, swdge=False, plan=None):
        self.ctx = ctx
        self.k = k
        self.n_per = n_per
        self.plan = plan
        self.nm = [tag + str(j) for j in range(k)]
        self.base = []
        for n in self.nm:
            (ctx.swdge_sem(n) if swdge else ctx.sem(n))
            self.base.append(ctx.val(n))
        self.issued = {}

    def value(self, i):
        if self.plan is not None:
            tot = sum(self.plan[j] for j in range(i % self.k, i + 1, self.k))
            return self.base[i % self.k] + 16 * tot
        return self.base[i % self.k] + 16 * self.n_per * (i // self.k + 1)

    def issue(self, i, ins):
        nm = self.nm[i % self.k]
        v = self.ctx.inc(nm, 16)
        ins.then_inc(self.ctx.s(nm), 16)
        self.issued[i] = self.issued.get(i, 0) + 1
        assert v <= self.value(i), f"Rot {nm}: over-issued use {i}"
        return v

    def wait(self, eng, i):
        eng.wait_ge(self.ctx.s(self.nm[i % self.k]), self.value(i))

    def wait_if(self, eng, i):
        self.wait(eng, i)

    def final(self):
        return [(n, self.ctx.val(n)) for n in self.nm if self.ctx.val(n) > self.base[j]
                ] if False else [(n, self.ctx.val(n)) for j, n in enumerate(self.nm)
                                 if self.ctx.val(n) > self.base[j]]


def vfence(v, consts):
    return v.tensor_copy(consts["fence"][0:1, 0:1], consts["scal"][0:1, 0:1])


def sfence(s, consts):
    return s.copy(consts["fence"][0:1, 1:2], consts["scal"][0:1, 0:1])


def wait_all(eng, ctx, rots):
    for r in rots:
        for n, v in r.final():
            eng.wait_ge(ctx.s(n), v)


def p_nat_to_t(ctx, src_nat, dst32, dst16, consts, tag):
    nc = ctx.nc
    NTT = ctx.NTT
    GP = 4
    tp = ctx.sem("tp" + tag)
    evv = ctx.sem("ev" + tag)
    evs = ctx.sem("es" + tag)
    tp0 = ctx.val("tp" + tag)
    ev0 = ctx.val("ev" + tag)
    es0 = ctx.val("es" + tag)
    src_r = src_nat.rearrange("(n p) d -> n p d", p=128)
    with contextlib.ExitStack() as es:
        xt = [es.enter_context(nc.sbuf_tensor(f"x{tag}{i}", [128, D], F32)) for i in range(2)]
        g32 = es.enter_context(nc.sbuf_tensor(f"g32{tag}", [128, NB, GP * 128], F32))
        g16 = es.enter_context(nc.sbuf_tensor(f"g16{tag}", [128, NB, GP * 128], BF16))
        pst = es.enter_context(nc.psum_tensor(f"pt{tag}", [128, D], F32))
        idf = consts["idf"]
        ld = Rot(ctx, "ld" + tag, 2)
        sg = Rot(ctx, "sg" + tag, 2, n_per=NB * ((1 if dst32 is not None else 0) + 1))
        ph = ctx.phase("n2t" + tag)

        def f_sync(sy):
            for i in range(NTT):
                if i >= 2:
                    sy.wait_ge(ctx.s("tp" + tag), tp0 + i - 1)
                ld.issue(i, sy.dma_start(out=xt[i % 2][:], in_=src_r[i]))
                if (i + 1) % GP == 0:
                    g = i // GP
                    sy.wait_ge(ctx.s("ev" + tag), ev0 + i + 1)
                    sy.wait_ge(ctx.s("es" + tag), es0 + i + 1)
                    for j in range(NB):
                        if dst32 is not None:
                            sg.issue(g, sy.dma_start(out=dst32[j, :, tsl(g, GP * 128)],
                                                     in_=g32[:, j, :]))
                        sg.issue(g, sy.dma_start(out=dst16[j, :, tsl(g, GP * 128)],
                                                 in_=g16[:, j, :]))
            return sg.final()

        def f_tensor(t):
            for i in range(NTT):
                ld.wait(t, i)
                if i >= 1:
                    t.wait_ge(ctx.s("ev" + tag), ev0 + i)
                    t.wait_ge(ctx.s("es" + tag), es0 + i)
                last = None
                for j in range(NB):
                    last = t.transpose(pst[:, tsl(j, 128)], xt[i % 2][:, tsl(j, 128)], idf[:])
                last.then_inc(ctx.s("tp" + tag), 1)
                ctx.inc("tp" + tag, 1)
            return ("tp" + tag, ctx.val("tp" + tag))

        def f_vector(v):
            for i in range(NTT):
                g, gi = i // GP, i % GP
                v.wait_ge(ctx.s("tp" + tag), tp0 + i + 1)
                if gi == 0 and g >= 1:
                    sg.wait(v, g - 1)
                v.tensor_copy(g32[:, :, tsl(gi, 128)],
                              pst.ap().rearrange("p (a t) -> p a t", a=NB)).then_inc(ctx.s("ev" + tag), 1)
                ctx.inc("ev" + tag, 1)
            return ("ev" + tag, ctx.val("ev" + tag))

        def f_scalar(s):
            for i in range(NTT):
                g, gi = i // GP, i % GP
                s.wait_ge(ctx.s("ev" + tag), ev0 + i + 1)
                if gi == 0 and g >= 1:
                    sg.wait(s, g - 1)
                s.copy(g16[:, :, tsl(gi, 128)], g32[:, :, tsl(gi, 128)]).then_inc(ctx.s("es" + tag), 1)
                ctx.inc("es" + tag, 1)
            return ("es" + tag, ctx.val("es" + tag))

        ph.sync(f_sync)
        ph.tensor(f_tensor)
        ph.vector(f_vector)
        ph.scalar(f_scalar)
        ph.emit()


def p_ln_t(ctx, src32, src16, gamma, beta, out32, out16, consts, tag,
           out16_sb=None, shift=False, eps=EPS):
    nc = ctx.nc
    NT = ctx.NT
    sq = ctx.sem("sq" + tag)
    mm = ctx.sem("mm" + tag)
    vs = ctx.sem("vs" + tag)
    rs = ctx.sem("rs" + tag)
    ap = ctx.sem("ap" + tag)
    sq0 = ctx.val("sq" + tag)
    mm0 = ctx.val("mm" + tag)
    vs0 = ctx.val("vs" + tag)
    rs0 = ctx.val("rs" + tag)
    ap0 = ctx.val("ap" + tag)
    onesb = consts["onesb"]
    with contextlib.ExitStack() as es:
        s32 = [es.enter_context(nc.sbuf_tensor(f"s32{tag}{i}", [128, NB, 512], F32)) for i in range(2)]
        s16 = [es.enter_context(nc.sbuf_tensor(f"s16{tag}{i}", [128, NB, 512], BF16)) for i in range(2)]
        sqb = [es.enter_context(nc.sbuf_tensor(f"sqb{tag}{i}", [128, NB, 512], BF16)) for i in range(2)]
        stat = [es.enter_context(nc.sbuf_tensor(f"stt{tag}{i}", [128, 3, 512], F32)) for i in range(2)]
        o32 = s32
        o16 = [es.enter_context(nc.sbuf_tensor(f"o16{tag}{i}", [128, NB, 512], BF16)) for i in range(2)]
        psA = es.enter_context(nc.psum_tensor(f"pA{tag}", [128, 512], F32))
        psB = es.enter_context(nc.psum_tensor(f"pB{tag}", [128, 512], F32))
        ld = Rot(ctx, "ld" + tag, 2, n_per=2 * NB)
        so = Rot(ctx, "so" + tag, 2, n_per=NB * ((1 if out32 is not None else 0) + (1 if out16 is not None else 0)))
        ph = ctx.phase("lnt" + tag)

        def f_sync(sy):
            for i in range(NT):
                b = i % 2
                if i >= 2:
                    sy.wait_ge(ctx.s("ap" + tag), ap0 + i - 1)
                for j in range(NB):
                    sh = shift and j >= NB // 2
                    if sh and i == 0:
                        l1 = sy.dma_start(out=s32[b][:, j, 1:512], in_=src32[j, :, 0:511])
                        l2 = sy.dma_start(out=s16[b][:, j, 1:512], in_=src16[j, :, 0:511])
                    elif sh:
                        l1 = sy.dma_start(out=s32[b][:, j, :], in_=src32[j, :, i * 512 - 1: i * 512 + 511])
                        l2 = sy.dma_start(out=s16[b][:, j, :], in_=src16[j, :, i * 512 - 1: i * 512 + 511])
                    else:
                        l1 = sy.dma_start(out=s32[b][:, j, :], in_=src32[j, :, tsl(i, 512)])
                        l2 = sy.dma_start(out=s16[b][:, j, :], in_=src16[j, :, tsl(i, 512)])
                    ld.issue(i, l1)
                    ld.issue(i, l2)
                if out32 is not None or out16 is not None:
                    sy.wait_ge(ctx.s("ap" + tag), ap0 + i + 1)
                    for j in range(NB):
                        if out32 is not None:
                            so.issue(i, sy.dma_start(out=out32[j, :, tsl(i, 512)], in_=o32[b][:, j, :]))
                        if out16 is not None:
                            so.issue(i, sy.dma_start(out=out16[j, :, tsl(i, 512)], in_=o16[b][:, j, :]))
            f = so.final()
            return f if f else ld.final()

        def f_gpsimd(g):
            if shift:
                g.memset(s32[0][:, NB // 2:, 0:1], 0.0)
                g.memset(s16[0][:, NB // 2:, 0:1], 0.0)
            g.sem_inc(ctx.s("sq" + tag), 1)
            ctx.inc("sq" + tag, 1)
            for i in range(NT):
                ld.wait(g, i)
                g.tensor_mul(sqb[i % 2][:].rearrange("p a t -> p (a t)"),
                             s16[i % 2][:].rearrange("p a t -> p (a t)"),
                             s16[i % 2][:].rearrange("p a t -> p (a t)")).then_inc(ctx.s("sq" + tag), 1)
                ctx.inc("sq" + tag, 1)
            return ("sq" + tag, ctx.val("sq" + tag))

        def f_scalar(s):
            for i in range(NT):
                s.wait_ge(ctx.s("vs" + tag), vs0 + i + 1)
                s.activation(stat[i % 2][:, 2, :], stat[i % 2][:, 1, :], AF.Ln, bias=consts["scal"][:, 0:1])
                s.activation(stat[i % 2][:, 2, :], stat[i % 2][:, 2, :], AF.Exp,
                             scale=-0.5).then_inc(ctx.s("rs" + tag), 1)
                ctx.inc("rs" + tag, 1)
            return ("rs" + tag, ctx.val("rs" + tag))

        def f_tensor(t):
            for i in range(NT):
                ld.wait(t, i)
                if i >= 1:
                    t.wait_ge(ctx.s("vs" + tag), vs0 + i)
                for j in range(NB):
                    t.matmul(psA[:], onesb[:], s16[i % 2][:, j, :], start=(j == 0), stop=(j == NB - 1))
                t.wait_ge(ctx.s("sq" + tag), sq0 + i + 2)
                last = None
                for j in range(NB):
                    last = t.matmul(psB[:], onesb[:], sqb[i % 2][:, j, :], start=(j == 0), stop=(j == NB - 1))
                last.then_inc(ctx.s("mm" + tag), 1)
                ctx.inc("mm" + tag, 1)
            return ("mm" + tag, ctx.val("mm" + tag))

        def f_vector(v):
            inv = 1.0 / D
            for i in range(NT):
                b = i % 2
                v.wait_ge(ctx.s("mm" + tag), mm0 + i + 1)
                v.tensor_scalar(stat[b][:, 0, :], psA[:], inv, None, ALU.mult)
                v.tensor_tensor(stat[b][:, 1, :], stat[b][:, 0, :], stat[b][:, 0, :], ALU.mult)
                v.scalar_tensor_tensor(stat[b][:, 1, :], psB[:], inv, stat[b][:, 1, :],
                                       ALU.mult, ALU.subtract).then_inc(ctx.s("vs" + tag), 1)
                ctx.inc("vs" + tag, 1)
                v.wait_ge(ctx.s("rs" + tag), rs0 + i + 1)
                if i >= 2 and so.n_per > 0:
                    so.wait(v, i - 2)
                last = None
                for j in range(NB):
                    v.tensor_tensor(o32[b][:, j, :], s32[b][:, j, :], stat[b][:, 0, :], ALU.subtract)
                    if gamma is not None:
                        v.scalar_tensor_tensor(o32[b][:, j, :], o32[b][:, j, :], gamma[:, j:j + 1],
                                               stat[b][:, 2, :], ALU.mult, ALU.mult)
                        last = v.tensor_scalar(o32[b][:, j, :], o32[b][:, j, :], beta[:, j:j + 1], None, ALU.add)
                    else:
                        last = v.tensor_tensor(o32[b][:, j, :], o32[b][:, j, :], stat[b][:, 2, :], ALU.mult)
                    dst16 = out16_sb[:, j, tsl(i, 512)] if out16_sb is not None else (
                        o16[b][:, j, :] if out16 is not None else None)
                    if dst16 is not None:
                        last = v.tensor_copy(dst16, o32[b][:, j, :])
                last.then_inc(ctx.s("ap" + tag), 1)
                ctx.inc("ap" + tag, 1)
            return ("ap" + tag, ctx.val("ap" + tag))

        ph.sync(f_sync)
        ph.gpsimd(f_gpsimd)
        ph.scalar(f_scalar)
        ph.tensor(f_tensor)
        ph.vector(f_vector)
        ph.emit()


def p_attn_qkv(ctx, wsrc, bqkv_sb, xn2b, QT, KT, VAC, consts, tag="aq"):
    nc = ctx.nc
    NT = ctx.NT
    NTT = ctx.NTT
    mm = ctx.sem("mm" + tag)
    ev = ctx.sem("ev" + tag)
    vv = ctx.sem("vv" + tag)
    bb = ctx.sem("bb" + tag)
    mm0 = ctx.val("mm" + tag)
    ev0 = ctx.val("ev" + tag)
    vv0 = ctx.val("vv" + tag)
    bb0 = ctx.val("bb" + tag)
    wsrc_r = wsrc.rearrange("(a p) c -> a p c", p=128)
    NQK = 16
    with contextlib.ExitStack() as es:
        wq = es.enter_context(nc.sbuf_tensor("wq" + tag, [128, NB, 3072], BF16))
        qko = [es.enter_context(nc.sbuf_tensor(f"qko{tag}{i}", [128, 512], BF16)) for i in range(4)]
        vb_hi = es.enter_context(nc.sbuf_tensor("vbh" + tag, [1, 1024], BF16))
        vb_lo = es.enter_context(nc.sbuf_tensor("vbl" + tag, [1, 1024], BF16))
        vb_bc = es.enter_context(nc.sbuf_tensor("vbc" + tag, [128, 1024], F32))
        vaug = [es.enter_context(nc.sbuf_tensor(f"va{tag}{i}", [128, 2048], BF16)) for i in range(2)]
        psq = [es.enter_context(nc.psum_tensor(f"pq{tag}{i}", [128, 512], F32)) for i in range(4)]
        psb = es.enter_context(nc.psum_tensor(f"pb{tag}", [128, 1024], F32))
        onesb = consts["onesb"]
        wl = Rot(ctx, "wl" + tag, 1, n_per=NB, swdge=True)
        sq_ = Rot(ctx, "sq" + tag, 4)
        sv = Rot(ctx, "sv" + tag, 2)
        ph = ctx.phase("aqkv" + tag)

        def f_gpsimd(g):
            for j in range(NB):
                wl.issue(0, g.dma_start(out=wq[:, j, :], in_=wsrc_r[j]))
            last = None
            for i in range(2):
                last = g.memset(vaug[i].ap().rearrange("p (h c) -> p h c", c=128)[:, :, 64:128], 1.0)
            last.then_inc(ctx.s("bb" + tag), 1)
            ctx.inc("bb" + tag, 1)
            return ("bb" + tag, ctx.val("bb" + tag))

        def f_vector(v):
            row = consts["qkvb_row"]
            v.tensor_copy(vb_hi[:], row[0:1, 2048:3072])
            v.tensor_tensor(vb_lo[:], row[0:1, 2048:3072], vb_hi[:],
                            ALU.subtract).then_inc(ctx.s("bb" + tag), 1)
            ctx.inc("bb" + tag, 1)
            v.wait_ge(ctx.s("mm" + tag), mm0 + 1)
            v.tensor_copy(vb_bc[:], psb[:]).then_inc(ctx.s("bb" + tag), 1)
            ctx.inc("bb" + tag, 1)
            for i in range(NTT):
                b = i % 2
                if i >= 2:
                    sv.wait(v, i - 2)
                for vc in range(2):
                    v.wait_ge(ctx.s("mm" + tag), mm0 + 1 + NQK * NT + i * 2 + vc + 1)
                    last = v.tensor_tensor(
                        vaug[b].ap().rearrange("p (h c) -> p h c", c=128)[:, tsl(vc, 8), 0:64],
                        psq[2 + vc][:].rearrange("p (h c) -> p h c", c=64),
                        vb_bc[:, tsl(vc, 512)].rearrange("p (h c) -> p h c", c=64),
                        ALU.add)
                last.then_inc(ctx.s("vv" + tag), 1)
                ctx.inc("vv" + tag, 1)
            return ("vv" + tag, ctx.val("vv" + tag))

        def f_tensor(t):
            wl.wait(t, 0)
            t.wait_ge(ctx.s("bb" + tag), bb0 + 2)
            for vc in range(2):
                t.matmul(psb[:, tsl(vc, 512)], onesb[0:1, :], vb_hi[0:1, tsl(vc, 512)], start=True, stop=False)
                last = t.matmul(psb[:, tsl(vc, 512)], onesb[0:1, :], vb_lo[0:1, tsl(vc, 512)], start=False, stop=True)
            last.then_inc(ctx.s("mm" + tag), 1)
            ctx.inc("mm" + tag, 1)
            for ct in range(NQK):
                for tt in range(NT):
                    n = ct * NT + tt
                    k = n % 4
                    if n >= 4:
                        t.wait_ge(ctx.s("ev" + tag), ev0 + n - 3)
                    last = None
                    for j in range(NB):
                        last = t.matmul(psq[k][:], wq[:, j, tsl(ct, 128)], xn2b[:, j, tsl(tt, 512)],
                                        start=(j == 0), stop=(j == NB - 1))
                    last.then_inc(ctx.s("mm" + tag), 1)
                    ctx.inc("mm" + tag, 1)
            t.wait_ge(ctx.s("ev" + tag), ev0 + NQK * NT)
            for i in range(NTT):
                for vc in range(2):
                    if i >= 1:
                        t.wait_ge(ctx.s("vv" + tag), vv0 + i)
                    last = None
                    for j in range(NB):
                        last = t.matmul(psq[2 + vc][:], xn2b[:, j, tsl(i, 128)],
                                        wq[:, j, 2048 + vc * 512: 2560 + vc * 512],
                                        start=(j == 0), stop=(j == NB - 1))
                    last.then_inc(ctx.s("mm" + tag), 1)
                    ctx.inc("mm" + tag, 1)
            return ("mm" + tag, ctx.val("mm" + tag))

        def f_scalar(s):
            for ct in range(NQK):
                for tt in range(NT):
                    n = ct * NT + tt
                    k = n % 4
                    s.wait_ge(ctx.s("mm" + tag), mm0 + 1 + n + 1)
                    if n >= 4:
                        sq_.wait(s, n - 4)
                    s.activation(qko[k][:], psq[k][:], AF.Identity,
                                 bias=bqkv_sb[:, ct:ct + 1]).then_inc(ctx.s("ev" + tag), 1)
                    ctx.inc("ev" + tag, 1)
            return ("ev" + tag, ctx.val("ev" + tag))

        def f_sync(sy):
            for ct in range(NQK):
                for tt in range(NT):
                    n = ct * NT + tt
                    k = n % 4
                    sy.wait_ge(ctx.s("ev" + tag), ev0 + n + 1)
                    dst = QT if ct < 8 else KT
                    sq_.issue(n, sy.dma_start(out=dst[ct % 8, :, tsl(tt, 512)], in_=qko[k][:]))
            for i in range(NTT):
                sy.wait_ge(ctx.s("vv" + tag), vv0 + i + 1)
                sv.issue(i, sy.dma_start(out=VAC[i], in_=vaug[i % 2][:]))
            return sq_.final() + sv.final()

        ph.gpsimd(f_gpsimd)
        ph.vector(f_vector)
        ph.tensor(f_tensor)
        ph.scalar(f_scalar)
        ph.sync(f_sync)
        ph.emit()


def p_causal_attn(ctx, QT, KT, VAC, OTC, consts, tag="ca"):
    nc = ctx.nc
    NT = ctx.NT
    sA = ctx.sem("sA" + tag)
    xp = ctx.sem("xp" + tag)
    pv = ctx.sem("pv" + tag)
    dv = ctx.sem("dv" + tag)
    mk = ctx.sem("mk" + tag)
    sA0 = ctx.val("sA" + tag)
    xp0g = ctx.val("xp" + tag)
    pv0g = ctx.val("pv" + tag)
    dv0 = ctx.val("dv" + tag)
    mk0g = ctx.val("mk" + tag)
    cmask_d = consts["cmask_d"]
    with contextlib.ExitStack() as es:
        cmask = es.enter_context(nc.sbuf_tensor("cmsk" + tag, [128, 4, 512], F32))
        qtile = [es.enter_context(nc.sbuf_tensor(f"qt{tag}{i}", [128, 512], BF16)) for i in range(2)]
        ktile = [es.enter_context(nc.sbuf_tensor(f"kt{tag}{i}", [128, ctx.S], BF16)) for i in range(2)]
        vtile = [es.enter_context(nc.sbuf_tensor(f"vt{tag}{i}", [128, (ctx.S // 128) * 256], BF16)) for i in range(2)]
        pT = [es.enter_context(nc.sbuf_tensor(f"pT{tag}{i}", [128, 512], BF16)) for i in range(4)]
        rl = es.enter_context(nc.sbuf_tensor(f"rl{tag}", [64, 512], F32))
        otile = [es.enter_context(nc.sbuf_tensor(f"ot{tag}{i}", [128, 512], BF16)) for i in range(2)]
        psS = [es.enter_context(nc.psum_tensor(f"pS{tag}{i}", [128, 512], F32)) for i in range(4)]
        psO = [es.enter_context(nc.psum_tensor(f"pO{tag}{i}", [128, 512], F32)) for i in range(2)]
        nks_plan = [4 * qt + 4 for qt in range(NT) for hp in range(NB)]
        nks_plan = [4 * qt + 4 for qt in range(NT) for _ in range(NB)]
        ldq = Rot(ctx, "lq" + tag, 2)
        ldk = Rot(ctx, "lk" + tag, 2)
        ldv = Rot(ctx, "lv" + tag, 2, plan=nks_plan)
        lmk = Rot(ctx, "lm" + tag, 1)
        so = Rot(ctx, "so" + tag, 2)
        ph = ctx.phase("causal" + tag)
        HP = NB
        units = [(hp, qt) for qt in range(NT) for hp in range(HP)]
        sched = []
        sA_c, xp_c, pv_c, mk_c = sA0, xp0g, pv0g, mk0g
        for u, (hp, qt) in enumerate(units):
            nk = 4 * qt + 4
            sched.append({"hp": hp, "qt": qt, "nk": nk, "sA0": sA_c, "xp0": xp_c,
                          "pv0": pv_c, "mk0": mk_c})
            sA_c += nk
            xp_c += 2 * nk
            pv_c += nk
            mk_c += 4

        def f_sync(sy):
            lmk.issue(0, sy.dma_start(out=cmask[:], in_=cmask_d))
            for u, s_ in enumerate(sched):
                hp, qt, nk = s_["hp"], s_["qt"], s_["nk"]
                b = u % 2
                if u >= 2:
                    sy.wait_ge(ctx.s("dv" + tag), dv0 + u - 1)
                ldq.issue(u, sy.dma_start(out=qtile[b][:], in_=QT[hp, :, tsl(qt, 512)]))
                ldk.issue(u, sy.dma_start(out=ktile[b][:, 0:nk * 128], in_=KT[hp, :, 0:nk * 128]))
                for kt in range(nk):
                    ldv.issue(u, sy.dma_start(out=vtile[b][:, tsl(kt, 256)],
                                              in_=VAC[kt, :, hp * 256:(hp + 1) * 256]))
                sy.wait_ge(ctx.s("dv" + tag), dv0 + u + 1)
                so.issue(u, sy.dma_start(out=OTC[hp, :, tsl(qt, 512)], in_=otile[b][:]))
            return so.final()

        def f_tensor(t):
            for u, s_ in enumerate(sched):
                hp, qt, nk = s_["hp"], s_["qt"], s_["nk"]
                b = u % 2
                ldq.wait(t, u)
                ldk.wait(t, u)
                ldv.wait(t, u)
                if u >= 1:
                    t.wait_ge(ctx.s("dv" + tag), dv0 + u)
                    t.wait_ge(ctx.s("xp" + tag), s_["xp0"])

                def S_(kt):
                    kb_ = kt % 2
                    if kt >= 2:
                        t.wait_ge(ctx.s("xp" + tag), s_["xp0"] + 2 * (kt - 1))
                    t.matmul(psS[kb_][:], ktile[b][0:64, tsl(kt, 128)], qtile[b][0:64, :],
                             start=True, stop=True, tile_position=(0, 0))
                    t.matmul(psS[2 + kb_][:], ktile[b][64:128, tsl(kt, 128)], qtile[b][64:128, :],
                             start=True, stop=True, tile_position=(64, 0)).then_inc(ctx.s("sA" + tag), 1)
                    ctx.inc("sA" + tag, 1)

                def PV_(kt):
                    kb_ = kt % 2
                    t.wait_ge(ctx.s("xp" + tag), s_["xp0"] + 2 * (kt + 1))
                    t.matmul(psO[0][:], vtile[b][:, tsl(kt, 256)][:, 0:128], pT[2 * kb_][:],
                             start=(kt == 0), stop=(kt == nk - 1))
                    t.matmul(psO[1][:], vtile[b][:, tsl(kt, 256)][:, 128:256], pT[2 * kb_ + 1][:],
                             start=(kt == 0), stop=(kt == nk - 1)).then_inc(ctx.s("pv" + tag), 1)
                    ctx.inc("pv" + tag, 1)

                S_(0)
                for kt in range(1, nk):
                    S_(kt)
                    PV_(kt - 1)
                PV_(nk - 1)
            return ("pv" + tag, ctx.val("pv" + tag))

        def f_vector(v):
            lmk.wait(v, 0)
            for u, s_ in enumerate(sched):
                hp, qt, nk = s_["hp"], s_["qt"], s_["nk"]
                for kt in range(nk - 4, nk):
                    r = kt - (nk - 4)
                    kb_ = kt % 2
                    v.wait_ge(ctx.s("sA" + tag), s_["sA0"] + kt + 1)
                    v.tensor_tensor(psS[kb_][:], psS[kb_][:], cmask[:, r, :], ALU.add)
                    v.tensor_tensor(psS[2 + kb_][:], psS[2 + kb_][:], cmask[:, r, :],
                                    ALU.add).then_inc(ctx.s("mk" + tag), 1)
                    ctx.inc("mk" + tag, 1)
                if u >= 2:
                    so.wait(v, u - 2)
                v.wait_ge(ctx.s("pv" + tag), s_["pv0"] + nk)
                v.reciprocal(rl[:], psO[0][64:128, :])
                v.tensor_tensor(otile[u % 2][0:64, :], psO[0][0:64, :], rl[:], ALU.mult)
                v.reciprocal(rl[:], psO[1][64:128, :])
                v.tensor_tensor(otile[u % 2][64:128, :], psO[1][0:64, :], rl[:],
                                ALU.mult).then_inc(ctx.s("dv" + tag), 1)
                ctx.inc("dv" + tag, 1)
            return ("dv" + tag, ctx.val("dv" + tag))

        def f_scalar(s):
            for u, s_ in enumerate(sched):
                hp, qt, nk = s_["hp"], s_["qt"], s_["nk"]
                if u >= 1:
                    s.wait_ge(ctx.s("pv" + tag), s_["pv0"])
                for kt in range(nk):
                    kb_ = kt % 2
                    if kt >= nk - 4:
                        r = kt - (nk - 4)
                        s.wait_ge(ctx.s("mk" + tag), s_["mk0"] + r + 1)
                    else:
                        s.wait_ge(ctx.s("sA" + tag), s_["sA0"] + kt + 1)
                    if kt >= 2:
                        s.wait_ge(ctx.s("pv" + tag), s_["pv0"] + kt - 1)
                    s.activation(pT[2 * kb_][:], psS[kb_][:], AF.Exp, scale=0.125)
                    s.activation(pT[2 * kb_ + 1][:], psS[2 + kb_][:], AF.Exp,
                                 scale=0.125).then_inc(ctx.s("xp" + tag), 2)
                    ctx.inc("xp" + tag, 2)
            return ("xp" + tag, ctx.val("xp" + tag))

        ph.sync(f_sync)
        ph.tensor(f_tensor)
        ph.vector(f_vector)
        ph.scalar(f_scalar)
        ph.emit()


def p_proj_t(ctx, wsrc, rhs16, bias_sb, extra16, res32, out32, out16, consts,
             tag, nbin=NB, nbout=NB, act=None):
    nc = ctx.nc
    NT = ctx.NT
    mm = ctx.sem("mm" + tag)
    ev = ctx.sem("ev" + tag)
    vv = ctx.sem("vv" + tag)
    mm0 = ctx.val("mm" + tag)
    ev0 = ctx.val("ev" + tag)
    vv0 = ctx.val("vv" + tag)
    wsrc_r = wsrc.rearrange("(a p) c -> a p c", p=128)
    Dout = nbout * 128
    stream_rhs = nbin > NB
    with contextlib.ExitStack() as es:
        wsb = es.enter_context(nc.sbuf_tensor("w" + tag, [128, nbin, Dout], BF16))
        if stream_rhs:
            rsb = [es.enter_context(nc.sbuf_tensor(f"r{tag}{i}", [128, nbin, 512], BF16)) for i in range(2)]
        else:
            rsb = es.enter_context(nc.sbuf_tensor("r" + tag, [128, nbin, ctx.S], BF16))
        a32 = [es.enter_context(nc.sbuf_tensor(f"a{tag}{i}", [128, 512], F32)) for i in range(2)]
        e16 = [es.enter_context(nc.sbuf_tensor(f"e{tag}{i}", [128, 512], BF16)) for i in range(2)]
        o32 = [es.enter_context(nc.sbuf_tensor(f"o{tag}{i}", [128, 512], F32)) for i in range(2)]
        o16 = [es.enter_context(nc.sbuf_tensor(f"f{tag}{i}", [128, 512], BF16)) for i in range(2)]
        r32 = [es.enter_context(nc.sbuf_tensor(f"g{tag}{i}", [128, 512], F32)) for i in range(2)]
        ps = [es.enter_context(nc.psum_tensor(f"p{tag}{i}", [128, 512], F32)) for i in range(2)]
        wl = Rot(ctx, "wl" + tag, 1, n_per=nbin, swdge=True)
        rl_ = (Rot(ctx, "rl" + tag, 2, n_per=nbin) if stream_rhs
               else Rot(ctx, "rl" + tag, nbin))
        rr = Rot(ctx, "rr" + tag, 2)
        se = Rot(ctx, "se" + tag, 2)
        sv_ = Rot(ctx, "sv" + tag, 2,
                  n_per=(1 if out32 is not None else 0) + (1 if out16 is not None else 0))
        ph = ctx.phase("proj" + tag)
        if stream_rhs:
            units = [(tt, co) for tt in range(NT) for co in range(nbout)]
        else:
            units = [(tt, co) for co in range(nbout) for tt in range(NT)]
        NU = len(units)

        def f_gpsimd(g):
            for j in range(nbin):
                wl.issue(0, g.dma_start(out=(wsb[:, j, :]), in_=wsrc_r[j]))
            return wl.final()[-1]

        def f_sync(sy):
            if not stream_rhs:
                for j in range(nbin):
                    rl_.issue(j, sy.dma_start(out=rsb[:, j, :], in_=rhs16[j]))
            for u, (tt, co) in enumerate(units):
                b = u % 2
                if stream_rhs and co == 0:
                    ti = tt
                    if ti >= 2:
                        sy.wait_ge(ctx.s("mm" + tag), mm0 + (ti - 1) * nbout)
                    for j in range(nbin):
                        rl_.issue(ti, sy.dma_start(out=rsb[ti % 2][:, j, :],
                                                   in_=rhs16[j, :, tsl(ti, 512)]))
                if res32 is not None:
                    if u >= 2:
                        sy.wait_ge(ctx.s("vv" + tag), vv0 + u - 1)
                    rr.issue(u, sy.dma_start(out=r32[b][:], in_=res32[co, :, tsl(tt, 512)]))
                if extra16 is not None:
                    sy.wait_ge(ctx.s("ev" + tag), ev0 + 2 * (u + 1))
                    se.issue(u, sy.dma_start(out=extra16[co, :, tsl(tt, 512)], in_=e16[b][:]))
                if out32 is not None or out16 is not None:
                    sy.wait_ge(ctx.s("vv" + tag), vv0 + u + 1)
                    if out32 is not None:
                        sv_.issue(u, sy.dma_start(out=out32[co, :, tsl(tt, 512)], in_=o32[b][:]))
                    if out16 is not None:
                        sv_.issue(u, sy.dma_start(out=out16[co, :, tsl(tt, 512)], in_=o16[b][:]))
            return sv_.final() + se.final()

        def f_tensor(t):
            wl.wait(t, 0)
            if not stream_rhs:
                for j in range(nbin):
                    rl_.wait(t, j)
            for u, (tt, co) in enumerate(units):
                b = u % 2
                if stream_rhs:
                    rl_.wait(t, tt)
                if u >= 2:
                    t.wait_ge(ctx.s("ev" + tag), ev0 + 2 * (u - 1))
                last = None
                for j in range(nbin):
                    rt = rsb[tt % 2][:, j, :] if stream_rhs else rsb[:, j, tsl(tt, 512)]
                    last = t.matmul(ps[b][:], wsb[:, j, tsl(co, 128)], rt,
                                    start=(j == 0), stop=(j == nbin - 1))
                last.then_inc(ctx.s("mm" + tag), 1)
                ctx.inc("mm" + tag, 1)
            return ("mm" + tag, ctx.val("mm" + tag))

        def f_scalar(s):
            for u, (tt, co) in enumerate(units):
                b = u % 2
                s.wait_ge(ctx.s("mm" + tag), mm0 + u + 1)
                if u >= 2:
                    s.wait_ge(ctx.s("vv" + tag), vv0 + u - 1)
                    if extra16 is not None:
                        se.wait(s, u - 2)
                bias = bias_sb[:, co:co + 1] if bias_sb is not None else consts["scal"][:, 3:4]
                fn = act if act is not None else AF.Identity
                last = s.activation(a32[b][:], ps[b][:], fn, bias=bias)
                if extra16 is not None:
                    last = s.copy(e16[b][:], a32[b][:])
                last.then_inc(ctx.s("ev" + tag), 2)
                ctx.inc("ev" + tag, 2)
            return ("ev" + tag, ctx.val("ev" + tag))

        def f_vector(v):
            for u, (tt, co) in enumerate(units):
                b = u % 2
                v.wait_ge(ctx.s("ev" + tag), ev0 + 2 * u + 2)
                if u >= 2:
                    sv_.wait(v, u - 2)
                if res32 is not None:
                    rr.wait(v, u)
                    last = v.tensor_tensor(o32[b][:], a32[b][:], r32[b][:], ALU.add)
                else:
                    last = v.tensor_copy(o32[b][:], a32[b][:])
                if out16 is not None:
                    last = v.tensor_copy(o16[b][:], o32[b][:])
                last.then_inc(ctx.s("vv" + tag), 1)
                ctx.inc("vv" + tag, 1)
            return ("vv" + tag, ctx.val("vv" + tag))

        ph.gpsimd(f_gpsimd)
        ph.sync(f_sync)
        ph.tensor(f_tensor)
        ph.scalar(f_scalar)
        ph.vector(f_vector)
        ph.emit()


def p_scan(ctx, wctx_d, wx_d, binl_sb, CTXB, XS0_32, XS0_16, Y1_32,
           Y2_32, Y2_16, consts, tag="sc"):
    nc = ctx.nc
    NT = ctx.NT
    NIT = 8
    for n in ["cm", "um", "va", "th", "gp", "vu", "xc", "fi"]:
        ctx.sem(n + tag)
    b0 = {n: ctx.val(n + tag) for n in ["cm", "um", "va", "th", "gp", "vu", "xc", "fi"]}
    PER = {"um": 2 * NIT, "va": 2 * NIT, "th": 2 * NIT, "gp": NIT,
           "vu": NIT, "xc": NIT + 1, "cm": 1, "fi": 1}

    def base(n, i):
        return b0[n] + PER[n] * i

    wctx_r = wctx_d.rearrange("(a p) c -> a p c", p=128)
    wx_r = wx_d.rearrange("(a p) c -> a p c", p=128)
    flat = lambda ap: ap.rearrange("p a t -> p (a t)")
    with contextlib.ExitStack() as es:
        wctx = es.enter_context(nc.sbuf_tensor("wc" + tag, [128, NB, D], BF16))
        wx = es.enter_context(nc.sbuf_tensor("wx" + tag, [128, NB, D], BF16))
        ctxb = [es.enter_context(nc.sbuf_tensor(f"cb{tag}{i}", [128, NB, 512], BF16)) for i in range(2)]
        c0 = es.enter_context(nc.sbuf_tensor("c0" + tag, [128, NB, 512], F32))
        xs = es.enter_context(nc.sbuf_tensor("xs" + tag, [128, NB, 512], F32))
        vs = es.enter_context(nc.sbuf_tensor("vs" + tag, [128, NB, 512], F32))
        tmp = es.enter_context(nc.sbuf_tensor("tp" + tag, [128, NB, 512], F32))
        u32 = es.enter_context(nc.sbuf_tensor("u3" + tag, [128, NB, 512], F32))
        xsb = es.enter_context(nc.sbuf_tensor("xb" + tag, [128, NB, 512], BF16))
        up = es.enter_context(nc.psum_tensor("up" + tag, [128, NB, 512], F32))
        wload = Rot(ctx, "wv" + tag, 1, n_per=2 * NB, swdge=True)
        lc = Rot(ctx, "lc" + tag, 2, n_per=NB)
        lx = Rot(ctx, "lx" + tag, 1, n_per=2 * NB)
        ly = Rot(ctx, "ly" + tag, 1, n_per=NB)
        st = Rot(ctx, "st" + tag, 1, n_per=2 * NB)
        ph = ctx.phase("scan" + tag)
        H4 = lambda h: slice(h * 4, h * 4 + 4)

        def f_gpsimd(g):
            for j in range(NB):
                wload.issue(0, g.dma_start(out=wctx[:, j, :], in_=wctx_r[j]))
            for j in range(NB):
                wload.issue(0, g.dma_start(out=wx[:, j, :], in_=wx_r[j]))
            for i in range(NT):
                lx.wait(g, i)
                g.memset(flat(vs[:]), 0.0)
                for it in range(NIT):
                    if it >= 1:
                        g.wait_ge(ctx.s("vu" + tag), base("vu", i) + it)
                    g.wait_ge(ctx.s("th" + tag), base("th", i) + 2 * (it + 1))
                    g.tensor_sub(flat(tmp[:]), flat(u32[:]), flat(xs[:]))
                    g.tensor_sub(flat(tmp[:]), flat(tmp[:]), flat(vs[:])).then_inc(ctx.s("gp" + tag), 1)
                    ctx.inc("gp" + tag, 1)
            return ("gp" + tag, ctx.val("gp" + tag))

        def f_sync(sy):
            for i in range(NT):
                if i >= 1:
                    st.wait(sy, i - 1)
                if i >= 2:
                    sy.wait_ge(ctx.s("cm" + tag), base("cm", i - 1) + 1)
                for j in range(NB):
                    lc.issue(i, sy.dma_start(out=ctxb[i % 2][:, j, :], in_=CTXB[j, :, tsl(i, 512)]))
                for j in range(NB):
                    lx.issue(i, sy.dma_start(out=xs[:, j, :], in_=XS0_32[j, :, tsl(i, 512)]))
                    lx.issue(i, sy.dma_start(out=xsb[:, j, :], in_=XS0_16[j, :, tsl(i, 512)]))
                sy.wait_ge(ctx.s("vu" + tag), base("vu", i) + NIT)
                for j in range(NB):
                    ly.issue(i, sy.dma_start(out=tmp[:, j, :], in_=Y1_32[j, :, tsl(i, 512)]))
                sy.wait_ge(ctx.s("fi" + tag), base("fi", i) + 1)
                sy.wait_ge(ctx.s("xc" + tag), base("xc", i) + NIT + 1)
                for j in range(NB):
                    st.issue(i, sy.dma_start(out=Y2_32[j, :, tsl(i, 512)], in_=u32[:, j, :]))
                    st.issue(i, sy.dma_start(out=Y2_16[j, :, tsl(i, 512)], in_=xsb[:, j, :]))
            return st.final()

        def f_tensor(t):
            wload.wait(t, 0)
            for i in range(NT):
                lx.wait(t, i)
                lc.wait(t, i)
                if i >= 1:
                    t.wait_ge(ctx.s("va" + tag), base("va", i))
                    st.wait(t, i - 1)
                for od in range(NB):
                    for j in range(NB):
                        last = t.matmul(up[:, od, :], wctx[:, j, tsl(od, 128)], ctxb[i % 2][:, j, :],
                                        start=(j == 0), stop=(j == NB - 1))
                last.then_inc(ctx.s("cm" + tag), 1)
                ctx.inc("cm" + tag, 1)
                for it in range(NIT):
                    t.wait_ge(ctx.s("xc" + tag), base("xc", i) + it + 1)
                    for half in range(2):
                        if it >= 1:
                            t.wait_ge(ctx.s("va" + tag), base("va", i) + 2 * (it - 1) + half + 1)
                        last = None
                        for od in range(4 * half, 4 * half + 4):
                            for j in range(NB):
                                last = t.matmul(up[:, od, :], wx[:, j, tsl(od, 128)], xsb[:, j, :],
                                                start=(j == 0), stop=(j == NB - 1))
                        last.then_inc(ctx.s("um" + tag), 1)
                        ctx.inc("um" + tag, 1)
            return ("um" + tag, ctx.val("um" + tag))

        def f_scalar(s):
            for i in range(NT):
                s.wait_ge(ctx.s("cm" + tag), base("cm", i) + 1)
                s.wait_ge(ctx.s("va" + tag), base("va", i))
                last = None
                for od in range(NB):
                    last = s.activation(c0[:, od, :], up[:, od, :], AF.Identity, bias=binl_sb[:, od:od + 1])
                last.then_inc(ctx.s("xc" + tag), 1)
                ctx.inc("xc" + tag, 1)
                for it in range(NIT):
                    for half in range(2):
                        s.wait_ge(ctx.s("va" + tag), base("va", i) + 2 * it + half + 1)
                        s.activation(flat(u32[:, H4(half), :]), flat(u32[:, H4(half), :]),
                                     AF.Tanh).then_inc(ctx.s("th" + tag), 1)
                        ctx.inc("th" + tag, 1)
                    if it < NIT - 1:
                        s.wait_ge(ctx.s("vu" + tag), base("vu", i) + it + 1)
                        s.copy(flat(xsb[:]), flat(xs[:])).then_inc(ctx.s("xc" + tag), 1)
                        ctx.inc("xc" + tag, 1)
                s.wait_ge(ctx.s("fi" + tag), base("fi", i) + 1)
                s.copy(flat(xsb[:]), flat(u32[:])).then_inc(ctx.s("xc" + tag), 1)
                ctx.inc("xc" + tag, 1)
            return ("xc" + tag, ctx.val("xc" + tag))

        def f_vector(v):
            for i in range(NT):
                for it in range(NIT):
                    for half in range(2):
                        v.wait_ge(ctx.s("um" + tag), base("um", i) + 2 * it + half + 1)
                        v.tensor_tensor(flat(u32[:, H4(half), :]), flat(up[:, H4(half), :]),
                                        flat(c0[:, H4(half), :]), ALU.add).then_inc(ctx.s("va" + tag), 1)
                        ctx.inc("va" + tag, 1)
                    v.wait_ge(ctx.s("gp" + tag), base("gp", i) + it + 1)
                    v.scalar_tensor_tensor(flat(vs[:]), flat(tmp[:]), 0.1, flat(vs[:]), ALU.mult, ALU.add)
                    v.scalar_tensor_tensor(flat(xs[:]), flat(vs[:]), 0.1, flat(xs[:]), ALU.mult,
                                           ALU.add).then_inc(ctx.s("vu" + tag), 1)
                    ctx.inc("vu" + tag, 1)
                ly.wait(v, i)
                v.tensor_tensor(flat(u32[:]), flat(xs[:]), flat(tmp[:]), ALU.add).then_inc(ctx.s("fi" + tag), 1)
                ctx.inc("fi" + tag, 1)
            return ("fi" + tag, ctx.val("fi" + tag))

        ph.gpsimd(f_gpsimd)
        ph.sync(f_sync)
        ph.tensor(f_tensor)
        ph.scalar(f_scalar)
        ph.vector(f_vector)
        ph.emit()


def p_ffn_qkv(ctx, wsrc, qbc_d, RQC, RQS, RKC, RKS,
              Z16, QTl, KTl, VAUGl, consts, tag):
    nc = ctx.nc
    NTT = ctx.NTT
    for n in ["mm", "s1", "va", "s2", "qe", "tp", "pe", "tb", "ve", "hi"]:
        ctx.sem(n + tag)
    b0 = {n: ctx.val(n + tag) for n in ["mm", "s1", "va", "s2", "qe", "tp", "pe", "tb", "ve", "hi"]}
    onesb = consts["onesb"]
    wsrc_r = wsrc.rearrange("(a p) c -> a p c", p=128)
    with contextlib.ExitStack() as es:
        wq = es.enter_context(nc.sbuf_tensor("wq" + tag, [128, NB, 3072], BF16))
        CSQ = es.enter_context(nc.sbuf_tensor("csq" + tag, [128, NTT, 64], F32))
        SSQ = es.enter_context(nc.sbuf_tensor("ssx" + tag, [128, NTT, 64], F32))
        CSK = es.enter_context(nc.sbuf_tensor("csk" + tag, [128, NTT, 64], F32))
        SSK = es.enter_context(nc.sbuf_tensor("ssk" + tag, [128, NTT, 64], F32))
        bt = [es.enter_context(nc.sbuf_tensor(f"bt{tag}{i}", [128, 64], F32)) for i in range(8)]
        qbc = es.enter_context(nc.sbuf_tensor("qb" + tag, [128, 4, 64], F32))
        zb = [es.enter_context(nc.sbuf_tensor(f"zb{tag}{i}", [128, NB, 130], BF16)) for i in range(2)]
        sqst = [es.enter_context(nc.sbuf_tensor(f"sg{tag}{i}", [128, 512], BF16)) for i in range(2)]
        ssA = [es.enter_context(nc.sbuf_tensor(f"sa{tag}{i}", [128, 8], F32)) for i in range(2)]
        ssB = [es.enter_context(nc.sbuf_tensor(f"sb{tag}{i}", [128, 8], F32)) for i in range(2)]
        rs_t = es.enter_context(nc.sbuf_tensor("rs" + tag, [128, 8], F32))
        st2 = es.enter_context(nc.sbuf_tensor("t2" + tag, [128, 512], F32))
        st3 = es.enter_context(nc.sbuf_tensor("t3" + tag, [128, 512], F32))
        q16 = [es.enter_context(nc.sbuf_tensor(f"q6{tag}{i}", [128, 512], BF16)) for i in range(2)]
        qts = [es.enter_context(nc.sbuf_tensor(f"qt{tag}{i}", [128, 512], BF16)) for i in range(2)]
        vaug = [es.enter_context(nc.sbuf_tensor(f"va{tag}{i}", [128, 2048], BF16)) for i in range(2)]
        pq = [es.enter_context(nc.psum_tensor(f"pq{tag}{i}", [128, 512], F32)) for i in range(3)]
        psT = es.enter_context(nc.psum_tensor(f"pT{tag}", [128, 512], BF16))
        wl = Rot(ctx, "wl" + tag, 1, n_per=NB, swdge=True)
        lq = Rot(ctx, "lr" + tag, 1, n_per=1)
        lt = Rot(ctx, "lt" + tag, 2, n_per=4)
        lz = Rot(ctx, "lz" + tag, 2, n_per=NB)
        sv = Rot(ctx, "sw" + tag, 2)
        sq_ = Rot(ctx, "sy" + tag, 2, n_per=4)
        ph = ctx.phase("fqkv" + tag)
        idb = consts["idb"]
        hview = lambda ap: ap.rearrange("p (a b) -> p a b", b=64)

        def f_gpsimd(g):
            for j in range(NB):
                wl.issue(0, g.dma_start(out=wq[:, j, :], in_=wsrc_r[j]))
            for i in range(2):
                g.memset(vaug[i].ap().rearrange("p (h c) -> p h c", c=128)[:, :, 64:128], 1.0)
            g.memset(zb[0][:, :, 0:1], 0.0)
            g.memset(zb[1][:, :, 0:1], 0.0)
            g.sem_inc(ctx.s("hi" + tag), 1)
            ctx.inc("hi" + tag, 1)
            return ("hi" + tag, ctx.val("hi" + tag))

        def f_sync(sy):
            lq.issue(0, sy.dma_start(out=qbc[:], in_=qbc_d))
            for i in range(NTT):
                if i >= 2:
                    sy.wait_ge(ctx.s("tb" + tag), b0["tb"] + i - 1)
                for bi, R in enumerate([RQC, RQS, RKC, RKS]):
                    lt.issue(i, sy.dma_start(out=bt[4 * (i % 2) + bi][:], in_=R[i]))
            for i in range(NTT):
                b = i % 2
                if i >= 2:
                    sy.wait_ge(ctx.s("mm" + tag), b0["mm"] + 1 + 6 * (i - 1))
                for j in range(NB):
                    if i == 0:
                        lz.issue(i, sy.dma_start(out=zb[b][:, j, 1:129], in_=Z16[j, :, 0:128]))
                    else:
                        lz.issue(i, sy.dma_start(out=zb[b][:, j, 0:129], in_=Z16[j, :, i * 128 - 1: i * 128 + 128]))
                for cg in range(4):
                    n = i * 4 + cg
                    sy.wait_ge(ctx.s("pe" + tag), b0["pe"] + n + 1)
                    dst = QTl if cg < 2 else KTl
                    for c in range(4):
                        jj = 4 * (cg % 2) + c
                        sq_.issue(n, sy.dma_start(out=dst[jj, :, tsl(i, 128)], in_=qts[n % 2][:, tsl(c, 128)]))
                sy.wait_ge(ctx.s("ve" + tag), b0["ve"] + 2 * (i + 1))
                sv.issue(i, sy.dma_start(out=VAUGl[i], in_=vaug[b][:]))
            return sq_.final() + sv.final()

        def f_tensor(t):
            t.wait_ge(ctx.s("hi" + tag), b0["hi"] + 1)
            wl.wait(t, 0)
            t.wait_ge(ctx.s("tb" + tag), b0["tb"] + NTT)
            for i in range(NTT):
                b = i % 2
                lz.wait(t, i)

                def mm_cg(cg):
                    k = (i * 6 + cg) % 3
                    n_prev = i * 6 + cg - 3
                    if n_prev >= 0:
                        pi, pcg = divmod(n_prev, 6)
                        if pcg >= 4:
                            t.wait_ge(ctx.s("ve" + tag), b0["ve"] + 2 * pi + (pcg - 4) + 1)
                        else:
                            t.wait_ge(ctx.s("qe" + tag), b0["qe"] + pi * 4 + pcg + 1)
                    last = None
                    for j in range(NB):
                        o = 1 if j < 4 else 0
                        last = t.matmul(pq[k][:], zb[b][:, j, o:o + 128], wq[:, j, tsl(cg, 512)],
                                        start=(j == 0), stop=(j == NB - 1))
                    last.then_inc(ctx.s("mm" + tag), 1)
                    ctx.inc("mm" + tag, 1)

                def tr_cg(cg):
                    n = i * 4 + cg
                    t.wait_ge(ctx.s("qe" + tag), b0["qe"] + n + 1)
                    if n >= 1:
                        t.wait_ge(ctx.s("pe" + tag), b0["pe"] + n)
                    last = None
                    for c in range(4):
                        last = t.transpose(psT[:, tsl(c, 128)], q16[n % 2][:, tsl(c, 128)], idb[:])
                    last.then_inc(ctx.s("tp" + tag), 1)
                    ctx.inc("tp" + tag, 1)

                mm_cg(0); mm_cg(1); mm_cg(2)
                tr_cg(0); mm_cg(3)
                tr_cg(1); mm_cg(4)
                tr_cg(2); mm_cg(5)
                tr_cg(3)
            return ("tp" + tag, ctx.val("tp" + tag))

        def f_scalar(s):
            for i in range(NTT):
                for cg in range(4):
                    k = (i * 6 + cg) % 3
                    n = i * 4 + cg
                    s.wait_ge(ctx.s("mm" + tag), b0["mm"] + i * 6 + cg + 1)
                    if n >= 2:
                        s.wait_ge(ctx.s("va" + tag), b0["va"] + n - 1)
                    s.activation(sqst[n % 2][:], pq[k][:], AF.Square).then_inc(ctx.s("s1" + tag), 1)
                    ctx.inc("s1" + tag, 1)
                    s.wait_ge(ctx.s("va" + tag), b0["va"] + n + 1)
                    if n >= 2:
                        s.wait_ge(ctx.s("qe" + tag), b0["qe"] + n - 1)
                    s.activation(ssB[n % 2][:], ssA[n % 2][:], AF.Ln, bias=consts["scal"][:, 4:5])
                    s.activation(ssB[n % 2][:], ssB[n % 2][:], AF.Exp, scale=-0.5).then_inc(ctx.s("s2" + tag), 1)
                    ctx.inc("s2" + tag, 1)
            return ("s2" + tag, ctx.val("s2" + tag))

        def f_vector(v):
            lq.wait(v, 0)
            for i in range(NTT):
                lt.wait(v, i)
                o = 4 * (i % 2)
                v.scalar_tensor_tensor(CSQ[:, i, :], bt[o + 0][:], 8.0, qbc[:, 0, :], ALU.mult, ALU.mult)
                v.scalar_tensor_tensor(SSQ[:, i, :], bt[o + 1][:], 8.0, qbc[:, 1, :], ALU.mult, ALU.mult)
                v.scalar_tensor_tensor(CSK[:, i, :], bt[o + 2][:], 1.0, qbc[:, 2, :], ALU.mult, ALU.mult)
                v.scalar_tensor_tensor(SSK[:, i, :], bt[o + 3][:], 1.0, qbc[:, 3, :], ALU.mult,
                                       ALU.mult).then_inc(ctx.s("tb" + tag), 1)
                ctx.inc("tb" + tag, 1)
            for i in range(NTT):
                b = i % 2
                for cg in range(4):
                    k = (i * 6 + cg) % 3
                    n = i * 4 + cg
                    CS = CSQ if cg < 2 else CSK
                    SS = SSQ if cg < 2 else SSK
                    v.wait_ge(ctx.s("s1" + tag), b0["s1"] + n + 1)
                    if n >= 2:
                        v.wait_ge(ctx.s("s2" + tag), b0["s2"] + n - 1)
                    v.reduce_sum(ssA[n % 2][:], hview(sqst[n % 2].ap()), AX.X).then_inc(ctx.s("va" + tag), 1)
                    ctx.inc("va" + tag, 1)
                    v.wait_ge(ctx.s("s2" + tag), b0["s2"] + n + 1)
                    v.tensor_scalar(rs_t[:], ssB[n % 2][:], 1.0e12, None, ALU.min)
                    cst = CS[:, i, :].rearrange("p (o b) -> p o b", o=1).broadcast_to((128, 8, 64))
                    sst = SS[:, i, :].rearrange("p (o b) -> p o b", o=1)
                    pqv = hview(pq[k].ap())
                    v.tensor_tensor(hview(st2.ap()), pqv, cst, ALU.mult)
                    v.tensor_tensor(hview(st3.ap())[:, :, 0:32], pqv[:, :, 32:64],
                                    sst[:, :, 0:32].broadcast_to((128, 8, 32)), ALU.mult)
                    v.tensor_tensor(hview(st3.ap())[:, :, 32:64], pqv[:, :, 0:32],
                                    sst[:, :, 32:64].broadcast_to((128, 8, 32)), ALU.mult)
                    v.tensor_tensor(st2[:], st2[:], st3[:], ALU.add)
                    if n >= 2:
                        v.wait_ge(ctx.s("tp" + tag), b0["tp"] + n - 1)
                    v.tensor_tensor(hview(q16[n % 2].ap()), hview(st2.ap()),
                                    rs_t.ap().rearrange("p (a o) -> p a o", o=1).broadcast_to((128, 8, 64)),
                                    ALU.mult).then_inc(ctx.s("qe" + tag), 1)
                    ctx.inc("qe" + tag, 1)
                    v.wait_ge(ctx.s("tp" + tag), b0["tp"] + n + 1)
                    if n >= 2:
                        sq_.wait(v, n - 2)
                    v.tensor_copy(qts[n % 2][:], psT[:]).then_inc(ctx.s("pe" + tag), 1)
                    ctx.inc("pe" + tag, 1)
                for vc in range(2):
                    k = (i * 6 + 4 + vc) % 3
                    v.wait_ge(ctx.s("mm" + tag), b0["mm"] + i * 6 + 4 + vc + 1)
                    if i >= 2:
                        sv.wait(v, i - 2)
                    v.tensor_copy(
                        vaug[b].ap().rearrange("p (h c) -> p h c", c=128)[:, tsl(vc, 8), 0:64],
                        pq[k].ap().rearrange("p (h c) -> p h c", c=64)).then_inc(ctx.s("ve" + tag), 1)
                    ctx.inc("ve" + tag, 1)
            return ("pe" + tag, ctx.val("pe" + tag))

        ph.gpsimd(f_gpsimd)
        ph.sync(f_sync)
        ph.tensor(f_tensor)
        ph.scalar(f_scalar)
        ph.vector(f_vector)
        ph.emit()


def p_ffn_attn(ctx, wo_d, qsr, ksr, QTl, KTl, VAUGl, Z32, U32, U16,
               consts, tag):
    nc = ctx.nc
    NBLK = ctx.NBLK
    NH = H
    for n in ["mm", "mk", "xp", "pv", "dv", "wm", "ev", "vv", "cb"]:
        ctx.sem(n + tag)
    b0 = {n: ctx.val(n + tag) for n in ["mm", "mk", "xp", "pv", "dv", "wm", "ev", "vv", "cb"]}
    onesb = consts["onesb"]
    lmask_d = consts["lmask_d"]
    wo_r = wo_d.rearrange("(a p) c -> a p c", p=128)
    with contextlib.ExitStack() as es:
        lmask = es.enter_context(nc.sbuf_tensor("lmsk" + tag, [128, 4, 256], F32))
        wo = es.enter_context(nc.sbuf_tensor("wo" + tag, [128, NB, D], BF16))
        qtb = [es.enter_context(nc.sbuf_tensor(f"qt{tag}{i}", [128, NB, 256], BF16)) for i in range(2)]
        ktb = [es.enter_context(nc.sbuf_tensor(f"kt{tag}{i}", [128, NB, 512], BF16)) for i in range(2)]
        vtb = [es.enter_context(nc.sbuf_tensor(f"vt{tag}{i}", [128, 4, 2048], BF16)) for i in range(2)]
        zrb = [es.enter_context(nc.sbuf_tensor(f"zr{tag}{i}", [128, NB, 256], F32)) for i in range(2)]
        pS16 = [es.enter_context(nc.sbuf_tensor(f"pp{tag}{i}", [128, 4, 256], BF16)) for i in range(2)]
        rl = es.enter_context(nc.sbuf_tensor("rl" + tag, [64, 256], F32))
        osb = [es.enter_context(nc.sbuf_tensor(f"ob{tag}{i}", [128, NB, 256], BF16)) for i in range(2)]
        a32 = [es.enter_context(nc.sbuf_tensor(f"a3{tag}{i}", [128, 256], F32)) for i in range(2)]
        o32 = [es.enter_context(nc.sbuf_tensor(f"o3{tag}{i}", [128, 256], F32)) for i in range(2)]
        o16 = [es.enter_context(nc.sbuf_tensor(f"o6{tag}{i}", [128, 256], BF16)) for i in range(2)]
        crow = es.enter_context(nc.sbuf_tensor("cr" + tag, [1, 2, 64], F32))
        cmx = es.enter_context(nc.sbuf_tensor("cm" + tag, [1, 64], F32))
        cneg = es.enter_context(nc.sbuf_tensor("cn" + tag, [1, 1], BF16))
        cbx = es.enter_context(nc.sbuf_tensor("cx" + tag, [128, 1], F32))
        psS = [es.enter_context(nc.psum_tensor(f"pS{tag}{i}", [128, 4, 256], F32)) for i in range(2)]
        psO = [es.enter_context(nc.psum_tensor(f"pO{tag}{i}", [128, 256], F32)) for i in range(2)]
        psW = [es.enter_context(nc.psum_tensor(f"pW{tag}{i}", [128, 256], F32)) for i in range(2)]
        wl = Rot(ctx, "wl" + tag, 1, n_per=NB, swdge=True)
        lq = Rot(ctx, "lb" + tag, 2, n_per=NB)
        lk = Rot(ctx, "lc" + tag, 2, n_per=NB)
        lv = Rot(ctx, "ld" + tag, 2, n_per=4)
        lz = Rot(ctx, "le" + tag, 2, n_per=NB)
        lcr = Rot(ctx, "lf" + tag, 1, n_per=2)
        lmk = Rot(ctx, "lm" + tag, 1)
        so = Rot(ctx, "sg" + tag, 2, n_per=2)
        ph = ctx.phase("fatt" + tag)

        def f_gpsimd(g):
            for j in range(NB):
                wl.issue(0, g.dma_start(out=wo[:, j, :], in_=wo_r[j]))
            return wl.final()[-1]

        def f_sync(sy):
            lmk.issue(0, sy.dma_start(out=lmask[:], in_=lmask_d))
            lcr.issue(0, sy.dma_start(out=crow[0:1, 0, :], in_=qsr))
            lcr.issue(0, sy.dma_start(out=crow[0:1, 1, :], in_=ksr))
            for blk in range(NBLK):
                b = blk % 2
                if blk >= 2:
                    sy.wait_ge(ctx.s("dv" + tag), b0["dv"] + (blk - 1) * NH)
                    sy.wait_ge(ctx.s("vv" + tag), b0["vv"] + (blk - 1) * NB)
                for j in range(NB):
                    lq.issue(blk, sy.dma_start(out=qtb[b][:, j, :], in_=QTl[j, :, tsl(blk, 256)]))
                if blk == 0:
                    for j in range(NB):
                        lk.issue(blk, sy.dma_start(out=ktb[b][:, j, 256:512], in_=KTl[j, :, 0:256]))
                    for s_ in range(2):
                        lv.issue(blk, sy.dma_start(out=vtb[b][:, 2 + s_, :], in_=VAUGl[s_]))
                    for s_ in range(2):
                        lv.issue(blk, sy.dma_start(out=vtb[b][:, s_, 0:64], in_=VAUGl[0, :, 0:64]))
                else:
                    for j in range(NB):
                        lk.issue(blk, sy.dma_start(out=ktb[b][:, j, :],
                                                   in_=KTl[j, :, (blk - 1) * 256:(blk + 1) * 256]))
                    for s_ in range(4):
                        lv.issue(blk, sy.dma_start(out=vtb[b][:, s_, :], in_=VAUGl[2 * (blk - 1) + s_]))
                for j in range(NB):
                    lz.issue(blk, sy.dma_start(out=zrb[b][:, j, :], in_=Z32[j, :, tsl(blk, 256)]))
                if blk >= 1:
                    for co in range(NB):
                        w = (blk - 1) * NB + co
                        sy.wait_ge(ctx.s("vv" + tag), b0["vv"] + w + 1)
                        so.issue(w, sy.dma_start(out=U32[co, :, tsl(blk - 1, 256)], in_=o32[w % 2][:]))
                        so.issue(w, sy.dma_start(out=U16[co, :, tsl(blk - 1, 256)], in_=o16[w % 2][:]))
            for co in range(NB):
                w = (NBLK - 1) * NB + co
                sy.wait_ge(ctx.s("vv" + tag), b0["vv"] + w + 1)
                so.issue(w, sy.dma_start(out=U32[co, :, tsl(NBLK - 1, 256)], in_=o32[w % 2][:]))
                so.issue(w, sy.dma_start(out=U16[co, :, tsl(NBLK - 1, 256)], in_=o16[w % 2][:]))
            return so.final()

        def f_tensor(t):
            t.wait_ge(ctx.s("cb" + tag), b0["cb"] + 1)
            t.matmul(psW[0][:, 0:1], onesb[0:1, :], cneg[0:1, 0:1],
                     start=True, stop=True).then_inc(ctx.s("cb" + tag), 1)
            ctx.inc("cb" + tag, 1)
            wl.wait(t, 0)

            def PV_(PVu):
                t.wait_ge(ctx.s("xp" + tag), b0["xp"] + PVu + 1)
                if PVu >= 1:
                    t.wait_ge(ctx.s("dv" + tag), b0["dv"] + PVu - 1 + 1)
                pb_, ph_ = divmod(PVu, NH)
                pns0 = 2 if pb_ == 0 else 0
                pbb = pb_ % 2
                last = None
                for s_ in range(pns0, 4):
                    last = t.matmul(psO[PVu % 2][:], vtb[pbb][:, s_, ph_ * 128:ph_ * 128 + 128],
                                    pS16[PVu % 2][:, s_, :], start=(s_ == pns0), stop=(s_ == 3))
                last.then_inc(ctx.s("pv" + tag), 1)
                ctx.inc("pv" + tag, 1)

            for blk in range(NBLK):
                b = blk % 2
                if blk >= 1:
                    PV_(blk * NH - 1)
                lq.wait(t, blk)
                lk.wait(t, blk)
                lv.wait(t, blk)
                ns0 = 2 if blk == 0 else 0
                for h in range(NH):
                    u = blk * NH + h
                    if u >= 2:
                        t.wait_ge(ctx.s("xp" + tag), b0["xp"] + u - 1)
                    hr = slice((h % 2) * 64, (h % 2) * 64 + 64)
                    last = None
                    for s_ in range(ns0, 4):
                        last = t.matmul(psS[u % 2][:, s_, :], ktb[b][hr, h // 2, tsl(s_, 128)],
                                        qtb[b][hr, h // 2, :], start=True, stop=True,
                                        tile_position=((h % 2) * 64, 0))
                    last.then_inc(ctx.s("mm" + tag), 1)
                    ctx.inc("mm" + tag, 1)
                    if h >= 1:
                        PV_(u - 1)
                if blk >= 1:
                    wb_ = blk - 1
                    t.wait_ge(ctx.s("dv" + tag), b0["dv"] + wb_ * NH + NH)
                    if wb_ == 0:
                        t.wait_ge(ctx.s("cb" + tag), b0["cb"] + 3)
                    for co in range(NB):
                        w = wb_ * NB + co
                        if w >= 2:
                            t.wait_ge(ctx.s("ev" + tag), b0["ev"] + w - 1)
                        last = None
                        for j in range(NB):
                            last = t.matmul(psW[w % 2][:], wo[:, j, tsl(co, 128)], osb[wb_ % 2][:, j, :],
                                            start=(j == 0), stop=(j == NB - 1))
                        last.then_inc(ctx.s("wm" + tag), 1)
                        ctx.inc("wm" + tag, 1)
            PV_(NBLK * NH - 1)
            wb_ = NBLK - 1
            t.wait_ge(ctx.s("dv" + tag), b0["dv"] + wb_ * NH + NH)
            for co in range(NB):
                w = wb_ * NB + co
                if w >= 2:
                    t.wait_ge(ctx.s("ev" + tag), b0["ev"] + w - 1)
                last = None
                for j in range(NB):
                    last = t.matmul(psW[w % 2][:], wo[:, j, tsl(co, 128)], osb[wb_ % 2][:, j, :],
                                    start=(j == 0), stop=(j == NB - 1))
                last.then_inc(ctx.s("wm" + tag), 1)
                ctx.inc("wm" + tag, 1)
            return ("wm" + tag, ctx.val("wm" + tag))

        def f_vector(v):
            lcr.wait(v, 0)
            v.tensor_tensor(cmx[:], crow[0:1, 0, :], crow[0:1, 1, :], ALU.mult)
            v.reduce_max(cmx[:, 0:1], cmx[:], AX.X)
            v.tensor_scalar(cneg[:], cmx[:, 0:1], -8.0, None, ALU.mult).then_inc(ctx.s("cb" + tag), 1)
            ctx.inc("cb" + tag, 1)
            v.wait_ge(ctx.s("cb" + tag), b0["cb"] + 2)
            v.tensor_copy(cbx[:], psW[0][:, 0:1]).then_inc(ctx.s("cb" + tag), 1)
            ctx.inc("cb" + tag, 1)
            lmk.wait(v, 0)

            def div_(u):
                db_, dh_ = divmod(u, NH)
                v.wait_ge(ctx.s("pv" + tag), b0["pv"] + u + 1)
                if db_ >= 2 and dh_ == 0:
                    v.wait_ge(ctx.s("wm" + tag), b0["wm"] + (db_ - 1) * NB)
                v.reciprocal(rl[:], psO[u % 2][64:128, :])
                v.tensor_tensor(osb[db_ % 2][(dh_ % 2) * 64:(dh_ % 2) * 64 + 64, dh_ // 2, :],
                                psO[u % 2][0:64, :], rl[:], ALU.mult)
                vfence(v, consts).then_inc(ctx.s("dv" + tag), 1)
                ctx.inc("dv" + tag, 1)

            for blk in range(NBLK):
                ns0 = 2 if blk == 0 else 0
                if blk >= 1:
                    div_(blk * NH - 1)
                for h in range(NH):
                    u = blk * NH + h
                    v.wait_ge(ctx.s("mm" + tag), b0["mm"] + u + 1)
                    v.tensor_tensor(psS[u % 2][:, ns0:4, :], psS[u % 2][:, ns0:4, :],
                                    lmask[:, ns0:4, :], ALU.add)
                    vfence(v, consts).then_inc(ctx.s("mk" + tag), 1)
                    ctx.inc("mk" + tag, 1)
                    if h >= 1:
                        div_(u - 1)
                if blk >= 1:
                    wb_ = blk - 1
                    for co in range(NB):
                        w = wb_ * NB + co
                        v.wait_ge(ctx.s("ev" + tag), b0["ev"] + w + 1)
                        lz.wait(v, wb_)
                        if w >= 2:
                            so.wait(v, w - 2)
                        v.tensor_tensor(o32[w % 2][:], a32[w % 2][:], zrb[wb_ % 2][:, co, :], ALU.add)
                        v.tensor_copy(o16[w % 2][:], o32[w % 2][:])
                        vfence(v, consts).then_inc(ctx.s("vv" + tag), 1)
                        ctx.inc("vv" + tag, 1)
            div_(NBLK * NH - 1)
            wb_ = NBLK - 1
            for co in range(NB):
                w = wb_ * NB + co
                v.wait_ge(ctx.s("ev" + tag), b0["ev"] + w + 1)
                lz.wait(v, wb_)
                if w >= 2:
                    so.wait(v, w - 2)
                v.tensor_tensor(o32[w % 2][:], a32[w % 2][:], zrb[wb_ % 2][:, co, :], ALU.add)
                v.tensor_copy(o16[w % 2][:], o32[w % 2][:])
                vfence(v, consts).then_inc(ctx.s("vv" + tag), 1)
                ctx.inc("vv" + tag, 1)
            return ("vv" + tag, ctx.val("vv" + tag))

        def f_scalar(s):
            s.wait_ge(ctx.s("cb" + tag), b0["cb"] + 3)
            for blk in range(NBLK):
                ns0 = 2 if blk == 0 else 0
                for h in range(NH):
                    u = blk * NH + h
                    s.wait_ge(ctx.s("mk" + tag), b0["mk"] + u + 1)
                    if u >= 2:
                        s.wait_ge(ctx.s("pv" + tag), b0["pv"] + u - 1)
                    s.activation(pS16[u % 2][:, ns0:4, :], psS[u % 2][:, ns0:4, :], AF.Exp,
                                 bias=cbx[:, 0:1])
                    sfence(s, consts).then_inc(ctx.s("xp" + tag), 1)
                    ctx.inc("xp" + tag, 1)
                if blk >= 1:
                    wb_ = blk - 1
                    for co in range(NB):
                        w = wb_ * NB + co
                        s.wait_ge(ctx.s("wm" + tag), b0["wm"] + w + 1)
                        if w >= 2:
                            s.wait_ge(ctx.s("vv" + tag), b0["vv"] + w - 1)
                        s.activation(a32[w % 2][:], psW[w % 2][:], AF.Identity,
                                     bias=consts["scal"][:, 3:4])
                        sfence(s, consts).then_inc(ctx.s("ev" + tag), 1)
                        ctx.inc("ev" + tag, 1)
            wb_ = NBLK - 1
            for co in range(NB):
                w = wb_ * NB + co
                s.wait_ge(ctx.s("wm" + tag), b0["wm"] + w + 1)
                if w >= 2:
                    s.wait_ge(ctx.s("vv" + tag), b0["vv"] + w - 1)
                s.activation(a32[w % 2][:], psW[w % 2][:], AF.Identity,
                             bias=consts["scal"][:, 3:4])
                sfence(s, consts).then_inc(ctx.s("ev" + tag), 1)
                ctx.inc("ev" + tag, 1)
            return ("ev" + tag, ctx.val("ev" + tag))

        ph.gpsimd(f_gpsimd)
        ph.sync(f_sync)
        ph.tensor(f_tensor)
        ph.vector(f_vector)
        ph.scalar(f_scalar)
        ph.emit()


def p_ffn_geglu(ctx, w1_d, w2_d, b1_sb, b2_sb, USH16, U32, Z32n, Z16n,
                consts, tag):
    nc = ctx.nc
    NT = ctx.NT
    NK = 22
    CW = [128] * 21 + [42]
    for n in ["ma", "ev", "hh", "mw", "e2", "vv"]:
        ctx.sem(n + tag)
    b0 = {n: ctx.val(n + tag) for n in ["ma", "ev", "hh", "mw", "e2", "vv"]}
    w1_r = w1_d.rearrange("(a p) c -> a p c", p=128)
    with contextlib.ExitStack() as es:
        w1 = es.enter_context(nc.sbuf_tensor("w1" + tag, [128, NB, 5460], BF16))
        w2 = es.enter_context(nc.sbuf_tensor("w2" + tag, [128, NK, D], BF16))
        ush = [es.enter_context(nc.sbuf_tensor(f"us{tag}{i}", [128, NB, 512], BF16)) for i in range(2)]
        ur = [es.enter_context(nc.sbuf_tensor(f"ur{tag}{i}", [128, 512], F32)) for i in range(2)]
        h16 = es.enter_context(nc.sbuf_tensor("h6" + tag, [128, NK, 512], BF16))
        ga32 = [es.enter_context(nc.sbuf_tensor(f"ga{tag}{i}", [128, 512], F32)) for i in range(2)]
        gg32 = [es.enter_context(nc.sbuf_tensor(f"gg{tag}{i}", [128, 512], F32)) for i in range(2)]
        z32 = [es.enter_context(nc.sbuf_tensor(f"z3{tag}{i}", [128, 512], F32)) for i in range(2)]
        z16 = [es.enter_context(nc.sbuf_tensor(f"z6{tag}{i}", [128, 512], BF16)) for i in range(2)]
        a2 = [es.enter_context(nc.sbuf_tensor(f"a2{tag}{i}", [128, 512], F32)) for i in range(2)]
        psA = [es.enter_context(nc.psum_tensor(f"pA{tag}{i}", [128, 512], F32)) for i in range(2)]
        psG = [es.enter_context(nc.psum_tensor(f"pG{tag}{i}", [128, 512], F32)) for i in range(2)]
        psW = [es.enter_context(nc.psum_tensor(f"pw{tag}{i}", [128, 512], F32)) for i in range(2)]
        wl = Rot(ctx, "wl" + tag, 1, n_per=NB + NK, swdge=True)
        lu = Rot(ctx, "lu" + tag, 2, n_per=NB)
        rr = Rot(ctx, "rr" + tag, 2, n_per=1)
        so = Rot(ctx, "sh" + tag, 2, n_per=2)
        ph = ctx.phase("geglu" + tag)

        def f_gpsimd(g):
            for j in range(NB):
                wl.issue(0, g.dma_start(out=w1[:, j, :], in_=w1_r[j]))
            for k in range(NK):
                wl.issue(0, g.dma_start(out=w2[0:CW[k], k, :], in_=w2_d[128 * k:128 * k + CW[k], :]))
            return wl.final()[-1]

        def f_sync2(sy):
            for tt in range(NT):
                b = tt % 2
                if tt >= 2:
                    sy.wait_ge(ctx.s("ma" + tag), b0["ma"] + (tt - 1) * 2 * NK)
                for j in range(NB):
                    lu.issue(tt, sy.dma_start(out=ush[b][:, j, :], in_=USH16[j, :, tsl(tt, 512)]))
                for co in range(NB):
                    w = tt * NB + co
                    if w >= 2:
                        sy.wait_ge(ctx.s("vv" + tag), b0["vv"] + w - 1)
                    rr.issue(w, sy.dma_start(out=ur[w % 2][:], in_=U32[co, :, tsl(tt, 512)]))
                    sy.wait_ge(ctx.s("vv" + tag), b0["vv"] + w + 1)
                    so.issue(w, sy.dma_start(out=Z32n[co, :, tsl(tt, 512)], in_=z32[w % 2][:]))
                    so.issue(w, sy.dma_start(out=Z16n[co, :, tsl(tt, 512)], in_=z16[w % 2][:]))
            return so.final()

        def f_tensor(t):
            wl.wait(t, 0)
            for tt in range(NT):
                b = tt % 2
                lu.wait(t, tt)
                for k in range(NK):
                    n = tt * NK + k
                    if n >= 2:
                        t.wait_ge(ctx.s("ev" + tag), b0["ev"] + n - 1)
                    last = None
                    for j in range(NB):
                        last = t.matmul(psA[n % 2][0:CW[k], :], w1[:, j, 128 * k:128 * k + CW[k]],
                                        ush[b][:, j, :], start=(j == 0), stop=(j == NB - 1))
                    last.then_inc(ctx.s("ma" + tag), 1)
                    ctx.inc("ma" + tag, 1)
                    last = None
                    for j in range(NB):
                        last = t.matmul(psG[n % 2][0:CW[k], :], w1[:, j, 2730 + 128 * k:2730 + 128 * k + CW[k]],
                                        ush[b][:, j, :], start=(j == 0), stop=(j == NB - 1))
                    last.then_inc(ctx.s("ma" + tag), 1)
                    ctx.inc("ma" + tag, 1)
                t.wait_ge(ctx.s("hh" + tag), b0["hh"] + (tt + 1) * NK)
                for co in range(NB):
                    w = tt * NB + co
                    if w >= 2:
                        t.wait_ge(ctx.s("e2" + tag), b0["e2"] + w - 1)
                    last = None
                    for k in range(NK):
                        last = t.matmul(psW[w % 2][:], w2[0:CW[k], k, tsl(co, 128)], h16[0:CW[k], k, :],
                                        start=(k == 0), stop=(k == NK - 1))
                    last.then_inc(ctx.s("mw" + tag), 1)
                    ctx.inc("mw" + tag, 1)
            return ("mw" + tag, ctx.val("mw" + tag))

        def f_scalar(s):
            for tt in range(NT):
                for k in range(NK):
                    n = tt * NK + k
                    s.wait_ge(ctx.s("ma" + tag), b0["ma"] + 2 * n + 2)
                    if n >= 2:
                        s.wait_ge(ctx.s("hh" + tag), b0["hh"] + n - 1)
                    s.activation(ga32[n % 2][0:CW[k], :], psA[n % 2][0:CW[k], :], AF.Identity,
                                 bias=b1_sb[0:CW[k], k:k + 1])
                    s.activation(gg32[n % 2][0:CW[k], :], psG[n % 2][0:CW[k], :], AF.Gelu,
                                 bias=b1_sb[0:CW[k], 22 + k:23 + k]).then_inc(ctx.s("ev" + tag), 1)
                    ctx.inc("ev" + tag, 1)
                for co in range(NB):
                    w = tt * NB + co
                    s.wait_ge(ctx.s("mw" + tag), b0["mw"] + w + 1)
                    if w >= 2:
                        s.wait_ge(ctx.s("vv" + tag), b0["vv"] + w - 1)
                    s.activation(a2[w % 2][:], psW[w % 2][:], AF.Identity,
                                 bias=b2_sb[:, co:co + 1]).then_inc(ctx.s("e2" + tag), 1)
                    ctx.inc("e2" + tag, 1)
            return ("e2" + tag, ctx.val("e2" + tag))

        def f_vector(v):
            for tt in range(NT):
                for k in range(NK):
                    n = tt * NK + k
                    v.wait_ge(ctx.s("ev" + tag), b0["ev"] + n + 1)
                    if tt >= 1 and k == 0:
                        v.wait_ge(ctx.s("mw" + tag), b0["mw"] + tt * NB)
                    v.tensor_tensor(h16[0:CW[k], k, :], ga32[n % 2][0:CW[k], :], gg32[n % 2][0:CW[k], :],
                                    ALU.mult).then_inc(ctx.s("hh" + tag), 1)
                    ctx.inc("hh" + tag, 1)
                for co in range(NB):
                    w = tt * NB + co
                    v.wait_ge(ctx.s("e2" + tag), b0["e2"] + w + 1)
                    rr.wait(v, w)
                    if w >= 2:
                        so.wait(v, w - 2)
                    v.tensor_tensor(z32[w % 2][:], a2[w % 2][:], ur[w % 2][:], ALU.add)
                    v.tensor_copy(z16[w % 2][:], z32[w % 2][:]).then_inc(ctx.s("vv" + tag), 1)
                    ctx.inc("vv" + tag, 1)
            return ("vv" + tag, ctx.val("vv" + tag))

        ph.gpsimd(f_gpsimd)
        ph.sync(f_sync2)
        ph.tensor(f_tensor)
        ph.scalar(f_scalar)
        ph.vector(f_vector)
        ph.emit()


def p_final(ctx, x_nat, S32, Z32, OUT, consts, tag="fz"):
    nc = ctx.nc
    NT = ctx.NT
    for n in ["sg", "gm", "tp", "ev", "ad"]:
        ctx.sem(n + tag)
    b0 = {n: ctx.val(n + tag) for n in ["sg", "gm", "tp", "ev", "ad"]}
    x_r = x_nat.rearrange("(n p) d -> n p d", p=128)
    out_r = OUT.rearrange("(n p) d -> n p d", p=128)
    idf = consts["idf"]
    with contextlib.ExitStack() as es:
        sb = [es.enter_context(nc.sbuf_tensor(f"sb{tag}{i}", [128, NB, 512], F32)) for i in range(2)]
        zb = [es.enter_context(nc.sbuf_tensor(f"zb{tag}{i}", [128, NB, 512], F32)) for i in range(2)]
        gb = es.enter_context(nc.sbuf_tensor("gb" + tag, [128, NB, 512], F32))
        xb = [es.enter_context(nc.sbuf_tensor(f"xb{tag}{i}", [128, D], F32)) for i in range(2)]
        ob = [es.enter_context(nc.sbuf_tensor(f"ob{tag}{i}", [128, D], F32)) for i in range(2)]
        pst = es.enter_context(nc.psum_tensor("pt" + tag, [128, D], F32))
        ls = Rot(ctx, "ls" + tag, 2, n_per=2 * NB)
        lx = Rot(ctx, "lx" + tag, 2, n_per=1)
        st = Rot(ctx, "st" + tag, 2, n_per=1)
        ph = ctx.phase("final" + tag)

        def f_sync(sy):
            for tt in range(NT):
                b = tt % 2
                if tt >= 2:
                    sy.wait_ge(ctx.s("tp" + tag), b0["tp"] + (tt - 1) * 4)
                    sy.wait_ge(ctx.s("sg" + tag), b0["sg"] + tt - 1)
                for j in range(NB):
                    ls.issue(tt, sy.dma_start(out=sb[b][:, j, :], in_=S32[j, :, tsl(tt, 512)]))
                    ls.issue(tt, sy.dma_start(out=zb[b][:, j, :], in_=Z32[j, :, tsl(tt, 512)]))
                for c in range(4):
                    i = tt * 4 + c
                    if i >= 2:
                        sy.wait_ge(ctx.s("ad" + tag), b0["ad"] + i - 1)
                    lx.issue(i, sy.dma_start(out=xb[i % 2][:], in_=x_r[i]))
                    sy.wait_ge(ctx.s("ad" + tag), b0["ad"] + i + 1)
                    st.issue(i, sy.dma_start(out=out_r[i], in_=ob[i % 2][:]))
            return st.final()

        def f_scalar(s):
            for tt in range(NT):
                b = tt % 2
                ls.wait(s, tt)
                if tt >= 1:
                    s.wait_ge(ctx.s("gm" + tag), b0["gm"] + tt)
                s.activation(gb[:].rearrange("p a t -> p (a t)"), zb[b][:].rearrange("p a t -> p (a t)"),
                             AF.Silu).then_inc(ctx.s("sg" + tag), 1)
                ctx.inc("sg" + tag, 1)
            return ("sg" + tag, ctx.val("sg" + tag))

        def f_gpsimd(g):
            for tt in range(NT):
                b = tt % 2
                g.wait_ge(ctx.s("sg" + tag), b0["sg"] + tt + 1)
                g.tensor_mul(sb[b][:].rearrange("p a t -> p (a t)"),
                             sb[b][:].rearrange("p a t -> p (a t)"),
                             gb[:].rearrange("p a t -> p (a t)")).then_inc(ctx.s("gm" + tag), 1)
                ctx.inc("gm" + tag, 1)
            return ("gm" + tag, ctx.val("gm" + tag))

        def f_tensor(t):
            for tt in range(NT):
                b = tt % 2
                t.wait_ge(ctx.s("gm" + tag), b0["gm"] + tt + 1)
                for c in range(4):
                    i = tt * 4 + c
                    if i >= 1:
                        t.wait_ge(ctx.s("ev" + tag), b0["ev"] + i)
                    last = None
                    for j in range(NB):
                        last = t.transpose(pst[:, tsl(j, 128)], sb[b][:, j, tsl(c, 128)], idf[:])
                    last.then_inc(ctx.s("tp" + tag), 1)
                    ctx.inc("tp" + tag, 1)
            return ("tp" + tag, ctx.val("tp" + tag))

        def f_vector(v):
            for tt in range(NT):
                for c in range(4):
                    i = tt * 4 + c
                    v.wait_ge(ctx.s("tp" + tag), b0["tp"] + i + 1)
                    v.tensor_copy(ob[i % 2][:], pst[:]).then_inc(ctx.s("ev" + tag), 1)
                    ctx.inc("ev" + tag, 1)
                    lx.wait(v, i)
                    if i >= 2:
                        st.wait(v, i - 2)
                    v.tensor_tensor(ob[i % 2][:], ob[i % 2][:], xb[i % 2][:], ALU.add).then_inc(ctx.s("ad" + tag), 1)
                    ctx.inc("ad" + tag, 1)
            return ("ad" + tag, ctx.val("ad" + tag))

        ph.sync(f_sync)
        ph.scalar(f_scalar)
        ph.gpsimd(f_gpsimd)
        ph.tensor(f_tensor)
        ph.vector(f_vector)
        ph.emit()


import ml_dtypes


def build_full(S=4096, race=False, dbg=False):
    nc = bass.Bass("TRN2", target_bir_lowering=False, detect_race_conditions=race)
    NTT = S // 128
    with contextlib.ExitStack() as stack:
        ctx = Ctx(nc, stack, S)
        DT = nc.dram_tensor
        x = DT("x", [S, 1024], F32, kind="ExternalInput").ap()
        qkvw = DT("qkv_w", [1024, 3072], F32, kind="ExternalInput").ap()
        outw = DT("out_w", [1024, 1024], F32, kind="ExternalInput").ap()
        wctx_d = DT("w_ctx", [1024, 1024], F32, kind="ExternalInput").ap()
        wx_d = DT("w_x", [1024, 1024], F32, kind="ExternalInput").ap()
        ffw1_d = DT("ff_w1", [1024, 4096], F32, kind="ExternalInput").ap()
        ffw2_d = DT("ff_w2", [4096, 1024], F32, kind="ExternalInput").ap()
        Wqkv = DT("fWqkv", [6, 1024, 3072], F32, kind="ExternalInput").ap()
        Wo = DT("fWo", [6, 1024, 1024], F32, kind="ExternalInput").ap()
        W1 = DT("fW1", [6, 1024, 5460], F32, kind="ExternalInput").ap()
        W2 = DT("fW2", [6, 2730, 1024], F32, kind="ExternalInput").ap()
        ins_small = {}
        for nm, shp in [("P_qkvb", [128, 24]), ("P_qkvb_row", [1, 3072]),
                        ("P_outb", [128, 8]), ("P_gattn", [128, 8]), ("P_battn", [128, 8]),
                        ("P_g1", [128, 8]), ("P_b1", [128, 8]), ("P_g2", [128, 8]),
                        ("P_b2", [128, 8]), ("P_binl", [128, 8]),
                        ("P_ffb1", [128, 32]), ("P_ffb2", [128, 8]),
                        ("P_qs", [6, 64]), ("P_ks", [6, 64]), ("P_qbc", [6, 128, 4, 64]),
                        ("P_lng", [6, 128, 8]), ("P_lnb", [6, 128, 8]),
                        ("P_fb1", [6, 128, 44]), ("P_fb2", [6, 128, 8]),
                        ("C_idf", [128, 128]), ("C_scal", [128, 8]),
                        ("C_cmask", [128, 4, 512]), ("C_lmask", [128, 4, 256]),
                        ("C_rqc", [NTT, 128, 64]), ("C_rqs", [NTT, 128, 64]),
                        ("C_rkc", [NTT, 128, 64]), ("C_rks", [NTT, 128, 64])]:
            ins_small[nm] = DT(nm, shp, F32, kind="ExternalInput").ap()
        for nm, shp in [("C_idb", [128, 128]), ("C_onesb", [128, 128])]:
            ins_small[nm] = DT(nm, shp, BF16, kind="ExternalInput").ap()
        OUT = DT("OUT", [S, 1024], F32, kind="ExternalOutput").ap()

        ST = lambda nm, f32=True: DT(nm, [NB, 128, S], F32 if f32 else BF16,
                                     kind=("ExternalOutput" if dbg else "Internal")).ap()
        XT32, XT16 = ST("XT32"), ST("XT16", False)
        XN32, XN16 = ST("XN32"), ST("XN16", False)
        QT, KT = ST("QT", False), ST("KT", False)
        VAC = DT("VAC", [NTT, 128, 2048], BF16, kind="Internal").ap()
        OTC = ST("OTC", False)
        CTXB = ST("CTXB", False)
        Y1, Y1B = ST("Y1"), ST("Y1B", False)
        XS0A, XS0B = ST("XS0A"), ST("XS0B", False)
        Y2, Y2B = ST("Y2"), ST("Y2B", False)
        HN16 = ST("HN16", False)
        H16 = DT("H16", [32, 128, S], BF16, kind="Internal").ap()
        S32 = ST("S32")
        ZA32, ZA16 = ST("ZA32"), ST("ZA16", False)
        ZB32, ZB16 = ST("ZB32"), ST("ZB16", False)
        U32, U16 = ST("U32"), ST("U16", False)
        USH16 = ST("USH16", False)
        QTl, KTl = ST("QTl", False), ST("KTl", False)
        VAUGl = DT("VAUGl", [NTT, 128, 2048], BF16, kind="Internal").ap()

        SB = {}
        for nm, shp, dt_ in [("idf", [128, 128], F32), ("idb", [128, 128], BF16),
                             ("onesb", [128, 128], BF16), ("scal", [128, 8], F32),
                             ("qkvb", [128, 24], F32), ("qkvbr", [1, 3072], F32),
                             ("outb", [128, 8], F32), ("ga", [128, 8], F32), ("ba", [128, 8], F32),
                             ("g1", [128, 8], F32), ("b1", [128, 8], F32),
                             ("g2", [128, 8], F32), ("b2", [128, 8], F32), ("binl", [128, 8], F32),
                             ("ffb1", [128, 32], F32), ("ffb2", [128, 8], F32),
                             ("lng", [128, 6, 8], F32), ("lnb", [128, 6, 8], F32),
                             ("fb1", [128, 6, 44], F32), ("fb2", [128, 6, 8], F32),
                             ("fence", [1, 4], F32)]:
            SB[nm] = stack.enter_context(nc.sbuf_tensor(nm, shp, dt_))
        consts = {"idf": SB["idf"], "idb": SB["idb"], "onesb": SB["onesb"],
                  "scal": SB["scal"], "cmask_d": ins_small["C_cmask"],
                  "lmask_d": ins_small["C_lmask"], "qkvb_row": SB["qkvbr"],
                  "fence": SB["fence"]}

        ctx.sem("cl")
        ph = ctx.phase("consts")
        def f_sync(sy):
            pairs = [(SB["idf"][:], ins_small["C_idf"]), (SB["idb"][:], ins_small["C_idb"]),
                     (SB["onesb"][:], ins_small["C_onesb"]), (SB["scal"][:], ins_small["C_scal"]),
                     (SB["qkvb"][:], ins_small["P_qkvb"]), (SB["qkvbr"][:], ins_small["P_qkvb_row"]),
                     (SB["outb"][:], ins_small["P_outb"]), (SB["ga"][:], ins_small["P_gattn"]),
                     (SB["ba"][:], ins_small["P_battn"]), (SB["g1"][:], ins_small["P_g1"]),
                     (SB["b1"][:], ins_small["P_b1"]), (SB["g2"][:], ins_small["P_g2"]),
                     (SB["b2"][:], ins_small["P_b2"]), (SB["binl"][:], ins_small["P_binl"]),
                     (SB["ffb1"][:], ins_small["P_ffb1"]), (SB["ffb2"][:], ins_small["P_ffb2"])]
            for li in range(6):
                pairs += [(SB["lng"][:, li, :], ins_small["P_lng"][li]),
                          (SB["lnb"][:, li, :], ins_small["P_lnb"][li]),
                          (SB["fb1"][:, li, :], ins_small["P_fb1"][li]),
                          (SB["fb2"][:, li, :], ins_small["P_fb2"][li])]
            for dst, src in pairs:
                sy.dma_start(out=dst, in_=src).then_inc(ctx.s("cl"), 16)
                ctx.inc("cl", 16)
            return ("cl", ctx.val("cl"))
        ph.sync(f_sync)
        ph.emit()

        p_nat_to_t(ctx, x, XT32, XT16, consts, "A")
        p_ln_t(ctx, XT32, XT16, None, None, XN32, XN16, consts, "B")
        with contextlib.ExitStack() as s2:
            xn2b = s2.enter_context(nc.sbuf_tensor("xn2b", [128, NB, S], BF16))
            p_ln_t(ctx, XN32, XN16, SB["ga"], SB["ba"], None, None, consts, "C", out16_sb=xn2b)
            p_attn_qkv(ctx, qkvw, SB["qkvb"], xn2b, QT, KT, VAC, consts, "D")
        p_causal_attn(ctx, QT, KT, VAC, OTC, consts, "E")
        p_proj_t(ctx, outw, OTC, SB["outb"], CTXB, XN32, Y1, Y1B, consts, "F")
        p_ln_t(ctx, Y1, Y1B, SB["g1"], SB["b1"], XS0A, XS0B, consts, "G")
        p_scan(ctx, wctx_d, wx_d, SB["binl"], CTXB, XS0A, XS0B, Y1, Y2, Y2B, consts, "H")
        p_ln_t(ctx, Y2, Y2B, SB["g2"], SB["b2"], None, HN16, consts, "I")
        p_proj_t(ctx, ffw1_d, HN16, SB["ffb1"], H16, None, None, None, consts, "J",
                 nbout=32, act=AF.Gelu)
        p_proj_t(ctx, ffw2_d, H16, SB["ffb2"], None, Y2, S32, None, consts, "K", nbin=32)

        if dbg:
            ZDBG = [ST("ZD%d" % li) for li in range(6)]
            UDBG = [ST("UD%d" % li) for li in range(6)]
            QDBG = [ST("QD%d" % li, False) for li in range(6)]
            KDBG = [ST("KD%d" % li, False) for li in range(6)]
            VDBG = [DT("VD%d" % li, [NTT, 128, 2048], BF16, kind="ExternalOutput").ap() for li in range(6)]
        zin32, zin16 = XN32, XN16
        for li in range(6):
            zo32 = (ZDBG[li] if dbg else (ZA32 if li % 2 == 0 else ZB32))
            zo16 = ZA16 if li % 2 == 0 else ZB16
            p_ffn_qkv(ctx, Wqkv[li], ins_small["P_qbc"][li],
                      ins_small["C_rqc"], ins_small["C_rqs"], ins_small["C_rkc"], ins_small["C_rks"],
                      zin16, (QDBG[li] if dbg else QTl), (KDBG[li] if dbg else KTl),
                      (VDBG[li] if dbg else VAUGl), consts, "P%d" % li)
            p_ffn_attn(ctx, Wo[li], ins_small["P_qs"][li:li+1, :], ins_small["P_ks"][li:li+1, :],
                       (QDBG[li] if dbg else QTl), (KDBG[li] if dbg else KTl),
                       (VDBG[li] if dbg else VAUGl), zin32, (UDBG[li] if dbg else U32), U16, consts, "Q%d" % li)
            uu = UDBG[li] if dbg else U32
            p_ln_t(ctx, uu, U16, SB["lng"][:, li, :], SB["lnb"][:, li, :], None, USH16,
                   consts, "R%d" % li, shift=True)
            p_ffn_geglu(ctx, W1[li], W2[li], SB["fb1"][:, li, :], SB["fb2"][:, li, :],
                        USH16, uu, zo32, zo16, consts, "S%d" % li)
            zin32, zin16 = zo32, zo16

        p_final(ctx, x, S32, zin32, OUT, consts, "Z")
    return nc


def _host_inputs(S=4096):
    NTT = S // 128
    HDh = np.arange(0, 64, 2, dtype=np.float64)
    inv_freq = 1.0 / 10000.0 ** (HDh / 64)
    t = np.arange(S, dtype=np.float64)
    freqs = np.concatenate([t[:, None] * inv_freq[None, :]] * 2, 1)
    base_scale = (HDh + 0.4 * 64) / (1.4 * 64)
    power = (t - S // 2) / 128.0
    scale = np.concatenate([base_scale[None, :] ** power[:, None]] * 2, 1)
    sgn = np.where(np.arange(64) < 32, -1.0, 1.0)[None, :]
    C = {}
    C["C_rqc"] = (np.cos(freqs) * scale).astype(np.float32).reshape(NTT, 128, 64)
    C["C_rqs"] = (sgn * np.sin(freqs) * scale).astype(np.float32).reshape(NTT, 128, 64)
    C["C_rkc"] = (np.cos(freqs) / scale).astype(np.float32).reshape(NTT, 128, 64)
    C["C_rks"] = (sgn * np.sin(freqs) / scale).astype(np.float32).reshape(NTT, 128, 64)
    lm = np.zeros((128, 4, 256), np.float32)
    kk = np.arange(128)[:, None]; qq = np.arange(256)[None, :]
    for s_ in range(4):
        d = qq - kk + 256 - 128 * s_
        lm[:, s_, :] = np.where((d >= 0) & (d <= 256), 0.0, NEG)
    C["C_lmask"] = lm
    cm = np.zeros((128, 4, 512), np.float32)
    qq = np.arange(512)[None, :]
    for r in range(4):
        cm[:, r, :] = np.where(kk <= qq - 128 * r, 0.0, NEG)
    C["C_cmask"] = cm
    C["C_idf"] = np.eye(128, dtype=np.float32)
    C["C_idb"] = np.eye(128, dtype=ml_dtypes.bfloat16)
    C["C_onesb"] = np.ones((128, 128), dtype=ml_dtypes.bfloat16)
    sc = np.zeros((128, 8), np.float32)
    sc[:, 0] = 1e-5; sc[:, 1] = 1.0; sc[:, 4] = 1e-38
    C["C_scal"] = sc
    return C


def _layout_params(I):
    P = {}
    r8 = lambda a: np.ascontiguousarray(np.asarray(a, np.float32).reshape(8, 128).T)
    P["P_qkvb"] = np.ascontiguousarray(np.asarray(I["L0_attn_qkv_b"], np.float32).reshape(24, 128).T)
    P["P_qkvb_row"] = np.asarray(I["L0_attn_qkv_b"], np.float32).reshape(1, 3072)
    P["P_outb"] = r8(I["L0_attn_out_b"])
    P["P_gattn"] = r8(I["L0_attn_g_attn"]); P["P_battn"] = r8(I["L0_attn_b_attn"])
    P["P_g1"] = r8(I["L0_attn_g1"]); P["P_b1"] = r8(I["L0_attn_b1"])
    P["P_g2"] = r8(I["L0_attn_g2"]); P["P_b2"] = r8(I["L0_attn_b2"])
    P["P_binl"] = r8(I["L0_attn_b_inl"])
    P["P_ffb1"] = np.ascontiguousarray(np.asarray(I["L0_attn_ff_b1"], np.float32).reshape(32, 128).T)
    P["P_ffb2"] = r8(I["L0_attn_ff_b2"])
    qs = np.asarray(I["L0_ffn_q_scale"], np.float32)
    ks = np.asarray(I["L0_ffn_k_scale"], np.float32)
    perm = (np.arange(64) + 32) % 64
    P["P_qs"] = np.ascontiguousarray(qs); P["P_ks"] = np.ascontiguousarray(ks)
    rows = np.stack([qs, qs[:, perm], ks, ks[:, perm]], 1)
    P["P_qbc"] = np.ascontiguousarray(
        np.broadcast_to(rows[:, None, :, :], (6, 128, 4, 64)).astype(np.float32))
    P["P_lng"] = np.ascontiguousarray(np.asarray(I["L0_ffn_ln_g"], np.float32).reshape(6, 8, 128).transpose(0, 2, 1))
    P["P_lnb"] = np.ascontiguousarray(np.asarray(I["L0_ffn_ln_b"], np.float32).reshape(6, 8, 128).transpose(0, 2, 1))
    fb1 = np.zeros((6, 128, 44), np.float32)
    b1v = np.asarray(I["L0_ffn_b1"], np.float32)
    for li in range(6):
        for k in range(22):
            w = min(128, 2730 - 128 * k)
            fb1[li, 0:w, k] = b1v[li, 128 * k:128 * k + w]
            fb1[li, 0:w, 22 + k] = b1v[li, 2730 + 128 * k:2730 + 128 * k + w]
    P["P_fb1"] = fb1
    P["P_fb2"] = np.ascontiguousarray(np.asarray(I["L0_ffn_b2"], np.float32).reshape(6, 8, 128).transpose(0, 2, 1))
    return P


def kernel(**inputs):
    from concourse.bass_utils import run_bass_kernel_spmd
    S = 4096
    nc = build_full(S=S)
    C = _host_inputs(S)
    P = _layout_params(inputs)
    f32c = lambda a: np.ascontiguousarray(np.asarray(a, np.float32))
    shared = dict(C)
    shared.update(P)
    shared["qkv_w"] = f32c(inputs["L0_attn_qkv_w"])
    shared["out_w"] = f32c(inputs["L0_attn_out_w"])
    shared["w_ctx"] = f32c(inputs["L0_attn_w_ctx"])
    shared["w_x"] = f32c(inputs["L0_attn_w_x"])
    shared["ff_w1"] = f32c(inputs["L0_attn_ff_w1"])
    shared["ff_w2"] = f32c(inputs["L0_attn_ff_w2"])
    shared["fWqkv"] = f32c(inputs["L0_ffn_Wqkv"])
    shared["fWo"] = f32c(inputs["L0_ffn_Wo"])
    shared["fW1"] = f32c(inputs["L0_ffn_W1"])
    shared["fW2"] = f32c(inputs["L0_ffn_W2"])
    xfull = np.asarray(inputs["L0_attn_x"], np.float32)
    in_maps = []
    for b in range(8):
        m = dict(shared)
        m["x"] = np.ascontiguousarray(xfull[b])
        in_maps.append(m)
    res = run_bass_kernel_spmd(nc, in_maps, core_ids=list(range(8)))
    out = np.stack([np.asarray(res.results[b]["OUT"], np.float32) for b in range(8)], axis=0)
    return out


# version 05
# speedup vs baseline: 1.0605x; 1.0605x over previous
import numpy as np
import contextlib
import concourse.bass as bass
import concourse.mybir as mybir

F32 = mybir.dt.float32
BF16 = mybir.dt.bfloat16
AF = mybir.ActivationFunctionType
ALU = mybir.AluOpType
AX = mybir.AxisListType

D = 1024
NB = 8
H = 16
HD = 64
NEG = -1.0e5
EPS = 1e-5
POOL = 68


def tsl(i, n):
    return slice(i * n, (i + 1) * n)


class Ctx:
    def __init__(self, nc, stack, S):
        self.nc = nc
        self.stack = stack
        self.S = S
        self.NT = S // 512
        self.NTT = S // 128
        self.NBLK = S // 256
        self.pool = [stack.enter_context(nc.semaphore(f"sp{i}")) for i in range(POOL)]
        self.pool_cnt = [0] * POOL
        self.next_slot = 1
        self.names = {}
        self.bar_cnt = 0
        self.names["bar"] = 0
        self.bar = self.pool[0]
        self.n_swdge = 0

    def begin_phase_sems(self):
        self.next_slot = 1

    def sem(self, name):
        i = self.next_slot
        self.next_slot += 1
        assert self.next_slot <= POOL, "sem pool exhausted"
        self.names[name] = i
        return self.pool[i]

    def swdge_sem(self, name):
        s = self.stack.enter_context(self.nc.semaphore(name))
        self.pool.append(s)
        self.pool_cnt.append(0)
        self.names[name] = len(self.pool) - 1
        self.n_swdge += 1
        assert self.n_swdge <= 35, "too many swdge sems"
        return s

    def s(self, name):
        return self.pool[self.names[name]]

    def inc(self, name, by=1):
        i = self.names[name]
        self.pool_cnt[i] += by
        return self.pool_cnt[i]

    def val(self, name):
        return self.pool_cnt[self.names[name]]

    def phase(self, name):
        return Phase(self, name)


PROFILE_SCOPES = False


class Phase:
    def __init__(self, ctx, name):
        self.ctx = ctx
        self.name = name
        self.fns = []

    def tensor(self, f): self.fns.append(("tensor", f))
    def vector(self, f): self.fns.append(("vector", f))
    def scalar(self, f): self.fns.append(("scalar", f))
    def gpsimd(self, f): self.fns.append(("gpsimd", f))
    def sync(self, f): self.fns.append(("sync", f))

    def emit(self):
        ctx = self.ctx
        nc = ctx.nc
        with contextlib.ExitStack() as _es:
            if PROFILE_SCOPES:
                _es.enter_context(nc.named_scope(self.name))
            self._emit_block()
        ctx.bar_cnt += len(self.fns)
        if not hasattr(ctx, "phase_log"):
            ctx.phase_log = []
        ctx.phase_log.append((self.name, {n: (ctx.names[n], ctx.pool_cnt[ctx.names[n]]) for n in list(ctx.names)}))
        ctx.begin_phase_sems()

    def _emit_block(self):
        ctx = self.ctx
        nc = ctx.nc
        bar = ctx.bar
        enter = ctx.bar_cnt
        with nc.Block() as block:
            for eng_name, f in self.fns:
                def run(eng, f=f):
                    if enter:
                        eng.wait_ge(bar, enter)
                    last = f(eng)
                    if isinstance(last, tuple):
                        last = [last]
                    if isinstance(last, list):
                        for sname, v in last:
                            eng.wait_ge(ctx.s(sname), v)
                    eng.sem_inc(bar, 1)
                getattr(block, eng_name)(run)


class Rot:

    def __init__(self, ctx, tag, k, n_per=1, swdge=False, plan=None):
        self.ctx = ctx
        self.k = k
        self.n_per = n_per
        self.plan = plan
        self.nm = [tag + str(j) for j in range(k)]
        self.base = []
        for n in self.nm:
            (ctx.swdge_sem(n) if swdge else ctx.sem(n))
            self.base.append(ctx.val(n))
        self.issued = {}

    def value(self, i):
        if self.plan is not None:
            tot = sum(self.plan[j] for j in range(i % self.k, i + 1, self.k))
            return self.base[i % self.k] + 16 * tot
        return self.base[i % self.k] + 16 * self.n_per * (i // self.k + 1)

    def issue(self, i, ins):
        nm = self.nm[i % self.k]
        v = self.ctx.inc(nm, 16)
        ins.then_inc(self.ctx.s(nm), 16)
        self.issued[i] = self.issued.get(i, 0) + 1
        assert v <= self.value(i), f"Rot {nm}: over-issued use {i}"
        return v

    def wait(self, eng, i):
        eng.wait_ge(self.ctx.s(self.nm[i % self.k]), self.value(i))

    def wait_if(self, eng, i):
        self.wait(eng, i)

    def final(self):
        return [(n, self.ctx.val(n)) for n in self.nm if self.ctx.val(n) > self.base[j]
                ] if False else [(n, self.ctx.val(n)) for j, n in enumerate(self.nm)
                                 if self.ctx.val(n) > self.base[j]]


def vfence(v, consts):
    return v.tensor_copy(consts["fence"][0:1, 0:1], consts["scal"][0:1, 0:1])


def sfence(s, consts):
    return s.copy(consts["fence"][0:1, 1:2], consts["scal"][0:1, 0:1])


def wait_all(eng, ctx, rots):
    for r in rots:
        for n, v in r.final():
            eng.wait_ge(ctx.s(n), v)


def p_nat_to_t(ctx, src_nat, dst32, dst16, consts, tag):
    nc = ctx.nc
    NTT = ctx.NTT
    GP = 4
    tp = ctx.sem("tp" + tag)
    evv = ctx.sem("ev" + tag)
    evs = ctx.sem("es" + tag)
    tp0 = ctx.val("tp" + tag)
    ev0 = ctx.val("ev" + tag)
    es0 = ctx.val("es" + tag)
    src_r = src_nat.rearrange("(n p) d -> n p d", p=128)
    with contextlib.ExitStack() as es:
        xt = [es.enter_context(nc.sbuf_tensor(f"x{tag}{i}", [128, D], F32)) for i in range(2)]
        g32 = es.enter_context(nc.sbuf_tensor(f"g32{tag}", [128, NB, GP * 128], F32))
        g16 = es.enter_context(nc.sbuf_tensor(f"g16{tag}", [128, NB, GP * 128], BF16))
        pst = es.enter_context(nc.psum_tensor(f"pt{tag}", [128, D], F32))
        idf = consts["idf"]
        ld = Rot(ctx, "ld" + tag, 2)
        sg = Rot(ctx, "sg" + tag, 2, n_per=NB * ((1 if dst32 is not None else 0) + 1))
        ph = ctx.phase("n2t" + tag)

        def f_sync(sy):
            for i in range(NTT):
                if i >= 2:
                    sy.wait_ge(ctx.s("tp" + tag), tp0 + i - 1)
                ld.issue(i, sy.dma_start(out=xt[i % 2][:], in_=src_r[i]))
                if (i + 1) % GP == 0:
                    g = i // GP
                    sy.wait_ge(ctx.s("ev" + tag), ev0 + i + 1)
                    sy.wait_ge(ctx.s("es" + tag), es0 + i + 1)
                    for j in range(NB):
                        if dst32 is not None:
                            sg.issue(g, sy.dma_start(out=dst32[j, :, tsl(g, GP * 128)],
                                                     in_=g32[:, j, :]))
                        sg.issue(g, sy.dma_start(out=dst16[j, :, tsl(g, GP * 128)],
                                                 in_=g16[:, j, :]))
            return sg.final()

        def f_tensor(t):
            for i in range(NTT):
                ld.wait(t, i)
                if i >= 1:
                    t.wait_ge(ctx.s("ev" + tag), ev0 + i)
                    t.wait_ge(ctx.s("es" + tag), es0 + i)
                last = None
                for j in range(NB):
                    last = t.transpose(pst[:, tsl(j, 128)], xt[i % 2][:, tsl(j, 128)], idf[:])
                last.then_inc(ctx.s("tp" + tag), 1)
                ctx.inc("tp" + tag, 1)
            return ("tp" + tag, ctx.val("tp" + tag))

        def f_vector(v):
            for i in range(NTT):
                g, gi = i // GP, i % GP
                v.wait_ge(ctx.s("tp" + tag), tp0 + i + 1)
                if gi == 0 and g >= 1:
                    sg.wait(v, g - 1)
                v.tensor_copy(g32[:, :, tsl(gi, 128)],
                              pst.ap().rearrange("p (a t) -> p a t", a=NB)).then_inc(ctx.s("ev" + tag), 1)
                ctx.inc("ev" + tag, 1)
            return ("ev" + tag, ctx.val("ev" + tag))

        def f_scalar(s):
            for i in range(NTT):
                g, gi = i // GP, i % GP
                s.wait_ge(ctx.s("ev" + tag), ev0 + i + 1)
                if gi == 0 and g >= 1:
                    sg.wait(s, g - 1)
                s.copy(g16[:, :, tsl(gi, 128)], g32[:, :, tsl(gi, 128)]).then_inc(ctx.s("es" + tag), 1)
                ctx.inc("es" + tag, 1)
            return ("es" + tag, ctx.val("es" + tag))

        ph.sync(f_sync)
        ph.tensor(f_tensor)
        ph.vector(f_vector)
        ph.scalar(f_scalar)
        ph.emit()


def p_ln_t(ctx, src32, src16, gamma, beta, out32, out16, consts, tag,
           out16_sb=None, shift=False, eps=EPS):
    nc = ctx.nc
    NT = ctx.NT
    sq = ctx.sem("sq" + tag)
    gz = ctx.sem("gz" + tag)
    mm = ctx.sem("mm" + tag)
    vs = ctx.sem("vs" + tag)
    rs = ctx.sem("rs" + tag)
    ap = ctx.sem("ap" + tag)
    sq0 = ctx.val("sq" + tag)
    gz0 = ctx.val("gz" + tag)
    mm0 = ctx.val("mm" + tag)
    vs0 = ctx.val("vs" + tag)
    rs0 = ctx.val("rs" + tag)
    ap0 = ctx.val("ap" + tag)
    onesb = consts["onesb"]
    with contextlib.ExitStack() as es:
        s32 = [es.enter_context(nc.sbuf_tensor(f"s32{tag}{i}", [128, NB, 512], F32)) for i in range(2)]
        s16 = [es.enter_context(nc.sbuf_tensor(f"s16{tag}{i}", [128, NB, 512], BF16)) for i in range(2)]
        sqb = [es.enter_context(nc.sbuf_tensor(f"sqb{tag}{i}", [128, NB, 512], BF16)) for i in range(2)]
        stat = [es.enter_context(nc.sbuf_tensor(f"stt{tag}{i}", [128, 3, 512], F32)) for i in range(2)]
        o32 = s32
        o16 = [es.enter_context(nc.sbuf_tensor(f"o16{tag}{i}", [128, NB, 512], BF16)) for i in range(2)]
        psA = es.enter_context(nc.psum_tensor(f"pA{tag}", [128, 512], F32))
        psB = es.enter_context(nc.psum_tensor(f"pB{tag}", [128, 512], F32))
        ld = Rot(ctx, "ld" + tag, 2, n_per=2 * NB)
        so = Rot(ctx, "so" + tag, 2, n_per=NB * ((1 if out32 is not None else 0) + (1 if out16 is not None else 0)))
        ph = ctx.phase("lnt" + tag)

        def f_sync(sy):
            def emit_stores(i):
                sy.wait_ge(ctx.s("ap" + tag), ap0 + i + 1)
                for j in range(NB):
                    if out32 is not None:
                        so.issue(i, sy.dma_start(out=out32[j, :, tsl(i, 512)], in_=o32[i % 2][:, j, :]))
                    if out16 is not None:
                        so.issue(i, sy.dma_start(out=out16[j, :, tsl(i, 512)], in_=o16[i % 2][:, j, :]))

            has_out = out32 is not None or out16 is not None
            for i in range(NT):
                b = i % 2
                if i >= 2:
                    sy.wait_ge(ctx.s("ap" + tag), ap0 + i - 1)
                for j in range(NB):
                    sh = shift and j >= NB // 2
                    if sh and i == 0:
                        l1 = sy.dma_start(out=s32[b][:, j, 1:512], in_=src32[j, :, 0:511])
                        l2 = sy.dma_start(out=s16[b][:, j, 1:512], in_=src16[j, :, 0:511])
                    elif sh:
                        l1 = sy.dma_start(out=s32[b][:, j, :], in_=src32[j, :, i * 512 - 1: i * 512 + 511])
                        l2 = sy.dma_start(out=s16[b][:, j, :], in_=src16[j, :, i * 512 - 1: i * 512 + 511])
                    else:
                        l1 = sy.dma_start(out=s32[b][:, j, :], in_=src32[j, :, tsl(i, 512)])
                        l2 = sy.dma_start(out=s16[b][:, j, :], in_=src16[j, :, tsl(i, 512)])
                    ld.issue(i, l1)
                    ld.issue(i, l2)
                if has_out and i >= 1:
                    emit_stores(i - 1)
            if has_out:
                emit_stores(NT - 1)
            f = so.final()
            return f if f else ld.final()

        def f_gpsimd(g):
            if shift:
                g.memset(s32[0][:, NB // 2:, 0:1], 0.0)
                g.memset(s16[0][:, NB // 2:, 0:1], 0.0)
            g.sem_inc(ctx.s("gz" + tag), 1)
            ctx.inc("gz" + tag, 1)
            return ("gz" + tag, ctx.val("gz" + tag))

        def f_scalar(s):
            s.wait_ge(ctx.s("gz" + tag), gz0 + 1)

            def lnexp(i):
                s.wait_ge(ctx.s("vs" + tag), vs0 + i + 1)
                s.activation(stat[i % 2][:, 2, :], stat[i % 2][:, 1, :], AF.Ln, bias=consts["scal"][:, 0:1])
                s.activation(stat[i % 2][:, 2, :], stat[i % 2][:, 2, :], AF.Exp,
                             scale=-0.5).then_inc(ctx.s("rs" + tag), 1)
                ctx.inc("rs" + tag, 1)

            for i in range(NT):
                if i >= 1:
                    lnexp(i - 1)
                ld.wait(s, i)
                if i >= 2:
                    s.wait_ge(ctx.s("mm" + tag), mm0 + i - 1)
                s.activation(sqb[i % 2][:].rearrange("p a t -> p (a t)"),
                             s16[i % 2][:].rearrange("p a t -> p (a t)"),
                             AF.Square).then_inc(ctx.s("sq" + tag), 1)
                ctx.inc("sq" + tag, 1)
            lnexp(NT - 1)
            return ("rs" + tag, ctx.val("rs" + tag))

        def _dead_scalar(s):
            for i in range(NT):
                s.wait_ge(ctx.s("vs" + tag), vs0 + i + 1)
                s.activation(stat[i % 2][:, 2, :], stat[i % 2][:, 1, :], AF.Ln, bias=consts["scal"][:, 0:1])
                s.activation(stat[i % 2][:, 2, :], stat[i % 2][:, 2, :], AF.Exp,
                             scale=-0.5).then_inc(ctx.s("rs" + tag), 1)
                ctx.inc("rs" + tag, 1)
            return ("rs" + tag, ctx.val("rs" + tag))

        def f_tensor(t):
            for i in range(NT):
                ld.wait(t, i)
                if i >= 1:
                    t.wait_ge(ctx.s("vs" + tag), vs0 + i)
                for j in range(NB):
                    t.matmul(psA[:], onesb[:], s16[i % 2][:, j, :], start=(j == 0), stop=(j == NB - 1))
                if i == 0:
                    t.wait_ge(ctx.s("gz" + tag), gz0 + 1)
                t.wait_ge(ctx.s("sq" + tag), sq0 + i + 1)
                last = None
                for j in range(NB):
                    last = t.matmul(psB[:], onesb[:], sqb[i % 2][:, j, :], start=(j == 0), stop=(j == NB - 1))
                last.then_inc(ctx.s("mm" + tag), 1)
                ctx.inc("mm" + tag, 1)
            return ("mm" + tag, ctx.val("mm" + tag))

        def f_vector(v):
            inv = 1.0 / D
            for i in range(NT):
                b = i % 2
                v.wait_ge(ctx.s("mm" + tag), mm0 + i + 1)
                v.tensor_scalar(stat[b][:, 0, :], psA[:], inv, None, ALU.mult)
                v.tensor_tensor(stat[b][:, 1, :], stat[b][:, 0, :], stat[b][:, 0, :], ALU.mult)
                v.scalar_tensor_tensor(stat[b][:, 1, :], psB[:], inv, stat[b][:, 1, :],
                                       ALU.mult, ALU.subtract).then_inc(ctx.s("vs" + tag), 1)
                ctx.inc("vs" + tag, 1)
                v.wait_ge(ctx.s("rs" + tag), rs0 + i + 1)
                if i >= 2 and so.n_per > 0:
                    so.wait(v, i - 2)
                last = None
                for j in range(NB):
                    v.tensor_tensor(o32[b][:, j, :], s32[b][:, j, :], stat[b][:, 0, :], ALU.subtract)
                    if gamma is not None:
                        v.scalar_tensor_tensor(o32[b][:, j, :], o32[b][:, j, :], gamma[:, j:j + 1],
                                               stat[b][:, 2, :], ALU.mult, ALU.mult)
                        last = v.tensor_scalar(o32[b][:, j, :], o32[b][:, j, :], beta[:, j:j + 1], None, ALU.add)
                    else:
                        last = v.tensor_tensor(o32[b][:, j, :], o32[b][:, j, :], stat[b][:, 2, :], ALU.mult)
                    dst16 = out16_sb[:, j, tsl(i, 512)] if out16_sb is not None else (
                        o16[b][:, j, :] if out16 is not None else None)
                    if dst16 is not None:
                        last = v.tensor_copy(dst16, o32[b][:, j, :])
                last.then_inc(ctx.s("ap" + tag), 1)
                ctx.inc("ap" + tag, 1)
            return ("ap" + tag, ctx.val("ap" + tag))

        ph.sync(f_sync)
        ph.gpsimd(f_gpsimd)
        ph.scalar(f_scalar)
        ph.tensor(f_tensor)
        ph.vector(f_vector)
        ph.emit()


def p_attn_qkv(ctx, wsrc, bqkv_sb, xn2b, QT, KT, VAC, consts, tag="aq"):
    nc = ctx.nc
    NT = ctx.NT
    NTT = ctx.NTT
    mm = ctx.sem("mm" + tag)
    ev = ctx.sem("ev" + tag)
    vv = ctx.sem("vv" + tag)
    bb = ctx.sem("bb" + tag)
    mm0 = ctx.val("mm" + tag)
    ev0 = ctx.val("ev" + tag)
    vv0 = ctx.val("vv" + tag)
    bb0 = ctx.val("bb" + tag)
    wsrc_r = wsrc.rearrange("(a p) c -> a p c", p=128)
    NQK = 16
    with contextlib.ExitStack() as es:
        wq = es.enter_context(nc.sbuf_tensor("wq" + tag, [128, NB, 3072], BF16))
        qko = [es.enter_context(nc.sbuf_tensor(f"qko{tag}{i}", [128, 512], BF16)) for i in range(4)]
        vb_hi = es.enter_context(nc.sbuf_tensor("vbh" + tag, [1, 1024], BF16))
        vb_lo = es.enter_context(nc.sbuf_tensor("vbl" + tag, [1, 1024], BF16))
        vb_bc = es.enter_context(nc.sbuf_tensor("vbc" + tag, [128, 1024], F32))
        vaug = [es.enter_context(nc.sbuf_tensor(f"va{tag}{i}", [128, 2048], BF16)) for i in range(2)]
        psq = [es.enter_context(nc.psum_tensor(f"pq{tag}{i}", [128, 512], F32)) for i in range(4)]
        psb = es.enter_context(nc.psum_tensor(f"pb{tag}", [128, 1024], F32))
        onesb = consts["onesb"]
        wl = Rot(ctx, "wl" + tag, 1, n_per=NB, swdge=True)
        sq_ = Rot(ctx, "sq" + tag, 4)
        sv = Rot(ctx, "sv" + tag, 2)
        ph = ctx.phase("aqkv" + tag)

        def f_gpsimd(g):
            for j in range(NB):
                wl.issue(0, g.dma_start(out=wq[:, j, :], in_=wsrc_r[j]))
            last = None
            for i in range(2):
                last = g.memset(vaug[i].ap().rearrange("p (h c) -> p h c", c=128)[:, :, 64:128], 1.0)
            last.then_inc(ctx.s("bb" + tag), 1)
            ctx.inc("bb" + tag, 1)
            return ("bb" + tag, ctx.val("bb" + tag))

        def f_vector(v):
            row = consts["qkvb_row"]
            v.tensor_copy(vb_hi[:], row[0:1, 2048:3072])
            v.tensor_tensor(vb_lo[:], row[0:1, 2048:3072], vb_hi[:],
                            ALU.subtract).then_inc(ctx.s("bb" + tag), 1)
            ctx.inc("bb" + tag, 1)
            v.wait_ge(ctx.s("mm" + tag), mm0 + 1)
            v.tensor_copy(vb_bc[:], psb[:]).then_inc(ctx.s("bb" + tag), 1)
            ctx.inc("bb" + tag, 1)
            for i in range(NTT):
                b = i % 2
                if i >= 2:
                    sv.wait(v, i - 2)
                for vc in range(2):
                    v.wait_ge(ctx.s("mm" + tag), mm0 + 1 + NQK * NT + i * 2 + vc + 1)
                    last = v.tensor_tensor(
                        vaug[b].ap().rearrange("p (h c) -> p h c", c=128)[:, tsl(vc, 8), 0:64],
                        psq[2 + vc][:].rearrange("p (h c) -> p h c", c=64),
                        vb_bc[:, tsl(vc, 512)].rearrange("p (h c) -> p h c", c=64),
                        ALU.add)
                last.then_inc(ctx.s("vv" + tag), 1)
                ctx.inc("vv" + tag, 1)
            return ("vv" + tag, ctx.val("vv" + tag))

        def f_tensor(t):
            wl.wait(t, 0)
            t.wait_ge(ctx.s("bb" + tag), bb0 + 2)
            for vc in range(2):
                t.matmul(psb[:, tsl(vc, 512)], onesb[0:1, :], vb_hi[0:1, tsl(vc, 512)], start=True, stop=False)
                last = t.matmul(psb[:, tsl(vc, 512)], onesb[0:1, :], vb_lo[0:1, tsl(vc, 512)], start=False, stop=True)
            last.then_inc(ctx.s("mm" + tag), 1)
            ctx.inc("mm" + tag, 1)
            for ct in range(NQK):
                for tt in range(NT):
                    n = ct * NT + tt
                    k = n % 4
                    if n >= 4:
                        t.wait_ge(ctx.s("ev" + tag), ev0 + n - 3)
                    last = None
                    for j in range(NB):
                        last = t.matmul(psq[k][:], wq[:, j, tsl(ct, 128)], xn2b[:, j, tsl(tt, 512)],
                                        start=(j == 0), stop=(j == NB - 1))
                    last.then_inc(ctx.s("mm" + tag), 1)
                    ctx.inc("mm" + tag, 1)
            t.wait_ge(ctx.s("ev" + tag), ev0 + NQK * NT)
            for i in range(NTT):
                for vc in range(2):
                    if i >= 1:
                        t.wait_ge(ctx.s("vv" + tag), vv0 + i)
                    last = None
                    for j in range(NB):
                        last = t.matmul(psq[2 + vc][:], xn2b[:, j, tsl(i, 128)],
                                        wq[:, j, 2048 + vc * 512: 2560 + vc * 512],
                                        start=(j == 0), stop=(j == NB - 1))
                    last.then_inc(ctx.s("mm" + tag), 1)
                    ctx.inc("mm" + tag, 1)
            return ("mm" + tag, ctx.val("mm" + tag))

        def f_scalar(s):
            for ct in range(NQK):
                for tt in range(NT):
                    n = ct * NT + tt
                    k = n % 4
                    s.wait_ge(ctx.s("mm" + tag), mm0 + 1 + n + 1)
                    if n >= 4:
                        sq_.wait(s, n - 4)
                    s.activation(qko[k][:], psq[k][:], AF.Identity,
                                 bias=bqkv_sb[:, ct:ct + 1]).then_inc(ctx.s("ev" + tag), 1)
                    ctx.inc("ev" + tag, 1)
            return ("ev" + tag, ctx.val("ev" + tag))

        def f_sync(sy):
            for ct in range(NQK):
                for tt in range(NT):
                    n = ct * NT + tt
                    k = n % 4
                    sy.wait_ge(ctx.s("ev" + tag), ev0 + n + 1)
                    dst = QT if ct < 8 else KT
                    sq_.issue(n, sy.dma_start(out=dst[ct % 8, :, tsl(tt, 512)], in_=qko[k][:]))
            for i in range(NTT):
                sy.wait_ge(ctx.s("vv" + tag), vv0 + i + 1)
                sv.issue(i, sy.dma_start(out=VAC[i], in_=vaug[i % 2][:]))
            return sq_.final() + sv.final()

        ph.gpsimd(f_gpsimd)
        ph.vector(f_vector)
        ph.tensor(f_tensor)
        ph.scalar(f_scalar)
        ph.sync(f_sync)
        ph.emit()


def p_causal_attn(ctx, QT, KT, VAC, OTC, consts, tag="ca"):
    nc = ctx.nc
    NT = ctx.NT
    sA = ctx.sem("sA" + tag)
    xp = ctx.sem("xp" + tag)
    pv = ctx.sem("pv" + tag)
    dv = ctx.sem("dv" + tag)
    mk = ctx.sem("mk" + tag)
    sA0 = ctx.val("sA" + tag)
    xp0g = ctx.val("xp" + tag)
    pv0g = ctx.val("pv" + tag)
    dv0 = ctx.val("dv" + tag)
    mk0g = ctx.val("mk" + tag)
    cmask_d = consts["cmask_d"]
    with contextlib.ExitStack() as es:
        cmask = es.enter_context(nc.sbuf_tensor("cmsk" + tag, [128, 4, 512], F32))
        qtile = [es.enter_context(nc.sbuf_tensor(f"qt{tag}{i}", [128, 512], BF16)) for i in range(2)]
        ktile = [es.enter_context(nc.sbuf_tensor(f"kt{tag}{i}", [128, ctx.S], BF16)) for i in range(2)]
        vtile = [es.enter_context(nc.sbuf_tensor(f"vt{tag}{i}", [128, (ctx.S // 128) * 256], BF16)) for i in range(2)]
        pT = [es.enter_context(nc.sbuf_tensor(f"pT{tag}{i}", [128, 512], BF16)) for i in range(4)]
        rl = es.enter_context(nc.sbuf_tensor(f"rl{tag}", [64, 512], F32))
        otile = [es.enter_context(nc.sbuf_tensor(f"ot{tag}{i}", [128, 512], BF16)) for i in range(2)]
        psS = [es.enter_context(nc.psum_tensor(f"pS{tag}{i}", [128, 512], F32)) for i in range(4)]
        psO = [es.enter_context(nc.psum_tensor(f"pO{tag}{i}", [128, 512], F32)) for i in range(2)]
        nks_plan = [4 * qt + 4 for qt in range(NT) for hp in range(NB)]
        nks_plan = [4 * qt + 4 for qt in range(NT) for _ in range(NB)]
        ldq = Rot(ctx, "lq" + tag, 2)
        ldk = Rot(ctx, "lk" + tag, 2)
        ldv = Rot(ctx, "lv" + tag, 2, plan=nks_plan)
        lmk = Rot(ctx, "lm" + tag, 1)
        so = Rot(ctx, "so" + tag, 2)
        ph = ctx.phase("causal" + tag)
        HP = NB
        units = [(hp, qt) for qt in range(NT) for hp in range(HP)]
        sched = []
        sA_c, xp_c, pv_c, mk_c = sA0, xp0g, pv0g, mk0g
        for u, (hp, qt) in enumerate(units):
            nk = 4 * qt + 4
            sched.append({"hp": hp, "qt": qt, "nk": nk, "sA0": sA_c, "xp0": xp_c,
                          "pv0": pv_c, "mk0": mk_c})
            sA_c += nk
            xp_c += 2 * nk
            pv_c += nk
            mk_c += 4

        def f_sync(sy):
            lmk.issue(0, sy.dma_start(out=cmask[:], in_=cmask_d))
            for u, s_ in enumerate(sched):
                hp, qt, nk = s_["hp"], s_["qt"], s_["nk"]
                b = u % 2
                if u >= 2:
                    sy.wait_ge(ctx.s("dv" + tag), dv0 + u - 1)
                ldq.issue(u, sy.dma_start(out=qtile[b][:], in_=QT[hp, :, tsl(qt, 512)]))
                ldk.issue(u, sy.dma_start(out=ktile[b][:, 0:nk * 128], in_=KT[hp, :, 0:nk * 128]))
                for kt in range(nk):
                    ldv.issue(u, sy.dma_start(out=vtile[b][:, tsl(kt, 256)],
                                              in_=VAC[kt, :, hp * 256:(hp + 1) * 256]))
                sy.wait_ge(ctx.s("dv" + tag), dv0 + u + 1)
                so.issue(u, sy.dma_start(out=OTC[hp, :, tsl(qt, 512)], in_=otile[b][:]))
            return so.final()

        def f_tensor(t):
            for u, s_ in enumerate(sched):
                hp, qt, nk = s_["hp"], s_["qt"], s_["nk"]
                b = u % 2
                ldq.wait(t, u)
                ldk.wait(t, u)
                ldv.wait(t, u)
                if u >= 1:
                    t.wait_ge(ctx.s("dv" + tag), dv0 + u)
                    t.wait_ge(ctx.s("xp" + tag), s_["xp0"])

                def S_(kt):
                    kb_ = kt % 2
                    if kt >= 2:
                        t.wait_ge(ctx.s("xp" + tag), s_["xp0"] + 2 * (kt - 1))
                    t.matmul(psS[kb_][:], ktile[b][0:64, tsl(kt, 128)], qtile[b][0:64, :],
                             start=True, stop=True, tile_position=(0, 0))
                    t.matmul(psS[2 + kb_][:], ktile[b][64:128, tsl(kt, 128)], qtile[b][64:128, :],
                             start=True, stop=True, tile_position=(64, 0)).then_inc(ctx.s("sA" + tag), 1)
                    ctx.inc("sA" + tag, 1)

                def PV_(kt):
                    kb_ = kt % 2
                    t.wait_ge(ctx.s("xp" + tag), s_["xp0"] + 2 * (kt + 1))
                    t.matmul(psO[0][:], vtile[b][:, tsl(kt, 256)][:, 0:128], pT[2 * kb_][:],
                             start=(kt == 0), stop=(kt == nk - 1))
                    t.matmul(psO[1][:], vtile[b][:, tsl(kt, 256)][:, 128:256], pT[2 * kb_ + 1][:],
                             start=(kt == 0), stop=(kt == nk - 1)).then_inc(ctx.s("pv" + tag), 1)
                    ctx.inc("pv" + tag, 1)

                S_(0)
                for kt in range(1, nk):
                    S_(kt)
                    PV_(kt - 1)
                PV_(nk - 1)
            return ("pv" + tag, ctx.val("pv" + tag))

        def f_vector(v):
            lmk.wait(v, 0)
            for u, s_ in enumerate(sched):
                hp, qt, nk = s_["hp"], s_["qt"], s_["nk"]
                for kt in range(nk - 4, nk):
                    r = kt - (nk - 4)
                    kb_ = kt % 2
                    v.wait_ge(ctx.s("sA" + tag), s_["sA0"] + kt + 1)
                    v.tensor_tensor(psS[kb_][:], psS[kb_][:], cmask[:, r, :], ALU.add)
                    v.tensor_tensor(psS[2 + kb_][:], psS[2 + kb_][:], cmask[:, r, :],
                                    ALU.add).then_inc(ctx.s("mk" + tag), 1)
                    ctx.inc("mk" + tag, 1)
                if u >= 2:
                    so.wait(v, u - 2)
                v.wait_ge(ctx.s("pv" + tag), s_["pv0"] + nk)
                v.reciprocal(rl[:], psO[0][64:128, :])
                v.tensor_tensor(otile[u % 2][0:64, :], psO[0][0:64, :], rl[:], ALU.mult)
                v.reciprocal(rl[:], psO[1][64:128, :])
                v.tensor_tensor(otile[u % 2][64:128, :], psO[1][0:64, :], rl[:],
                                ALU.mult).then_inc(ctx.s("dv" + tag), 1)
                ctx.inc("dv" + tag, 1)
            return ("dv" + tag, ctx.val("dv" + tag))

        def f_scalar(s):
            for u, s_ in enumerate(sched):
                hp, qt, nk = s_["hp"], s_["qt"], s_["nk"]
                if u >= 1:
                    s.wait_ge(ctx.s("pv" + tag), s_["pv0"])
                for kt in range(nk):
                    kb_ = kt % 2
                    if kt >= nk - 4:
                        r = kt - (nk - 4)
                        s.wait_ge(ctx.s("mk" + tag), s_["mk0"] + r + 1)
                    else:
                        s.wait_ge(ctx.s("sA" + tag), s_["sA0"] + kt + 1)
                    if kt >= 2:
                        s.wait_ge(ctx.s("pv" + tag), s_["pv0"] + kt - 1)
                    s.activation(pT[2 * kb_][:], psS[kb_][:], AF.Exp, scale=0.125)
                    s.activation(pT[2 * kb_ + 1][:], psS[2 + kb_][:], AF.Exp,
                                 scale=0.125).then_inc(ctx.s("xp" + tag), 2)
                    ctx.inc("xp" + tag, 2)
            return ("xp" + tag, ctx.val("xp" + tag))

        ph.sync(f_sync)
        ph.tensor(f_tensor)
        ph.vector(f_vector)
        ph.scalar(f_scalar)
        ph.emit()


def p_proj_t(ctx, wsrc, rhs16, bias_sb, extra16, res32, out32, out16, consts,
             tag, nbin=NB, nbout=NB, act=None):
    nc = ctx.nc
    NT = ctx.NT
    mm = ctx.sem("mm" + tag)
    ev = ctx.sem("ev" + tag)
    vv = ctx.sem("vv" + tag)
    mm0 = ctx.val("mm" + tag)
    ev0 = ctx.val("ev" + tag)
    vv0 = ctx.val("vv" + tag)
    wsrc_r = wsrc.rearrange("(a p) c -> a p c", p=128)
    Dout = nbout * 128
    stream_rhs = nbin > NB
    with contextlib.ExitStack() as es:
        wsb = es.enter_context(nc.sbuf_tensor("w" + tag, [128, nbin, Dout], BF16))
        if stream_rhs:
            rsb = [es.enter_context(nc.sbuf_tensor(f"r{tag}{i}", [128, nbin, 512], BF16)) for i in range(2)]
        else:
            rsb = es.enter_context(nc.sbuf_tensor("r" + tag, [128, nbin, ctx.S], BF16))
        a32 = [es.enter_context(nc.sbuf_tensor(f"a{tag}{i}", [128, 512], F32)) for i in range(2)]
        e16 = [es.enter_context(nc.sbuf_tensor(f"e{tag}{i}", [128, 512], BF16)) for i in range(2)]
        o32 = [es.enter_context(nc.sbuf_tensor(f"o{tag}{i}", [128, 512], F32)) for i in range(2)]
        o16 = [es.enter_context(nc.sbuf_tensor(f"f{tag}{i}", [128, 512], BF16)) for i in range(2)]
        r32 = [es.enter_context(nc.sbuf_tensor(f"g{tag}{i}", [128, 512], F32)) for i in range(2)]
        ps = [es.enter_context(nc.psum_tensor(f"p{tag}{i}", [128, 512], F32)) for i in range(2)]
        wl = Rot(ctx, "wl" + tag, 1, n_per=nbin, swdge=True)
        rl_ = (Rot(ctx, "rl" + tag, 2, n_per=nbin) if stream_rhs
               else Rot(ctx, "rl" + tag, nbin))
        rr = Rot(ctx, "rr" + tag, 2)
        se = Rot(ctx, "se" + tag, 2)
        sv_ = Rot(ctx, "sv" + tag, 2,
                  n_per=(1 if out32 is not None else 0) + (1 if out16 is not None else 0))
        ph = ctx.phase("proj" + tag)
        if stream_rhs:
            units = [(tt, co) for tt in range(NT) for co in range(nbout)]
        else:
            units = [(tt, co) for co in range(nbout) for tt in range(NT)]
        NU = len(units)

        def f_gpsimd(g):
            for j in range(nbin):
                wl.issue(0, g.dma_start(out=(wsb[:, j, :]), in_=wsrc_r[j]))
            return wl.final()[-1]

        def f_sync(sy):
            if not stream_rhs:
                for j in range(nbin):
                    rl_.issue(j, sy.dma_start(out=rsb[:, j, :], in_=rhs16[j]))
            for u, (tt, co) in enumerate(units):
                b = u % 2
                if stream_rhs and co == 0:
                    ti = tt
                    if ti >= 2:
                        sy.wait_ge(ctx.s("mm" + tag), mm0 + (ti - 1) * nbout)
                    for j in range(nbin):
                        rl_.issue(ti, sy.dma_start(out=rsb[ti % 2][:, j, :],
                                                   in_=rhs16[j, :, tsl(ti, 512)]))
                if res32 is not None:
                    if u >= 2:
                        sy.wait_ge(ctx.s("vv" + tag), vv0 + u - 1)
                    rr.issue(u, sy.dma_start(out=r32[b][:], in_=res32[co, :, tsl(tt, 512)]))
                if extra16 is not None:
                    sy.wait_ge(ctx.s("ev" + tag), ev0 + 2 * (u + 1))
                    se.issue(u, sy.dma_start(out=extra16[co, :, tsl(tt, 512)], in_=e16[b][:]))
                if out32 is not None or out16 is not None:
                    sy.wait_ge(ctx.s("vv" + tag), vv0 + u + 1)
                    if out32 is not None:
                        sv_.issue(u, sy.dma_start(out=out32[co, :, tsl(tt, 512)], in_=o32[b][:]))
                    if out16 is not None:
                        sv_.issue(u, sy.dma_start(out=out16[co, :, tsl(tt, 512)], in_=o16[b][:]))
            return sv_.final() + se.final()

        def f_tensor(t):
            wl.wait(t, 0)
            if not stream_rhs:
                for j in range(nbin):
                    rl_.wait(t, j)
            for u, (tt, co) in enumerate(units):
                b = u % 2
                if stream_rhs:
                    rl_.wait(t, tt)
                if u >= 2:
                    t.wait_ge(ctx.s("ev" + tag), ev0 + 2 * (u - 1))
                last = None
                for j in range(nbin):
                    rt = rsb[tt % 2][:, j, :] if stream_rhs else rsb[:, j, tsl(tt, 512)]
                    last = t.matmul(ps[b][:], wsb[:, j, tsl(co, 128)], rt,
                                    start=(j == 0), stop=(j == nbin - 1))
                last.then_inc(ctx.s("mm" + tag), 1)
                ctx.inc("mm" + tag, 1)
            return ("mm" + tag, ctx.val("mm" + tag))

        def f_scalar(s):
            for u, (tt, co) in enumerate(units):
                b = u % 2
                s.wait_ge(ctx.s("mm" + tag), mm0 + u + 1)
                if u >= 2:
                    s.wait_ge(ctx.s("vv" + tag), vv0 + u - 1)
                    if extra16 is not None:
                        se.wait(s, u - 2)
                bias = bias_sb[:, co:co + 1] if bias_sb is not None else consts["scal"][:, 3:4]
                fn = act if act is not None else AF.Identity
                last = s.activation(a32[b][:], ps[b][:], fn, bias=bias)
                if extra16 is not None:
                    last = s.copy(e16[b][:], a32[b][:])
                last.then_inc(ctx.s("ev" + tag), 2)
                ctx.inc("ev" + tag, 2)
            return ("ev" + tag, ctx.val("ev" + tag))

        def f_vector(v):
            for u, (tt, co) in enumerate(units):
                b = u % 2
                v.wait_ge(ctx.s("ev" + tag), ev0 + 2 * u + 2)
                if u >= 2:
                    sv_.wait(v, u - 2)
                if res32 is not None:
                    rr.wait(v, u)
                    last = v.tensor_tensor(o32[b][:], a32[b][:], r32[b][:], ALU.add)
                else:
                    last = v.tensor_copy(o32[b][:], a32[b][:])
                if out16 is not None:
                    last = v.tensor_copy(o16[b][:], o32[b][:])
                last.then_inc(ctx.s("vv" + tag), 1)
                ctx.inc("vv" + tag, 1)
            return ("vv" + tag, ctx.val("vv" + tag))

        ph.gpsimd(f_gpsimd)
        ph.sync(f_sync)
        ph.tensor(f_tensor)
        ph.scalar(f_scalar)
        ph.vector(f_vector)
        ph.emit()


def p_scan(ctx, wctx_d, wx_d, binl_sb, CTXB, XS0_32, XS0_16, Y1_32,
           Y2_32, Y2_16, consts, tag="sc"):
    nc = ctx.nc
    NT = ctx.NT
    NIT = 8
    for n in ["cm", "um", "va", "th", "gp", "vu", "xc", "fi"]:
        ctx.sem(n + tag)
    b0 = {n: ctx.val(n + tag) for n in ["cm", "um", "va", "th", "gp", "vu", "xc", "fi"]}
    PER = {"um": 2 * NIT, "va": 2 * NIT, "th": 2 * NIT, "gp": 1,
           "vu": NIT, "xc": NIT + 1, "cm": 1, "fi": 1}

    def base(n, i):
        return b0[n] + PER[n] * i

    wctx_r = wctx_d.rearrange("(a p) c -> a p c", p=128)
    wx_r = wx_d.rearrange("(a p) c -> a p c", p=128)
    flat = lambda ap: ap.rearrange("p a t -> p (a t)")
    with contextlib.ExitStack() as es:
        wctx = es.enter_context(nc.sbuf_tensor("wc" + tag, [128, NB, D], BF16))
        wx = es.enter_context(nc.sbuf_tensor("wx" + tag, [128, NB, D], BF16))
        ctxb = [es.enter_context(nc.sbuf_tensor(f"cb{tag}{i}", [128, NB, 512], BF16)) for i in range(2)]
        c0 = es.enter_context(nc.sbuf_tensor("c0" + tag, [128, NB, 512], F32))
        xs = es.enter_context(nc.sbuf_tensor("xs" + tag, [128, NB, 512], F32))
        vs = es.enter_context(nc.sbuf_tensor("vs" + tag, [128, NB, 512], F32))
        tmp = es.enter_context(nc.sbuf_tensor("tp" + tag, [128, NB, 512], F32))
        u32 = es.enter_context(nc.sbuf_tensor("u3" + tag, [128, NB, 512], F32))
        xsb = es.enter_context(nc.sbuf_tensor("xb" + tag, [128, NB, 512], BF16))
        up = es.enter_context(nc.psum_tensor("up" + tag, [128, NB, 512], F32))
        wload = Rot(ctx, "wv" + tag, 1, n_per=2 * NB, swdge=True)
        lc = Rot(ctx, "lc" + tag, 2, n_per=NB)
        lx = Rot(ctx, "lx" + tag, 1, n_per=2 * NB)
        ly = Rot(ctx, "ly" + tag, 1, n_per=NB)
        st = Rot(ctx, "st" + tag, 1, n_per=2 * NB)
        ph = ctx.phase("scan" + tag)
        H4 = lambda h: slice(h * 4, h * 4 + 4)

        def f_gpsimd(g):
            for j in range(NB):
                wload.issue(0, g.dma_start(out=wctx[:, j, :], in_=wctx_r[j]))
            for j in range(NB):
                wload.issue(0, g.dma_start(out=wx[:, j, :], in_=wx_r[j]))
            for i in range(NT):
                lx.wait(g, i)
                g.memset(flat(vs[:]), 0.0).then_inc(ctx.s("gp" + tag), 1)
                ctx.inc("gp" + tag, 1)
            return ("gp" + tag, ctx.val("gp" + tag))

        def f_sync(sy):
            for i in range(NT):
                if i >= 1:
                    st.wait(sy, i - 1)
                if i >= 2:
                    sy.wait_ge(ctx.s("cm" + tag), base("cm", i - 1) + 1)
                for j in range(NB):
                    lc.issue(i, sy.dma_start(out=ctxb[i % 2][:, j, :], in_=CTXB[j, :, tsl(i, 512)]))
                for j in range(NB):
                    lx.issue(i, sy.dma_start(out=xs[:, j, :], in_=XS0_32[j, :, tsl(i, 512)]))
                    lx.issue(i, sy.dma_start(out=xsb[:, j, :], in_=XS0_16[j, :, tsl(i, 512)]))
                sy.wait_ge(ctx.s("vu" + tag), base("vu", i) + NIT)
                for j in range(NB):
                    ly.issue(i, sy.dma_start(out=tmp[:, j, :], in_=Y1_32[j, :, tsl(i, 512)]))
                sy.wait_ge(ctx.s("fi" + tag), base("fi", i) + 1)
                sy.wait_ge(ctx.s("xc" + tag), base("xc", i) + NIT + 1)
                for j in range(NB):
                    st.issue(i, sy.dma_start(out=Y2_32[j, :, tsl(i, 512)], in_=u32[:, j, :]))
                    st.issue(i, sy.dma_start(out=Y2_16[j, :, tsl(i, 512)], in_=xsb[:, j, :]))
            return st.final()

        def f_tensor(t):
            wload.wait(t, 0)
            for i in range(NT):
                lx.wait(t, i)
                lc.wait(t, i)
                if i >= 1:
                    t.wait_ge(ctx.s("va" + tag), base("va", i))
                    st.wait(t, i - 1)
                for od in range(NB):
                    for j in range(NB):
                        last = t.matmul(up[:, od, :], wctx[:, j, tsl(od, 128)], ctxb[i % 2][:, j, :],
                                        start=(j == 0), stop=(j == NB - 1))
                last.then_inc(ctx.s("cm" + tag), 1)
                ctx.inc("cm" + tag, 1)
                for it in range(NIT):
                    t.wait_ge(ctx.s("xc" + tag), base("xc", i) + it + 1)
                    for half in range(2):
                        if it >= 1:
                            t.wait_ge(ctx.s("va" + tag), base("va", i) + 2 * (it - 1) + half + 1)
                        last = None
                        for od in range(4 * half, 4 * half + 4):
                            for j in range(NB):
                                last = t.matmul(up[:, od, :], wx[:, j, tsl(od, 128)], xsb[:, j, :],
                                                start=(j == 0), stop=(j == NB - 1))
                        last.then_inc(ctx.s("um" + tag), 1)
                        ctx.inc("um" + tag, 1)
            return ("um" + tag, ctx.val("um" + tag))

        def f_scalar(s):
            for i in range(NT):
                s.wait_ge(ctx.s("cm" + tag), base("cm", i) + 1)
                s.wait_ge(ctx.s("va" + tag), base("va", i))
                last = None
                for od in range(NB):
                    last = s.activation(c0[:, od, :], up[:, od, :], AF.Identity, bias=binl_sb[:, od:od + 1])
                last.then_inc(ctx.s("xc" + tag), 1)
                ctx.inc("xc" + tag, 1)
                for it in range(NIT):
                    for half in range(2):
                        s.wait_ge(ctx.s("va" + tag), base("va", i) + 2 * it + half + 1)
                        s.activation(flat(u32[:, H4(half), :]), flat(u32[:, H4(half), :]),
                                     AF.Tanh).then_inc(ctx.s("th" + tag), 1)
                        ctx.inc("th" + tag, 1)
                    if it < NIT - 1:
                        s.wait_ge(ctx.s("vu" + tag), base("vu", i) + it + 1)
                        s.copy(flat(xsb[:]), flat(xs[:])).then_inc(ctx.s("xc" + tag), 1)
                        ctx.inc("xc" + tag, 1)
                s.wait_ge(ctx.s("fi" + tag), base("fi", i) + 1)
                s.copy(flat(xsb[:]), flat(u32[:])).then_inc(ctx.s("xc" + tag), 1)
                ctx.inc("xc" + tag, 1)
            return ("xc" + tag, ctx.val("xc" + tag))

        def f_vector(v):
            for i in range(NT):
                for it in range(NIT):
                    for half in range(2):
                        v.wait_ge(ctx.s("um" + tag), base("um", i) + 2 * it + half + 1)
                        v.tensor_tensor(flat(u32[:, H4(half), :]), flat(up[:, H4(half), :]),
                                        flat(c0[:, H4(half), :]), ALU.add).then_inc(ctx.s("va" + tag), 1)
                        ctx.inc("va" + tag, 1)
                    if it == 0:
                        v.wait_ge(ctx.s("gp" + tag), base("gp", i) + 1)
                    v.wait_ge(ctx.s("th" + tag), base("th", i) + 2 * (it + 1))
                    v.tensor_tensor(flat(tmp[:]), flat(u32[:]), flat(xs[:]), ALU.subtract)
                    v.scalar_tensor_tensor(flat(vs[:]), flat(tmp[:]), 0.1 / 0.9 ** (it + 1),
                                           flat(vs[:]), ALU.mult, ALU.add)
                    v.scalar_tensor_tensor(flat(xs[:]), flat(vs[:]), 0.1 * 0.9 ** (it + 1),
                                           flat(xs[:]), ALU.mult, ALU.add).then_inc(ctx.s("vu" + tag), 1)
                    ctx.inc("vu" + tag, 1)
                ly.wait(v, i)
                v.tensor_tensor(flat(u32[:]), flat(xs[:]), flat(tmp[:]), ALU.add).then_inc(ctx.s("fi" + tag), 1)
                ctx.inc("fi" + tag, 1)
            return ("fi" + tag, ctx.val("fi" + tag))

        ph.gpsimd(f_gpsimd)
        ph.sync(f_sync)
        ph.tensor(f_tensor)
        ph.scalar(f_scalar)
        ph.vector(f_vector)
        ph.emit()


def p_ffn_qkv(ctx, wsrc, qbc_d, RQC, RQS, RKC, RKS,
              Z16, QTl, KTl, VAUGl, consts, tag):
    nc = ctx.nc
    NTT = ctx.NTT
    for n in ["mm", "s1", "va", "s2", "qe", "tp", "pe", "tb", "ve", "hi"]:
        ctx.sem(n + tag)
    b0 = {n: ctx.val(n + tag) for n in ["mm", "s1", "va", "s2", "qe", "tp", "pe", "tb", "ve", "hi"]}
    onesb = consts["onesb"]
    wsrc_r = wsrc.rearrange("(a p) c -> a p c", p=128)
    with contextlib.ExitStack() as es:
        wq = es.enter_context(nc.sbuf_tensor("wq" + tag, [128, NB, 3072], BF16))
        CSQ = es.enter_context(nc.sbuf_tensor("csq" + tag, [128, NTT, 64], F32))
        SSQ = es.enter_context(nc.sbuf_tensor("ssx" + tag, [128, NTT, 64], F32))
        CSK = es.enter_context(nc.sbuf_tensor("csk" + tag, [128, NTT, 64], F32))
        SSK = es.enter_context(nc.sbuf_tensor("ssk" + tag, [128, NTT, 64], F32))
        bt = [es.enter_context(nc.sbuf_tensor(f"bt{tag}{i}", [128, 64], F32)) for i in range(8)]
        qbc = es.enter_context(nc.sbuf_tensor("qb" + tag, [128, 4, 64], F32))
        zb = [es.enter_context(nc.sbuf_tensor(f"zb{tag}{i}", [128, NB, 130], BF16)) for i in range(2)]
        sqst = [es.enter_context(nc.sbuf_tensor(f"sg{tag}{i}", [128, 512], BF16)) for i in range(2)]
        ssA = [es.enter_context(nc.sbuf_tensor(f"sa{tag}{i}", [128, 8], F32)) for i in range(2)]
        ssB = [es.enter_context(nc.sbuf_tensor(f"sb{tag}{i}", [128, 8], F32)) for i in range(2)]
        rs_t = es.enter_context(nc.sbuf_tensor("rs" + tag, [128, 8], F32))
        st2 = es.enter_context(nc.sbuf_tensor("t2" + tag, [128, 512], F32))
        st3 = es.enter_context(nc.sbuf_tensor("t3" + tag, [128, 512], F32))
        q16 = [es.enter_context(nc.sbuf_tensor(f"q6{tag}{i}", [128, 512], BF16)) for i in range(2)]
        qts = [es.enter_context(nc.sbuf_tensor(f"qt{tag}{i}", [128, 512], BF16)) for i in range(2)]
        vaug = [es.enter_context(nc.sbuf_tensor(f"va{tag}{i}", [128, 2048], BF16)) for i in range(2)]
        pq = [es.enter_context(nc.psum_tensor(f"pq{tag}{i}", [128, 512], F32)) for i in range(3)]
        psT = es.enter_context(nc.psum_tensor(f"pT{tag}", [128, 512], BF16))
        wl = Rot(ctx, "wl" + tag, 1, n_per=NB, swdge=True)
        lq = Rot(ctx, "lr" + tag, 1, n_per=1)
        lt = Rot(ctx, "lt" + tag, 2, n_per=4)
        lz = Rot(ctx, "lz" + tag, 2, n_per=NB)
        sv = Rot(ctx, "sw" + tag, 2)
        sq_ = Rot(ctx, "sy" + tag, 2, n_per=4)
        ph = ctx.phase("fqkv" + tag)
        idb = consts["idb"]
        hview = lambda ap: ap.rearrange("p (a b) -> p a b", b=64)

        def f_gpsimd(g):
            for j in range(NB):
                wl.issue(0, g.dma_start(out=wq[:, j, :], in_=wsrc_r[j]))
            for i in range(2):
                g.memset(vaug[i].ap().rearrange("p (h c) -> p h c", c=128)[:, :, 64:128], 1.0)
            g.memset(zb[0][:, :, 0:1], 0.0)
            g.memset(zb[1][:, :, 0:1], 0.0)
            g.sem_inc(ctx.s("hi" + tag), 1)
            ctx.inc("hi" + tag, 1)
            return ("hi" + tag, ctx.val("hi" + tag))

        def f_sync(sy):
            lq.issue(0, sy.dma_start(out=qbc[:], in_=qbc_d))
            for i in range(NTT):
                if i >= 2:
                    sy.wait_ge(ctx.s("tb" + tag), b0["tb"] + i - 1)
                for bi, R in enumerate([RQC, RQS, RKC, RKS]):
                    lt.issue(i, sy.dma_start(out=bt[4 * (i % 2) + bi][:], in_=R[i]))
            for i in range(NTT):
                b = i % 2
                if i >= 2:
                    sy.wait_ge(ctx.s("mm" + tag), b0["mm"] + 1 + 6 * (i - 1))
                for j in range(NB):
                    if i == 0:
                        lz.issue(i, sy.dma_start(out=zb[b][:, j, 1:129], in_=Z16[j, :, 0:128]))
                    else:
                        lz.issue(i, sy.dma_start(out=zb[b][:, j, 0:129], in_=Z16[j, :, i * 128 - 1: i * 128 + 128]))
                for cg in range(4):
                    n = i * 4 + cg
                    sy.wait_ge(ctx.s("pe" + tag), b0["pe"] + n + 1)
                    dst = QTl if cg < 2 else KTl
                    for c in range(4):
                        jj = 4 * (cg % 2) + c
                        sq_.issue(n, sy.dma_start(out=dst[jj, :, tsl(i, 128)], in_=qts[n % 2][:, tsl(c, 128)]))
                sy.wait_ge(ctx.s("ve" + tag), b0["ve"] + 2 * (i + 1))
                sv.issue(i, sy.dma_start(out=VAUGl[i], in_=vaug[b][:]))
            return sq_.final() + sv.final()

        def f_tensor(t):
            t.wait_ge(ctx.s("hi" + tag), b0["hi"] + 1)
            wl.wait(t, 0)
            t.wait_ge(ctx.s("tb" + tag), b0["tb"] + NTT)
            for i in range(NTT):
                b = i % 2
                lz.wait(t, i)

                def mm_cg(cg):
                    k = (i * 6 + cg) % 3
                    n_prev = i * 6 + cg - 3
                    if n_prev >= 0:
                        pi, pcg = divmod(n_prev, 6)
                        if pcg >= 4:
                            t.wait_ge(ctx.s("ve" + tag), b0["ve"] + 2 * pi + (pcg - 4) + 1)
                        else:
                            t.wait_ge(ctx.s("qe" + tag), b0["qe"] + pi * 4 + pcg + 1)
                    last = None
                    for j in range(NB):
                        o = 1 if j < 4 else 0
                        last = t.matmul(pq[k][:], zb[b][:, j, o:o + 128], wq[:, j, tsl(cg, 512)],
                                        start=(j == 0), stop=(j == NB - 1))
                    last.then_inc(ctx.s("mm" + tag), 1)
                    ctx.inc("mm" + tag, 1)

                def tr_cg(cg):
                    n = i * 4 + cg
                    t.wait_ge(ctx.s("qe" + tag), b0["qe"] + n + 1)
                    if n >= 1:
                        t.wait_ge(ctx.s("pe" + tag), b0["pe"] + n)
                    last = None
                    for c in range(4):
                        last = t.transpose(psT[:, tsl(c, 128)], q16[n % 2][:, tsl(c, 128)], idb[:])
                    last.then_inc(ctx.s("tp" + tag), 1)
                    ctx.inc("tp" + tag, 1)

                mm_cg(0); mm_cg(1); mm_cg(2)
                tr_cg(0); mm_cg(3)
                tr_cg(1); mm_cg(4)
                tr_cg(2); mm_cg(5)
                tr_cg(3)
            return ("tp" + tag, ctx.val("tp" + tag))

        def f_scalar(s):
            for i in range(NTT):
                for cg in range(4):
                    k = (i * 6 + cg) % 3
                    n = i * 4 + cg
                    s.wait_ge(ctx.s("mm" + tag), b0["mm"] + i * 6 + cg + 1)
                    if n >= 2:
                        s.wait_ge(ctx.s("va" + tag), b0["va"] + n - 1)
                    s.activation(sqst[n % 2][:], pq[k][:], AF.Square).then_inc(ctx.s("s1" + tag), 1)
                    ctx.inc("s1" + tag, 1)
                    s.wait_ge(ctx.s("va" + tag), b0["va"] + n + 1)
                    if n >= 2:
                        s.wait_ge(ctx.s("qe" + tag), b0["qe"] + n - 1)
                    s.activation(ssB[n % 2][:], ssA[n % 2][:], AF.Ln, bias=consts["scal"][:, 4:5])
                    s.activation(ssB[n % 2][:], ssB[n % 2][:], AF.Exp, scale=-0.5).then_inc(ctx.s("s2" + tag), 1)
                    ctx.inc("s2" + tag, 1)
            return ("s2" + tag, ctx.val("s2" + tag))

        def f_vector(v):
            lq.wait(v, 0)
            for i in range(NTT):
                lt.wait(v, i)
                o = 4 * (i % 2)
                v.scalar_tensor_tensor(CSQ[:, i, :], bt[o + 0][:], 8.0, qbc[:, 0, :], ALU.mult, ALU.mult)
                v.scalar_tensor_tensor(SSQ[:, i, :], bt[o + 1][:], 8.0, qbc[:, 1, :], ALU.mult, ALU.mult)
                v.scalar_tensor_tensor(CSK[:, i, :], bt[o + 2][:], 1.0, qbc[:, 2, :], ALU.mult, ALU.mult)
                v.scalar_tensor_tensor(SSK[:, i, :], bt[o + 3][:], 1.0, qbc[:, 3, :], ALU.mult,
                                       ALU.mult).then_inc(ctx.s("tb" + tag), 1)
                ctx.inc("tb" + tag, 1)
            for i in range(NTT):
                b = i % 2
                for cg in range(4):
                    k = (i * 6 + cg) % 3
                    n = i * 4 + cg
                    CS = CSQ if cg < 2 else CSK
                    SS = SSQ if cg < 2 else SSK
                    v.wait_ge(ctx.s("s1" + tag), b0["s1"] + n + 1)
                    if n >= 2:
                        v.wait_ge(ctx.s("s2" + tag), b0["s2"] + n - 1)
                    v.reduce_sum(ssA[n % 2][:], hview(sqst[n % 2].ap()), AX.X).then_inc(ctx.s("va" + tag), 1)
                    ctx.inc("va" + tag, 1)
                    v.wait_ge(ctx.s("s2" + tag), b0["s2"] + n + 1)
                    v.tensor_scalar(rs_t[:], ssB[n % 2][:], 1.0e12, None, ALU.min)
                    cst = CS[:, i, :].rearrange("p (o b) -> p o b", o=1).broadcast_to((128, 8, 64))
                    sst = SS[:, i, :].rearrange("p (o b) -> p o b", o=1)
                    pqv = hview(pq[k].ap())
                    v.tensor_tensor(hview(st2.ap()), pqv, cst, ALU.mult)
                    v.tensor_tensor(hview(st3.ap())[:, :, 0:32], pqv[:, :, 32:64],
                                    sst[:, :, 0:32].broadcast_to((128, 8, 32)), ALU.mult)
                    v.tensor_tensor(hview(st3.ap())[:, :, 32:64], pqv[:, :, 0:32],
                                    sst[:, :, 32:64].broadcast_to((128, 8, 32)), ALU.mult)
                    v.tensor_tensor(st2[:], st2[:], st3[:], ALU.add)
                    if n >= 2:
                        v.wait_ge(ctx.s("tp" + tag), b0["tp"] + n - 1)
                    v.tensor_tensor(hview(q16[n % 2].ap()), hview(st2.ap()),
                                    rs_t.ap().rearrange("p (a o) -> p a o", o=1).broadcast_to((128, 8, 64)),
                                    ALU.mult).then_inc(ctx.s("qe" + tag), 1)
                    ctx.inc("qe" + tag, 1)
                    v.wait_ge(ctx.s("tp" + tag), b0["tp"] + n + 1)
                    if n >= 2:
                        sq_.wait(v, n - 2)
                    v.tensor_copy(qts[n % 2][:], psT[:]).then_inc(ctx.s("pe" + tag), 1)
                    ctx.inc("pe" + tag, 1)
                for vc in range(2):
                    k = (i * 6 + 4 + vc) % 3
                    v.wait_ge(ctx.s("mm" + tag), b0["mm"] + i * 6 + 4 + vc + 1)
                    if i >= 2:
                        sv.wait(v, i - 2)
                    v.tensor_copy(
                        vaug[b].ap().rearrange("p (h c) -> p h c", c=128)[:, tsl(vc, 8), 0:64],
                        pq[k].ap().rearrange("p (h c) -> p h c", c=64)).then_inc(ctx.s("ve" + tag), 1)
                    ctx.inc("ve" + tag, 1)
            return ("pe" + tag, ctx.val("pe" + tag))

        ph.gpsimd(f_gpsimd)
        ph.sync(f_sync)
        ph.tensor(f_tensor)
        ph.scalar(f_scalar)
        ph.vector(f_vector)
        ph.emit()


def p_ffn_attn(ctx, wo_d, qsr, ksr, QTl, KTl, VAUGl, Z32, U32, U16,
               consts, tag):
    nc = ctx.nc
    NBLK = ctx.NBLK
    NH = H
    for n in ["mm", "mk", "xp", "pv", "dv", "wm", "ev", "vv", "cb"]:
        ctx.sem(n + tag)
    b0 = {n: ctx.val(n + tag) for n in ["mm", "mk", "xp", "pv", "dv", "wm", "ev", "vv", "cb"]}
    onesb = consts["onesb"]
    lmask_d = consts["lmask_d"]
    wo_r = wo_d.rearrange("(a p) c -> a p c", p=128)
    with contextlib.ExitStack() as es:
        lmask = es.enter_context(nc.sbuf_tensor("lmsk" + tag, [128, 4, 256], F32))
        wo = es.enter_context(nc.sbuf_tensor("wo" + tag, [128, NB, D], BF16))
        qtb = [es.enter_context(nc.sbuf_tensor(f"qt{tag}{i}", [128, NB, 256], BF16)) for i in range(2)]
        ktb = [es.enter_context(nc.sbuf_tensor(f"kt{tag}{i}", [128, NB, 512], BF16)) for i in range(2)]
        vtb = [es.enter_context(nc.sbuf_tensor(f"vt{tag}{i}", [128, 4, 2048], BF16)) for i in range(2)]
        zrb = [es.enter_context(nc.sbuf_tensor(f"zr{tag}{i}", [128, NB, 256], F32)) for i in range(2)]
        pS16 = [es.enter_context(nc.sbuf_tensor(f"pp{tag}{i}", [128, 4, 256], BF16)) for i in range(2)]
        rl = es.enter_context(nc.sbuf_tensor("rl" + tag, [64, 256], F32))
        osb = [es.enter_context(nc.sbuf_tensor(f"ob{tag}{i}", [128, NB, 256], BF16)) for i in range(2)]
        a32 = [es.enter_context(nc.sbuf_tensor(f"a3{tag}{i}", [128, 256], F32)) for i in range(2)]
        o32 = [es.enter_context(nc.sbuf_tensor(f"o3{tag}{i}", [128, 256], F32)) for i in range(2)]
        o16 = [es.enter_context(nc.sbuf_tensor(f"o6{tag}{i}", [128, 256], BF16)) for i in range(2)]
        crow = es.enter_context(nc.sbuf_tensor("cr" + tag, [1, 2, 64], F32))
        cmx = es.enter_context(nc.sbuf_tensor("cm" + tag, [1, 64], F32))
        cneg = es.enter_context(nc.sbuf_tensor("cn" + tag, [1, 1], BF16))
        cbx = es.enter_context(nc.sbuf_tensor("cx" + tag, [128, 1], F32))
        psS = [es.enter_context(nc.psum_tensor(f"pS{tag}{i}", [128, 4, 256], F32)) for i in range(2)]
        psO = [es.enter_context(nc.psum_tensor(f"pO{tag}{i}", [128, 256], F32)) for i in range(2)]
        psW = [es.enter_context(nc.psum_tensor(f"pW{tag}{i}", [128, 256], F32)) for i in range(2)]
        wl = Rot(ctx, "wl" + tag, 1, n_per=NB, swdge=True)
        lq = Rot(ctx, "lb" + tag, 2, n_per=NB)
        lk = Rot(ctx, "lc" + tag, 2, n_per=NB)
        lv = Rot(ctx, "ld" + tag, 2, n_per=4)
        lz = Rot(ctx, "le" + tag, 2, n_per=NB)
        lcr = Rot(ctx, "lf" + tag, 1, n_per=2)
        lmk = Rot(ctx, "lm" + tag, 1)
        so = Rot(ctx, "sg" + tag, 2, n_per=2)
        ph = ctx.phase("fatt" + tag)

        def f_gpsimd(g):
            for j in range(NB):
                wl.issue(0, g.dma_start(out=wo[:, j, :], in_=wo_r[j]))
            return wl.final()[-1]

        def f_sync(sy):
            lmk.issue(0, sy.dma_start(out=lmask[:], in_=lmask_d))
            lcr.issue(0, sy.dma_start(out=crow[0:1, 0, :], in_=qsr))
            lcr.issue(0, sy.dma_start(out=crow[0:1, 1, :], in_=ksr))
            for blk in range(NBLK):
                b = blk % 2
                if blk >= 2:
                    sy.wait_ge(ctx.s("dv" + tag), b0["dv"] + (blk - 1) * NH)
                    sy.wait_ge(ctx.s("vv" + tag), b0["vv"] + (blk - 1) * NB)
                for j in range(NB):
                    lq.issue(blk, sy.dma_start(out=qtb[b][:, j, :], in_=QTl[j, :, tsl(blk, 256)]))
                if blk == 0:
                    for j in range(NB):
                        lk.issue(blk, sy.dma_start(out=ktb[b][:, j, 256:512], in_=KTl[j, :, 0:256]))
                    for s_ in range(2):
                        lv.issue(blk, sy.dma_start(out=vtb[b][:, 2 + s_, :], in_=VAUGl[s_]))
                    for s_ in range(2):
                        lv.issue(blk, sy.dma_start(out=vtb[b][:, s_, 0:64], in_=VAUGl[0, :, 0:64]))
                else:
                    for j in range(NB):
                        lk.issue(blk, sy.dma_start(out=ktb[b][:, j, :],
                                                   in_=KTl[j, :, (blk - 1) * 256:(blk + 1) * 256]))
                    for s_ in range(4):
                        lv.issue(blk, sy.dma_start(out=vtb[b][:, s_, :], in_=VAUGl[2 * (blk - 1) + s_]))
                for j in range(NB):
                    lz.issue(blk, sy.dma_start(out=zrb[b][:, j, :], in_=Z32[j, :, tsl(blk, 256)]))
                if blk >= 1:
                    for co in range(NB):
                        w = (blk - 1) * NB + co
                        sy.wait_ge(ctx.s("vv" + tag), b0["vv"] + w + 1)
                        so.issue(w, sy.dma_start(out=U32[co, :, tsl(blk - 1, 256)], in_=o32[w % 2][:]))
                        so.issue(w, sy.dma_start(out=U16[co, :, tsl(blk - 1, 256)], in_=o16[w % 2][:]))
            for co in range(NB):
                w = (NBLK - 1) * NB + co
                sy.wait_ge(ctx.s("vv" + tag), b0["vv"] + w + 1)
                so.issue(w, sy.dma_start(out=U32[co, :, tsl(NBLK - 1, 256)], in_=o32[w % 2][:]))
                so.issue(w, sy.dma_start(out=U16[co, :, tsl(NBLK - 1, 256)], in_=o16[w % 2][:]))
            return so.final()

        def f_tensor(t):
            t.wait_ge(ctx.s("cb" + tag), b0["cb"] + 1)
            t.matmul(psW[0][:, 0:1], onesb[0:1, :], cneg[0:1, 0:1],
                     start=True, stop=True).then_inc(ctx.s("cb" + tag), 1)
            ctx.inc("cb" + tag, 1)
            wl.wait(t, 0)

            def PV_(PVu):
                t.wait_ge(ctx.s("xp" + tag), b0["xp"] + PVu + 1)
                if PVu >= 1:
                    t.wait_ge(ctx.s("dv" + tag), b0["dv"] + PVu - 1 + 1)
                pb_, ph_ = divmod(PVu, NH)
                pns0 = 2 if pb_ == 0 else 0
                pbb = pb_ % 2
                last = None
                for s_ in range(pns0, 4):
                    last = t.matmul(psO[PVu % 2][:], vtb[pbb][:, s_, ph_ * 128:ph_ * 128 + 128],
                                    pS16[PVu % 2][:, s_, :], start=(s_ == pns0), stop=(s_ == 3))
                last.then_inc(ctx.s("pv" + tag), 1)
                ctx.inc("pv" + tag, 1)

            for blk in range(NBLK):
                b = blk % 2
                if blk >= 1:
                    PV_(blk * NH - 1)
                lq.wait(t, blk)
                lk.wait(t, blk)
                lv.wait(t, blk)
                ns0 = 2 if blk == 0 else 0
                for h in range(NH):
                    u = blk * NH + h
                    if u >= 2:
                        t.wait_ge(ctx.s("xp" + tag), b0["xp"] + u - 1)
                    hr = slice((h % 2) * 64, (h % 2) * 64 + 64)
                    last = None
                    for s_ in range(ns0, 4):
                        last = t.matmul(psS[u % 2][:, s_, :], ktb[b][hr, h // 2, tsl(s_, 128)],
                                        qtb[b][hr, h // 2, :], start=True, stop=True,
                                        tile_position=((h % 2) * 64, 0))
                    last.then_inc(ctx.s("mm" + tag), 1)
                    ctx.inc("mm" + tag, 1)
                    if h >= 1:
                        PV_(u - 1)
                if blk >= 1:
                    wb_ = blk - 1
                    t.wait_ge(ctx.s("dv" + tag), b0["dv"] + wb_ * NH + NH)
                    if wb_ == 0:
                        t.wait_ge(ctx.s("cb" + tag), b0["cb"] + 3)
                    for co in range(NB):
                        w = wb_ * NB + co
                        if w >= 2:
                            t.wait_ge(ctx.s("ev" + tag), b0["ev"] + w - 1)
                        last = None
                        for j in range(NB):
                            last = t.matmul(psW[w % 2][:], wo[:, j, tsl(co, 128)], osb[wb_ % 2][:, j, :],
                                            start=(j == 0), stop=(j == NB - 1))
                        last.then_inc(ctx.s("wm" + tag), 1)
                        ctx.inc("wm" + tag, 1)
            PV_(NBLK * NH - 1)
            wb_ = NBLK - 1
            t.wait_ge(ctx.s("dv" + tag), b0["dv"] + wb_ * NH + NH)
            for co in range(NB):
                w = wb_ * NB + co
                if w >= 2:
                    t.wait_ge(ctx.s("ev" + tag), b0["ev"] + w - 1)
                last = None
                for j in range(NB):
                    last = t.matmul(psW[w % 2][:], wo[:, j, tsl(co, 128)], osb[wb_ % 2][:, j, :],
                                    start=(j == 0), stop=(j == NB - 1))
                last.then_inc(ctx.s("wm" + tag), 1)
                ctx.inc("wm" + tag, 1)
            return ("wm" + tag, ctx.val("wm" + tag))

        def f_vector(v):
            lcr.wait(v, 0)
            v.tensor_tensor(cmx[:], crow[0:1, 0, :], crow[0:1, 1, :], ALU.mult)
            v.reduce_max(cmx[:, 0:1], cmx[:], AX.X)
            v.tensor_scalar(cneg[:], cmx[:, 0:1], -8.0, None, ALU.mult).then_inc(ctx.s("cb" + tag), 1)
            ctx.inc("cb" + tag, 1)
            v.wait_ge(ctx.s("cb" + tag), b0["cb"] + 2)
            v.tensor_copy(cbx[:], psW[0][:, 0:1]).then_inc(ctx.s("cb" + tag), 1)
            ctx.inc("cb" + tag, 1)
            lmk.wait(v, 0)

            def div_(u):
                db_, dh_ = divmod(u, NH)
                v.wait_ge(ctx.s("pv" + tag), b0["pv"] + u + 1)
                if db_ >= 2 and dh_ == 0:
                    v.wait_ge(ctx.s("wm" + tag), b0["wm"] + (db_ - 1) * NB)
                v.reciprocal(rl[:], psO[u % 2][64:128, :])
                v.tensor_tensor(osb[db_ % 2][(dh_ % 2) * 64:(dh_ % 2) * 64 + 64, dh_ // 2, :],
                                psO[u % 2][0:64, :], rl[:], ALU.mult)
                vfence(v, consts).then_inc(ctx.s("dv" + tag), 1)
                ctx.inc("dv" + tag, 1)

            for blk in range(NBLK):
                ns0 = 2 if blk == 0 else 0
                if blk >= 1:
                    div_(blk * NH - 1)
                for h in range(NH):
                    u = blk * NH + h
                    v.wait_ge(ctx.s("mm" + tag), b0["mm"] + u + 1)
                    v.tensor_tensor(psS[u % 2][:, ns0:4, :], psS[u % 2][:, ns0:4, :],
                                    lmask[:, ns0:4, :], ALU.add)
                    vfence(v, consts).then_inc(ctx.s("mk" + tag), 1)
                    ctx.inc("mk" + tag, 1)
                    if h >= 1:
                        div_(u - 1)
                if blk >= 1:
                    wb_ = blk - 1
                    for co in range(NB):
                        w = wb_ * NB + co
                        v.wait_ge(ctx.s("ev" + tag), b0["ev"] + w + 1)
                        lz.wait(v, wb_)
                        if w >= 2:
                            so.wait(v, w - 2)
                        v.tensor_tensor(o32[w % 2][:], a32[w % 2][:], zrb[wb_ % 2][:, co, :], ALU.add)
                        v.tensor_copy(o16[w % 2][:], o32[w % 2][:])
                        vfence(v, consts).then_inc(ctx.s("vv" + tag), 1)
                        ctx.inc("vv" + tag, 1)
            div_(NBLK * NH - 1)
            wb_ = NBLK - 1
            for co in range(NB):
                w = wb_ * NB + co
                v.wait_ge(ctx.s("ev" + tag), b0["ev"] + w + 1)
                lz.wait(v, wb_)
                if w >= 2:
                    so.wait(v, w - 2)
                v.tensor_tensor(o32[w % 2][:], a32[w % 2][:], zrb[wb_ % 2][:, co, :], ALU.add)
                v.tensor_copy(o16[w % 2][:], o32[w % 2][:])
                vfence(v, consts).then_inc(ctx.s("vv" + tag), 1)
                ctx.inc("vv" + tag, 1)
            return ("vv" + tag, ctx.val("vv" + tag))

        def f_scalar(s):
            s.wait_ge(ctx.s("cb" + tag), b0["cb"] + 3)
            for blk in range(NBLK):
                ns0 = 2 if blk == 0 else 0
                for h in range(NH):
                    u = blk * NH + h
                    s.wait_ge(ctx.s("mk" + tag), b0["mk"] + u + 1)
                    if u >= 2:
                        s.wait_ge(ctx.s("pv" + tag), b0["pv"] + u - 1)
                    s.activation(pS16[u % 2][:, ns0:4, :], psS[u % 2][:, ns0:4, :], AF.Exp,
                                 bias=cbx[:, 0:1])
                    sfence(s, consts).then_inc(ctx.s("xp" + tag), 1)
                    ctx.inc("xp" + tag, 1)
                if blk >= 1:
                    wb_ = blk - 1
                    for co in range(NB):
                        w = wb_ * NB + co
                        s.wait_ge(ctx.s("wm" + tag), b0["wm"] + w + 1)
                        if w >= 2:
                            s.wait_ge(ctx.s("vv" + tag), b0["vv"] + w - 1)
                        s.activation(a32[w % 2][:], psW[w % 2][:], AF.Identity,
                                     bias=consts["scal"][:, 3:4])
                        sfence(s, consts).then_inc(ctx.s("ev" + tag), 1)
                        ctx.inc("ev" + tag, 1)
            wb_ = NBLK - 1
            for co in range(NB):
                w = wb_ * NB + co
                s.wait_ge(ctx.s("wm" + tag), b0["wm"] + w + 1)
                if w >= 2:
                    s.wait_ge(ctx.s("vv" + tag), b0["vv"] + w - 1)
                s.activation(a32[w % 2][:], psW[w % 2][:], AF.Identity,
                             bias=consts["scal"][:, 3:4])
                sfence(s, consts).then_inc(ctx.s("ev" + tag), 1)
                ctx.inc("ev" + tag, 1)
            return ("ev" + tag, ctx.val("ev" + tag))

        ph.gpsimd(f_gpsimd)
        ph.sync(f_sync)
        ph.tensor(f_tensor)
        ph.vector(f_vector)
        ph.scalar(f_scalar)
        ph.emit()


def p_ffn_geglu(ctx, w1_d, w2_d, b1_sb, b2_sb, USH16, U32, Z32n, Z16n,
                consts, tag):
    nc = ctx.nc
    NT = ctx.NT
    NK = 22
    CW = [128] * 21 + [42]
    for n in ["ma", "ev", "hh", "mw", "e2", "vv"]:
        ctx.sem(n + tag)
    b0 = {n: ctx.val(n + tag) for n in ["ma", "ev", "hh", "mw", "e2", "vv"]}
    w1_r = w1_d.rearrange("(a p) c -> a p c", p=128)
    with contextlib.ExitStack() as es:
        w1 = es.enter_context(nc.sbuf_tensor("w1" + tag, [128, NB, 5460], BF16))
        w2 = es.enter_context(nc.sbuf_tensor("w2" + tag, [128, NK, D], BF16))
        ush = [es.enter_context(nc.sbuf_tensor(f"us{tag}{i}", [128, NB, 512], BF16)) for i in range(2)]
        ur = [es.enter_context(nc.sbuf_tensor(f"ur{tag}{i}", [128, 512], F32)) for i in range(2)]
        h16 = es.enter_context(nc.sbuf_tensor("h6" + tag, [128, NK, 512], BF16))
        ga32 = [es.enter_context(nc.sbuf_tensor(f"ga{tag}{i}", [128, 512], F32)) for i in range(2)]
        gg32 = [es.enter_context(nc.sbuf_tensor(f"gg{tag}{i}", [128, 512], F32)) for i in range(2)]
        z32 = [es.enter_context(nc.sbuf_tensor(f"z3{tag}{i}", [128, 512], F32)) for i in range(2)]
        z16 = [es.enter_context(nc.sbuf_tensor(f"z6{tag}{i}", [128, 512], BF16)) for i in range(2)]
        a2 = [es.enter_context(nc.sbuf_tensor(f"a2{tag}{i}", [128, 512], F32)) for i in range(2)]
        psA = [es.enter_context(nc.psum_tensor(f"pA{tag}{i}", [128, 512], F32)) for i in range(2)]
        psG = [es.enter_context(nc.psum_tensor(f"pG{tag}{i}", [128, 512], F32)) for i in range(2)]
        psW = [es.enter_context(nc.psum_tensor(f"pw{tag}{i}", [128, 512], F32)) for i in range(2)]
        wl = Rot(ctx, "wl" + tag, 1, n_per=NB + NK, swdge=True)
        lu = Rot(ctx, "lu" + tag, 2, n_per=NB)
        rr = Rot(ctx, "rr" + tag, 2, n_per=1)
        so = Rot(ctx, "sh" + tag, 2, n_per=2)
        ph = ctx.phase("geglu" + tag)

        def f_gpsimd(g):
            for j in range(NB):
                wl.issue(0, g.dma_start(out=w1[:, j, :], in_=w1_r[j]))
            for k in range(NK):
                wl.issue(0, g.dma_start(out=w2[0:CW[k], k, :], in_=w2_d[128 * k:128 * k + CW[k], :]))
            return wl.final()[-1]

        def f_sync2(sy):
            for tt in range(NT):
                b = tt % 2
                if tt >= 2:
                    sy.wait_ge(ctx.s("ma" + tag), b0["ma"] + (tt - 1) * 2 * NK)
                for j in range(NB):
                    lu.issue(tt, sy.dma_start(out=ush[b][:, j, :], in_=USH16[j, :, tsl(tt, 512)]))
                for co in range(NB):
                    w = tt * NB + co
                    if w >= 2:
                        sy.wait_ge(ctx.s("vv" + tag), b0["vv"] + w - 1)
                    rr.issue(w, sy.dma_start(out=ur[w % 2][:], in_=U32[co, :, tsl(tt, 512)]))
                    sy.wait_ge(ctx.s("vv" + tag), b0["vv"] + w + 1)
                    so.issue(w, sy.dma_start(out=Z32n[co, :, tsl(tt, 512)], in_=z32[w % 2][:]))
                    so.issue(w, sy.dma_start(out=Z16n[co, :, tsl(tt, 512)], in_=z16[w % 2][:]))
            return so.final()

        def f_tensor(t):
            wl.wait(t, 0)
            for tt in range(NT):
                b = tt % 2
                lu.wait(t, tt)
                for k in range(NK):
                    n = tt * NK + k
                    if n >= 2:
                        t.wait_ge(ctx.s("ev" + tag), b0["ev"] + n - 1)
                    last = None
                    for j in range(NB):
                        last = t.matmul(psA[n % 2][0:CW[k], :], w1[:, j, 128 * k:128 * k + CW[k]],
                                        ush[b][:, j, :], start=(j == 0), stop=(j == NB - 1))
                    last.then_inc(ctx.s("ma" + tag), 1)
                    ctx.inc("ma" + tag, 1)
                    last = None
                    for j in range(NB):
                        last = t.matmul(psG[n % 2][0:CW[k], :], w1[:, j, 2730 + 128 * k:2730 + 128 * k + CW[k]],
                                        ush[b][:, j, :], start=(j == 0), stop=(j == NB - 1))
                    last.then_inc(ctx.s("ma" + tag), 1)
                    ctx.inc("ma" + tag, 1)
                t.wait_ge(ctx.s("hh" + tag), b0["hh"] + (tt + 1) * NK)
                for co in range(NB):
                    w = tt * NB + co
                    if w >= 2:
                        t.wait_ge(ctx.s("e2" + tag), b0["e2"] + w - 1)
                    last = None
                    for k in range(NK):
                        last = t.matmul(psW[w % 2][:], w2[0:CW[k], k, tsl(co, 128)], h16[0:CW[k], k, :],
                                        start=(k == 0), stop=(k == NK - 1))
                    last.then_inc(ctx.s("mw" + tag), 1)
                    ctx.inc("mw" + tag, 1)
            return ("mw" + tag, ctx.val("mw" + tag))

        def f_scalar(s):
            for tt in range(NT):
                for k in range(NK):
                    n = tt * NK + k
                    s.wait_ge(ctx.s("ma" + tag), b0["ma"] + 2 * n + 2)
                    if n >= 2:
                        s.wait_ge(ctx.s("hh" + tag), b0["hh"] + n - 1)
                    s.activation(ga32[n % 2][0:CW[k], :], psA[n % 2][0:CW[k], :], AF.Identity,
                                 bias=b1_sb[0:CW[k], k:k + 1])
                    s.activation(gg32[n % 2][0:CW[k], :], psG[n % 2][0:CW[k], :], AF.Gelu,
                                 bias=b1_sb[0:CW[k], 22 + k:23 + k]).then_inc(ctx.s("ev" + tag), 1)
                    ctx.inc("ev" + tag, 1)
                for co in range(NB):
                    w = tt * NB + co
                    s.wait_ge(ctx.s("mw" + tag), b0["mw"] + w + 1)
                    if w >= 2:
                        s.wait_ge(ctx.s("vv" + tag), b0["vv"] + w - 1)
                    s.activation(a2[w % 2][:], psW[w % 2][:], AF.Identity,
                                 bias=b2_sb[:, co:co + 1]).then_inc(ctx.s("e2" + tag), 1)
                    ctx.inc("e2" + tag, 1)
            return ("e2" + tag, ctx.val("e2" + tag))

        def f_vector(v):
            for tt in range(NT):
                for k in range(NK):
                    n = tt * NK + k
                    v.wait_ge(ctx.s("ev" + tag), b0["ev"] + n + 1)
                    if tt >= 1 and k == 0:
                        v.wait_ge(ctx.s("mw" + tag), b0["mw"] + tt * NB)
                    v.tensor_tensor(h16[0:CW[k], k, :], ga32[n % 2][0:CW[k], :], gg32[n % 2][0:CW[k], :],
                                    ALU.mult).then_inc(ctx.s("hh" + tag), 1)
                    ctx.inc("hh" + tag, 1)
                for co in range(NB):
                    w = tt * NB + co
                    v.wait_ge(ctx.s("e2" + tag), b0["e2"] + w + 1)
                    rr.wait(v, w)
                    if w >= 2:
                        so.wait(v, w - 2)
                    v.tensor_tensor(z32[w % 2][:], a2[w % 2][:], ur[w % 2][:], ALU.add)
                    v.tensor_copy(z16[w % 2][:], z32[w % 2][:]).then_inc(ctx.s("vv" + tag), 1)
                    ctx.inc("vv" + tag, 1)
            return ("vv" + tag, ctx.val("vv" + tag))

        ph.gpsimd(f_gpsimd)
        ph.sync(f_sync2)
        ph.tensor(f_tensor)
        ph.scalar(f_scalar)
        ph.vector(f_vector)
        ph.emit()


def p_final(ctx, x_nat, S32, Z32, OUT, consts, tag="fz"):
    nc = ctx.nc
    NT = ctx.NT
    for n in ["sg", "gm", "tp", "ev", "ad"]:
        ctx.sem(n + tag)
    b0 = {n: ctx.val(n + tag) for n in ["sg", "gm", "tp", "ev", "ad"]}
    x_r = x_nat.rearrange("(n p) d -> n p d", p=128)
    out_r = OUT.rearrange("(n p) d -> n p d", p=128)
    idf = consts["idf"]
    with contextlib.ExitStack() as es:
        sb = [es.enter_context(nc.sbuf_tensor(f"sb{tag}{i}", [128, NB, 512], F32)) for i in range(2)]
        zb = [es.enter_context(nc.sbuf_tensor(f"zb{tag}{i}", [128, NB, 512], F32)) for i in range(2)]
        gb = es.enter_context(nc.sbuf_tensor("gb" + tag, [128, NB, 512], F32))
        xb = [es.enter_context(nc.sbuf_tensor(f"xb{tag}{i}", [128, D], F32)) for i in range(2)]
        ob = [es.enter_context(nc.sbuf_tensor(f"ob{tag}{i}", [128, D], F32)) for i in range(2)]
        pst = es.enter_context(nc.psum_tensor("pt" + tag, [128, D], F32))
        ls = Rot(ctx, "ls" + tag, 2, n_per=2 * NB)
        lx = Rot(ctx, "lx" + tag, 2, n_per=1)
        st = Rot(ctx, "st" + tag, 2, n_per=1)
        ph = ctx.phase("final" + tag)

        def f_sync(sy):
            for tt in range(NT):
                b = tt % 2
                if tt >= 2:
                    sy.wait_ge(ctx.s("tp" + tag), b0["tp"] + (tt - 1) * 4)
                    sy.wait_ge(ctx.s("sg" + tag), b0["sg"] + tt - 1)
                for j in range(NB):
                    ls.issue(tt, sy.dma_start(out=sb[b][:, j, :], in_=S32[j, :, tsl(tt, 512)]))
                    ls.issue(tt, sy.dma_start(out=zb[b][:, j, :], in_=Z32[j, :, tsl(tt, 512)]))
                for c in range(4):
                    i = tt * 4 + c
                    if i >= 2:
                        sy.wait_ge(ctx.s("ad" + tag), b0["ad"] + i - 1)
                    lx.issue(i, sy.dma_start(out=xb[i % 2][:], in_=x_r[i]))
                    sy.wait_ge(ctx.s("ad" + tag), b0["ad"] + i + 1)
                    st.issue(i, sy.dma_start(out=out_r[i], in_=ob[i % 2][:]))
            return st.final()

        def f_scalar(s):
            for tt in range(NT):
                b = tt % 2
                ls.wait(s, tt)
                if tt >= 1:
                    s.wait_ge(ctx.s("gm" + tag), b0["gm"] + tt)
                s.activation(gb[:].rearrange("p a t -> p (a t)"), zb[b][:].rearrange("p a t -> p (a t)"),
                             AF.Silu).then_inc(ctx.s("sg" + tag), 1)
                ctx.inc("sg" + tag, 1)
            return ("sg" + tag, ctx.val("sg" + tag))

        def f_gpsimd(g):
            for tt in range(NT):
                b = tt % 2
                g.wait_ge(ctx.s("sg" + tag), b0["sg"] + tt + 1)
                g.tensor_mul(sb[b][:].rearrange("p a t -> p (a t)"),
                             sb[b][:].rearrange("p a t -> p (a t)"),
                             gb[:].rearrange("p a t -> p (a t)")).then_inc(ctx.s("gm" + tag), 1)
                ctx.inc("gm" + tag, 1)
            return ("gm" + tag, ctx.val("gm" + tag))

        def f_tensor(t):
            for tt in range(NT):
                b = tt % 2
                t.wait_ge(ctx.s("gm" + tag), b0["gm"] + tt + 1)
                for c in range(4):
                    i = tt * 4 + c
                    if i >= 1:
                        t.wait_ge(ctx.s("ev" + tag), b0["ev"] + i)
                    last = None
                    for j in range(NB):
                        last = t.transpose(pst[:, tsl(j, 128)], sb[b][:, j, tsl(c, 128)], idf[:])
                    last.then_inc(ctx.s("tp" + tag), 1)
                    ctx.inc("tp" + tag, 1)
            return ("tp" + tag, ctx.val("tp" + tag))

        def f_vector(v):
            for tt in range(NT):
                for c in range(4):
                    i = tt * 4 + c
                    v.wait_ge(ctx.s("tp" + tag), b0["tp"] + i + 1)
                    v.tensor_copy(ob[i % 2][:], pst[:]).then_inc(ctx.s("ev" + tag), 1)
                    ctx.inc("ev" + tag, 1)
                    lx.wait(v, i)
                    if i >= 2:
                        st.wait(v, i - 2)
                    v.tensor_tensor(ob[i % 2][:], ob[i % 2][:], xb[i % 2][:], ALU.add).then_inc(ctx.s("ad" + tag), 1)
                    ctx.inc("ad" + tag, 1)
            return ("ad" + tag, ctx.val("ad" + tag))

        ph.sync(f_sync)
        ph.scalar(f_scalar)
        ph.gpsimd(f_gpsimd)
        ph.tensor(f_tensor)
        ph.vector(f_vector)
        ph.emit()


import ml_dtypes


def build_full(S=4096, race=False, dbg=False):
    nc = bass.Bass("TRN2", target_bir_lowering=False, detect_race_conditions=race)
    NTT = S // 128
    with contextlib.ExitStack() as stack:
        ctx = Ctx(nc, stack, S)
        DT = nc.dram_tensor
        x = DT("x", [S, 1024], F32, kind="ExternalInput").ap()
        qkvw = DT("qkv_w", [1024, 3072], F32, kind="ExternalInput").ap()
        outw = DT("out_w", [1024, 1024], F32, kind="ExternalInput").ap()
        wctx_d = DT("w_ctx", [1024, 1024], F32, kind="ExternalInput").ap()
        wx_d = DT("w_x", [1024, 1024], F32, kind="ExternalInput").ap()
        ffw1_d = DT("ff_w1", [1024, 4096], F32, kind="ExternalInput").ap()
        ffw2_d = DT("ff_w2", [4096, 1024], F32, kind="ExternalInput").ap()
        Wqkv = DT("fWqkv", [6, 1024, 3072], F32, kind="ExternalInput").ap()
        Wo = DT("fWo", [6, 1024, 1024], F32, kind="ExternalInput").ap()
        W1 = DT("fW1", [6, 1024, 5460], F32, kind="ExternalInput").ap()
        W2 = DT("fW2", [6, 2730, 1024], F32, kind="ExternalInput").ap()
        ins_small = {}
        for nm, shp in [("P_qkvb", [128, 24]), ("P_qkvb_row", [1, 3072]),
                        ("P_outb", [128, 8]), ("P_gattn", [128, 8]), ("P_battn", [128, 8]),
                        ("P_g1", [128, 8]), ("P_b1", [128, 8]), ("P_g2", [128, 8]),
                        ("P_b2", [128, 8]), ("P_binl", [128, 8]),
                        ("P_ffb1", [128, 32]), ("P_ffb2", [128, 8]),
                        ("P_qs", [6, 64]), ("P_ks", [6, 64]), ("P_qbc", [6, 128, 4, 64]),
                        ("P_lng", [6, 128, 8]), ("P_lnb", [6, 128, 8]),
                        ("P_fb1", [6, 128, 44]), ("P_fb2", [6, 128, 8]),
                        ("C_idf", [128, 128]), ("C_scal", [128, 8]),
                        ("C_cmask", [128, 4, 512]), ("C_lmask", [128, 4, 256]),
                        ("C_rqc", [NTT, 128, 64]), ("C_rqs", [NTT, 128, 64]),
                        ("C_rkc", [NTT, 128, 64]), ("C_rks", [NTT, 128, 64])]:
            ins_small[nm] = DT(nm, shp, F32, kind="ExternalInput").ap()
        for nm, shp in [("C_idb", [128, 128]), ("C_onesb", [128, 128])]:
            ins_small[nm] = DT(nm, shp, BF16, kind="ExternalInput").ap()
        OUT = DT("OUT", [S, 1024], F32, kind="ExternalOutput").ap()

        ST = lambda nm, f32=True: DT(nm, [NB, 128, S], F32 if f32 else BF16,
                                     kind=("ExternalOutput" if dbg else "Internal")).ap()
        XT32, XT16 = ST("XT32"), ST("XT16", False)
        XN32, XN16 = ST("XN32"), ST("XN16", False)
        QT, KT = ST("QT", False), ST("KT", False)
        VAC = DT("VAC", [NTT, 128, 2048], BF16, kind="Internal").ap()
        OTC = ST("OTC", False)
        CTXB = ST("CTXB", False)
        Y1, Y1B = ST("Y1"), ST("Y1B", False)
        XS0A, XS0B = ST("XS0A"), ST("XS0B", False)
        Y2, Y2B = ST("Y2"), ST("Y2B", False)
        HN16 = ST("HN16", False)
        H16 = DT("H16", [32, 128, S], BF16, kind="Internal").ap()
        S32 = ST("S32")
        ZA32, ZA16 = ST("ZA32"), ST("ZA16", False)
        ZB32, ZB16 = ST("ZB32"), ST("ZB16", False)
        U32, U16 = ST("U32"), ST("U16", False)
        USH16 = ST("USH16", False)
        QTl, KTl = ST("QTl", False), ST("KTl", False)
        VAUGl = DT("VAUGl", [NTT, 128, 2048], BF16, kind="Internal").ap()

        SB = {}
        for nm, shp, dt_ in [("idf", [128, 128], F32), ("idb", [128, 128], BF16),
                             ("onesb", [128, 128], BF16), ("scal", [128, 8], F32),
                             ("qkvb", [128, 24], F32), ("qkvbr", [1, 3072], F32),
                             ("outb", [128, 8], F32), ("ga", [128, 8], F32), ("ba", [128, 8], F32),
                             ("g1", [128, 8], F32), ("b1", [128, 8], F32),
                             ("g2", [128, 8], F32), ("b2", [128, 8], F32), ("binl", [128, 8], F32),
                             ("ffb1", [128, 32], F32), ("ffb2", [128, 8], F32),
                             ("lng", [128, 6, 8], F32), ("lnb", [128, 6, 8], F32),
                             ("fb1", [128, 6, 44], F32), ("fb2", [128, 6, 8], F32),
                             ("fence", [1, 4], F32)]:
            SB[nm] = stack.enter_context(nc.sbuf_tensor(nm, shp, dt_))
        consts = {"idf": SB["idf"], "idb": SB["idb"], "onesb": SB["onesb"],
                  "scal": SB["scal"], "cmask_d": ins_small["C_cmask"],
                  "lmask_d": ins_small["C_lmask"], "qkvb_row": SB["qkvbr"],
                  "fence": SB["fence"]}

        ctx.sem("cl")
        ph = ctx.phase("consts")
        def f_sync(sy):
            pairs = [(SB["idf"][:], ins_small["C_idf"]), (SB["idb"][:], ins_small["C_idb"]),
                     (SB["onesb"][:], ins_small["C_onesb"]), (SB["scal"][:], ins_small["C_scal"]),
                     (SB["qkvb"][:], ins_small["P_qkvb"]), (SB["qkvbr"][:], ins_small["P_qkvb_row"]),
                     (SB["outb"][:], ins_small["P_outb"]), (SB["ga"][:], ins_small["P_gattn"]),
                     (SB["ba"][:], ins_small["P_battn"]), (SB["g1"][:], ins_small["P_g1"]),
                     (SB["b1"][:], ins_small["P_b1"]), (SB["g2"][:], ins_small["P_g2"]),
                     (SB["b2"][:], ins_small["P_b2"]), (SB["binl"][:], ins_small["P_binl"]),
                     (SB["ffb1"][:], ins_small["P_ffb1"]), (SB["ffb2"][:], ins_small["P_ffb2"])]
            for li in range(6):
                pairs += [(SB["lng"][:, li, :], ins_small["P_lng"][li]),
                          (SB["lnb"][:, li, :], ins_small["P_lnb"][li]),
                          (SB["fb1"][:, li, :], ins_small["P_fb1"][li]),
                          (SB["fb2"][:, li, :], ins_small["P_fb2"][li])]
            for dst, src in pairs:
                sy.dma_start(out=dst, in_=src).then_inc(ctx.s("cl"), 16)
                ctx.inc("cl", 16)
            return ("cl", ctx.val("cl"))
        ph.sync(f_sync)
        ph.emit()

        p_nat_to_t(ctx, x, XT32, XT16, consts, "A")
        p_ln_t(ctx, XT32, XT16, None, None, XN32, XN16, consts, "B")
        with contextlib.ExitStack() as s2:
            xn2b = s2.enter_context(nc.sbuf_tensor("xn2b", [128, NB, S], BF16))
            p_ln_t(ctx, XN32, XN16, SB["ga"], SB["ba"], None, None, consts, "C", out16_sb=xn2b)
            p_attn_qkv(ctx, qkvw, SB["qkvb"], xn2b, QT, KT, VAC, consts, "D")
        p_causal_attn(ctx, QT, KT, VAC, OTC, consts, "E")
        p_proj_t(ctx, outw, OTC, SB["outb"], CTXB, XN32, Y1, Y1B, consts, "F")
        p_ln_t(ctx, Y1, Y1B, SB["g1"], SB["b1"], XS0A, XS0B, consts, "G")
        p_scan(ctx, wctx_d, wx_d, SB["binl"], CTXB, XS0A, XS0B, Y1, Y2, Y2B, consts, "H")
        p_ln_t(ctx, Y2, Y2B, SB["g2"], SB["b2"], None, HN16, consts, "I")
        p_proj_t(ctx, ffw1_d, HN16, SB["ffb1"], H16, None, None, None, consts, "J",
                 nbout=32, act=AF.Gelu)
        p_proj_t(ctx, ffw2_d, H16, SB["ffb2"], None, Y2, S32, None, consts, "K", nbin=32)

        if dbg:
            ZDBG = [ST("ZD%d" % li) for li in range(6)]
            UDBG = [ST("UD%d" % li) for li in range(6)]
            QDBG = [ST("QD%d" % li, False) for li in range(6)]
            KDBG = [ST("KD%d" % li, False) for li in range(6)]
            VDBG = [DT("VD%d" % li, [NTT, 128, 2048], BF16, kind="ExternalOutput").ap() for li in range(6)]
        zin32, zin16 = XN32, XN16
        for li in range(6):
            zo32 = (ZDBG[li] if dbg else (ZA32 if li % 2 == 0 else ZB32))
            zo16 = ZA16 if li % 2 == 0 else ZB16
            p_ffn_qkv(ctx, Wqkv[li], ins_small["P_qbc"][li],
                      ins_small["C_rqc"], ins_small["C_rqs"], ins_small["C_rkc"], ins_small["C_rks"],
                      zin16, (QDBG[li] if dbg else QTl), (KDBG[li] if dbg else KTl),
                      (VDBG[li] if dbg else VAUGl), consts, "P%d" % li)
            p_ffn_attn(ctx, Wo[li], ins_small["P_qs"][li:li+1, :], ins_small["P_ks"][li:li+1, :],
                       (QDBG[li] if dbg else QTl), (KDBG[li] if dbg else KTl),
                       (VDBG[li] if dbg else VAUGl), zin32, (UDBG[li] if dbg else U32), U16, consts, "Q%d" % li)
            uu = UDBG[li] if dbg else U32
            p_ln_t(ctx, uu, U16, SB["lng"][:, li, :], SB["lnb"][:, li, :], None, USH16,
                   consts, "R%d" % li, shift=True)
            p_ffn_geglu(ctx, W1[li], W2[li], SB["fb1"][:, li, :], SB["fb2"][:, li, :],
                        USH16, uu, zo32, zo16, consts, "S%d" % li)
            zin32, zin16 = zo32, zo16

        p_final(ctx, x, S32, zin32, OUT, consts, "Z")
    return nc


def _host_inputs(S=4096):
    NTT = S // 128
    HDh = np.arange(0, 64, 2, dtype=np.float64)
    inv_freq = 1.0 / 10000.0 ** (HDh / 64)
    t = np.arange(S, dtype=np.float64)
    freqs = np.concatenate([t[:, None] * inv_freq[None, :]] * 2, 1)
    base_scale = (HDh + 0.4 * 64) / (1.4 * 64)
    power = (t - S // 2) / 128.0
    scale = np.concatenate([base_scale[None, :] ** power[:, None]] * 2, 1)
    sgn = np.where(np.arange(64) < 32, -1.0, 1.0)[None, :]
    C = {}
    C["C_rqc"] = (np.cos(freqs) * scale).astype(np.float32).reshape(NTT, 128, 64)
    C["C_rqs"] = (sgn * np.sin(freqs) * scale).astype(np.float32).reshape(NTT, 128, 64)
    C["C_rkc"] = (np.cos(freqs) / scale).astype(np.float32).reshape(NTT, 128, 64)
    C["C_rks"] = (sgn * np.sin(freqs) / scale).astype(np.float32).reshape(NTT, 128, 64)
    lm = np.zeros((128, 4, 256), np.float32)
    kk = np.arange(128)[:, None]; qq = np.arange(256)[None, :]
    for s_ in range(4):
        d = qq - kk + 256 - 128 * s_
        lm[:, s_, :] = np.where((d >= 0) & (d <= 256), 0.0, NEG)
    C["C_lmask"] = lm
    cm = np.zeros((128, 4, 512), np.float32)
    qq = np.arange(512)[None, :]
    for r in range(4):
        cm[:, r, :] = np.where(kk <= qq - 128 * r, 0.0, NEG)
    C["C_cmask"] = cm
    C["C_idf"] = np.eye(128, dtype=np.float32)
    C["C_idb"] = np.eye(128, dtype=ml_dtypes.bfloat16)
    C["C_onesb"] = np.ones((128, 128), dtype=ml_dtypes.bfloat16)
    sc = np.zeros((128, 8), np.float32)
    sc[:, 0] = 1e-5; sc[:, 1] = 1.0; sc[:, 4] = 1e-38
    C["C_scal"] = sc
    return C


def _layout_params(I):
    P = {}
    r8 = lambda a: np.ascontiguousarray(np.asarray(a, np.float32).reshape(8, 128).T)
    P["P_qkvb"] = np.ascontiguousarray(np.asarray(I["L0_attn_qkv_b"], np.float32).reshape(24, 128).T)
    P["P_qkvb_row"] = np.asarray(I["L0_attn_qkv_b"], np.float32).reshape(1, 3072)
    P["P_outb"] = r8(I["L0_attn_out_b"])
    P["P_gattn"] = r8(I["L0_attn_g_attn"]); P["P_battn"] = r8(I["L0_attn_b_attn"])
    P["P_g1"] = r8(I["L0_attn_g1"]); P["P_b1"] = r8(I["L0_attn_b1"])
    P["P_g2"] = r8(I["L0_attn_g2"]); P["P_b2"] = r8(I["L0_attn_b2"])
    P["P_binl"] = r8(I["L0_attn_b_inl"])
    P["P_ffb1"] = np.ascontiguousarray(np.asarray(I["L0_attn_ff_b1"], np.float32).reshape(32, 128).T)
    P["P_ffb2"] = r8(I["L0_attn_ff_b2"])
    qs = np.asarray(I["L0_ffn_q_scale"], np.float32)
    ks = np.asarray(I["L0_ffn_k_scale"], np.float32)
    perm = (np.arange(64) + 32) % 64
    P["P_qs"] = np.ascontiguousarray(qs); P["P_ks"] = np.ascontiguousarray(ks)
    rows = np.stack([qs, qs[:, perm], ks, ks[:, perm]], 1)
    P["P_qbc"] = np.ascontiguousarray(
        np.broadcast_to(rows[:, None, :, :], (6, 128, 4, 64)).astype(np.float32))
    P["P_lng"] = np.ascontiguousarray(np.asarray(I["L0_ffn_ln_g"], np.float32).reshape(6, 8, 128).transpose(0, 2, 1))
    P["P_lnb"] = np.ascontiguousarray(np.asarray(I["L0_ffn_ln_b"], np.float32).reshape(6, 8, 128).transpose(0, 2, 1))
    fb1 = np.zeros((6, 128, 44), np.float32)
    b1v = np.asarray(I["L0_ffn_b1"], np.float32)
    for li in range(6):
        for k in range(22):
            w = min(128, 2730 - 128 * k)
            fb1[li, 0:w, k] = b1v[li, 128 * k:128 * k + w]
            fb1[li, 0:w, 22 + k] = b1v[li, 2730 + 128 * k:2730 + 128 * k + w]
    P["P_fb1"] = fb1
    P["P_fb2"] = np.ascontiguousarray(np.asarray(I["L0_ffn_b2"], np.float32).reshape(6, 8, 128).transpose(0, 2, 1))
    return P


def kernel(**inputs):
    from concourse.bass_utils import run_bass_kernel_spmd
    S = 4096
    nc = build_full(S=S)
    C = _host_inputs(S)
    P = _layout_params(inputs)
    f32c = lambda a: np.ascontiguousarray(np.asarray(a, np.float32))
    shared = dict(C)
    shared.update(P)
    shared["qkv_w"] = f32c(inputs["L0_attn_qkv_w"])
    shared["out_w"] = f32c(inputs["L0_attn_out_w"])
    shared["w_ctx"] = f32c(inputs["L0_attn_w_ctx"])
    shared["w_x"] = f32c(inputs["L0_attn_w_x"])
    shared["ff_w1"] = f32c(inputs["L0_attn_ff_w1"])
    shared["ff_w2"] = f32c(inputs["L0_attn_ff_w2"])
    shared["fWqkv"] = f32c(inputs["L0_ffn_Wqkv"])
    shared["fWo"] = f32c(inputs["L0_ffn_Wo"])
    shared["fW1"] = f32c(inputs["L0_ffn_W1"])
    shared["fW2"] = f32c(inputs["L0_ffn_W2"])
    xfull = np.asarray(inputs["L0_attn_x"], np.float32)
    in_maps = []
    for b in range(8):
        m = dict(shared)
        m["x"] = np.ascontiguousarray(xfull[b])
        in_maps.append(m)
    res = run_bass_kernel_spmd(nc, in_maps, core_ids=list(range(8)))
    out = np.stack([np.asarray(res.results[b]["OUT"], np.float32) for b in range(8)], axis=0)
    return out


# version 06
# speedup vs baseline: 1.0878x; 1.0258x over previous
import numpy as np
import contextlib
import concourse.bass as bass
import concourse.mybir as mybir

F32 = mybir.dt.float32
BF16 = mybir.dt.bfloat16
AF = mybir.ActivationFunctionType
ALU = mybir.AluOpType
AX = mybir.AxisListType

D = 1024
NB = 8
H = 16
HD = 64
NEG = -1.0e5
EPS = 1e-5
POOL = 68


def tsl(i, n):
    return slice(i * n, (i + 1) * n)


class Ctx:
    def __init__(self, nc, stack, S):
        self.nc = nc
        self.stack = stack
        self.S = S
        self.NT = S // 512
        self.NTT = S // 128
        self.NBLK = S // 256
        self.pool = [stack.enter_context(nc.semaphore(f"sp{i}")) for i in range(POOL)]
        self.pool_cnt = [0] * POOL
        self.next_slot = 1
        self.names = {}
        self.bar_cnt = 0
        self.names["bar"] = 0
        self.bar = self.pool[0]
        self.n_swdge = 0

    def begin_phase_sems(self):
        self.next_slot = 1

    def sem(self, name):
        i = self.next_slot
        self.next_slot += 1
        assert self.next_slot <= POOL, "sem pool exhausted"
        self.names[name] = i
        return self.pool[i]

    def swdge_sem(self, name):
        s = self.stack.enter_context(self.nc.semaphore(name))
        self.pool.append(s)
        self.pool_cnt.append(0)
        self.names[name] = len(self.pool) - 1
        self.n_swdge += 1
        assert self.n_swdge <= 35, "too many swdge sems"
        return s

    def s(self, name):
        return self.pool[self.names[name]]

    def inc(self, name, by=1):
        i = self.names[name]
        self.pool_cnt[i] += by
        return self.pool_cnt[i]

    def val(self, name):
        return self.pool_cnt[self.names[name]]

    def phase(self, name):
        return Phase(self, name)


PROFILE_SCOPES = False


class Phase:
    def __init__(self, ctx, name):
        self.ctx = ctx
        self.name = name
        self.fns = []

    def tensor(self, f): self.fns.append(("tensor", f))
    def vector(self, f): self.fns.append(("vector", f))
    def scalar(self, f): self.fns.append(("scalar", f))
    def gpsimd(self, f): self.fns.append(("gpsimd", f))
    def sync(self, f): self.fns.append(("sync", f))

    def emit(self):
        ctx = self.ctx
        nc = ctx.nc
        with contextlib.ExitStack() as _es:
            if PROFILE_SCOPES:
                _es.enter_context(nc.named_scope(self.name))
            self._emit_block()
        ctx.bar_cnt += len(self.fns)
        if not hasattr(ctx, "phase_log"):
            ctx.phase_log = []
        ctx.phase_log.append((self.name, {n: (ctx.names[n], ctx.pool_cnt[ctx.names[n]]) for n in list(ctx.names)}))
        ctx.begin_phase_sems()

    def _emit_block(self):
        ctx = self.ctx
        nc = ctx.nc
        bar = ctx.bar
        enter = ctx.bar_cnt
        with nc.Block() as block:
            for eng_name, f in self.fns:
                def run(eng, f=f):
                    if enter:
                        eng.wait_ge(bar, enter)
                    last = f(eng)
                    if isinstance(last, tuple):
                        last = [last]
                    if isinstance(last, list):
                        for sname, v in last:
                            eng.wait_ge(ctx.s(sname), v)
                    eng.sem_inc(bar, 1)
                getattr(block, eng_name)(run)


class Rot:

    def __init__(self, ctx, tag, k, n_per=1, swdge=False, plan=None):
        self.ctx = ctx
        self.k = k
        self.n_per = n_per
        self.plan = plan
        self.nm = [tag + str(j) for j in range(k)]
        self.base = []
        for n in self.nm:
            (ctx.swdge_sem(n) if swdge else ctx.sem(n))
            self.base.append(ctx.val(n))
        self.issued = {}

    def value(self, i):
        if self.plan is not None:
            tot = sum(self.plan[j] for j in range(i % self.k, i + 1, self.k))
            return self.base[i % self.k] + 16 * tot
        return self.base[i % self.k] + 16 * self.n_per * (i // self.k + 1)

    def issue(self, i, ins):
        nm = self.nm[i % self.k]
        v = self.ctx.inc(nm, 16)
        ins.then_inc(self.ctx.s(nm), 16)
        self.issued[i] = self.issued.get(i, 0) + 1
        assert v <= self.value(i), f"Rot {nm}: over-issued use {i}"
        return v

    def wait(self, eng, i):
        eng.wait_ge(self.ctx.s(self.nm[i % self.k]), self.value(i))

    def wait_if(self, eng, i):
        self.wait(eng, i)

    def final(self):
        return [(n, self.ctx.val(n)) for n in self.nm if self.ctx.val(n) > self.base[j]
                ] if False else [(n, self.ctx.val(n)) for j, n in enumerate(self.nm)
                                 if self.ctx.val(n) > self.base[j]]


def vfence(v, consts):
    return v.tensor_copy(consts["fence"][0:1, 0:1], consts["scal"][0:1, 0:1])


def sfence(s, consts):
    return s.copy(consts["fence"][0:1, 1:2], consts["scal"][0:1, 0:1])


def wait_all(eng, ctx, rots):
    for r in rots:
        for n, v in r.final():
            eng.wait_ge(ctx.s(n), v)


def p_nat_to_t(ctx, src_nat, dst32, dst16, consts, tag):
    nc = ctx.nc
    NTT = ctx.NTT
    GP = 4
    tp = ctx.sem("tp" + tag)
    evv = ctx.sem("ev" + tag)
    evs = ctx.sem("es" + tag)
    tp0 = ctx.val("tp" + tag)
    ev0 = ctx.val("ev" + tag)
    es0 = ctx.val("es" + tag)
    src_r = src_nat.rearrange("(n p) d -> n p d", p=128)
    with contextlib.ExitStack() as es:
        xt = [es.enter_context(nc.sbuf_tensor(f"x{tag}{i}", [128, D], F32)) for i in range(2)]
        g32 = es.enter_context(nc.sbuf_tensor(f"g32{tag}", [128, NB, GP * 128], F32))
        g16 = es.enter_context(nc.sbuf_tensor(f"g16{tag}", [128, NB, GP * 128], BF16))
        pst = es.enter_context(nc.psum_tensor(f"pt{tag}", [128, D], F32))
        idf = consts["idf"]
        ld = Rot(ctx, "ld" + tag, 2)
        sg = Rot(ctx, "sg" + tag, 2, n_per=NB * ((1 if dst32 is not None else 0) + 1))
        ph = ctx.phase("n2t" + tag)

        def f_sync(sy):
            for i in range(NTT):
                if i >= 2:
                    sy.wait_ge(ctx.s("tp" + tag), tp0 + i - 1)
                ld.issue(i, sy.dma_start(out=xt[i % 2][:], in_=src_r[i]))
                if (i + 1) % GP == 0:
                    g = i // GP
                    sy.wait_ge(ctx.s("ev" + tag), ev0 + i + 1)
                    sy.wait_ge(ctx.s("es" + tag), es0 + i + 1)
                    for j in range(NB):
                        if dst32 is not None:
                            sg.issue(g, sy.dma_start(out=dst32[j, :, tsl(g, GP * 128)],
                                                     in_=g32[:, j, :]))
                        sg.issue(g, sy.dma_start(out=dst16[j, :, tsl(g, GP * 128)],
                                                 in_=g16[:, j, :]))
            return sg.final()

        def f_tensor(t):
            for i in range(NTT):
                ld.wait(t, i)
                if i >= 1:
                    t.wait_ge(ctx.s("ev" + tag), ev0 + i)
                    t.wait_ge(ctx.s("es" + tag), es0 + i)
                last = None
                for j in range(NB):
                    last = t.transpose(pst[:, tsl(j, 128)], xt[i % 2][:, tsl(j, 128)], idf[:])
                last.then_inc(ctx.s("tp" + tag), 1)
                ctx.inc("tp" + tag, 1)
            return ("tp" + tag, ctx.val("tp" + tag))

        def f_vector(v):
            for i in range(NTT):
                g, gi = i // GP, i % GP
                v.wait_ge(ctx.s("tp" + tag), tp0 + i + 1)
                if gi == 0 and g >= 1:
                    sg.wait(v, g - 1)
                v.tensor_copy(g32[:, :, tsl(gi, 128)],
                              pst.ap().rearrange("p (a t) -> p a t", a=NB)).then_inc(ctx.s("ev" + tag), 1)
                ctx.inc("ev" + tag, 1)
            return ("ev" + tag, ctx.val("ev" + tag))

        def f_scalar(s):
            for i in range(NTT):
                g, gi = i // GP, i % GP
                s.wait_ge(ctx.s("ev" + tag), ev0 + i + 1)
                if gi == 0 and g >= 1:
                    sg.wait(s, g - 1)
                s.copy(g16[:, :, tsl(gi, 128)], g32[:, :, tsl(gi, 128)]).then_inc(ctx.s("es" + tag), 1)
                ctx.inc("es" + tag, 1)
            return ("es" + tag, ctx.val("es" + tag))

        ph.sync(f_sync)
        ph.tensor(f_tensor)
        ph.vector(f_vector)
        ph.scalar(f_scalar)
        ph.emit()


def p_ln_t(ctx, src32, src16, gamma, beta, out32, out16, consts, tag,
           out16_sb=None, shift=False, eps=EPS):
    nc = ctx.nc
    NT = ctx.NT
    sq = ctx.sem("sq" + tag)
    gz = ctx.sem("gz" + tag)
    mm = ctx.sem("mm" + tag)
    vs = ctx.sem("vs" + tag)
    rs = ctx.sem("rs" + tag)
    ap = ctx.sem("ap" + tag)
    sq0 = ctx.val("sq" + tag)
    gz0 = ctx.val("gz" + tag)
    mm0 = ctx.val("mm" + tag)
    vs0 = ctx.val("vs" + tag)
    rs0 = ctx.val("rs" + tag)
    ap0 = ctx.val("ap" + tag)
    onesb = consts["onesb"]
    with contextlib.ExitStack() as es:
        s32 = [es.enter_context(nc.sbuf_tensor(f"s32{tag}{i}", [128, NB, 512], F32)) for i in range(2)]
        s16 = [es.enter_context(nc.sbuf_tensor(f"s16{tag}{i}", [128, NB, 512], BF16)) for i in range(2)]
        sqb = [es.enter_context(nc.sbuf_tensor(f"sqb{tag}{i}", [128, NB, 512], BF16)) for i in range(2)]
        stat = [es.enter_context(nc.sbuf_tensor(f"stt{tag}{i}", [128, 3, 512], F32)) for i in range(2)]
        o32 = s32
        o16 = [es.enter_context(nc.sbuf_tensor(f"o16{tag}{i}", [128, NB, 512], BF16)) for i in range(2)]
        psA = es.enter_context(nc.psum_tensor(f"pA{tag}", [128, 512], F32))
        psB = es.enter_context(nc.psum_tensor(f"pB{tag}", [128, 512], F32))
        ld = Rot(ctx, "ld" + tag, 2, n_per=2 * NB)
        so = Rot(ctx, "so" + tag, 2, n_per=NB * ((1 if out32 is not None else 0) + (1 if out16 is not None else 0)))
        ph = ctx.phase("lnt" + tag)

        def f_sync(sy):
            def emit_stores(i):
                sy.wait_ge(ctx.s("ap" + tag), ap0 + i + 1)
                for j in range(NB):
                    if out32 is not None:
                        so.issue(i, sy.dma_start(out=out32[j, :, tsl(i, 512)], in_=o32[i % 2][:, j, :]))
                    if out16 is not None:
                        so.issue(i, sy.dma_start(out=out16[j, :, tsl(i, 512)], in_=o16[i % 2][:, j, :]))

            has_out = out32 is not None or out16 is not None
            for i in range(NT):
                b = i % 2
                if i >= 2:
                    sy.wait_ge(ctx.s("ap" + tag), ap0 + i - 1)
                for j in range(NB):
                    sh = shift and j >= NB // 2
                    if sh and i == 0:
                        l1 = sy.dma_start(out=s32[b][:, j, 1:512], in_=src32[j, :, 0:511])
                        l2 = sy.dma_start(out=s16[b][:, j, 1:512], in_=src16[j, :, 0:511])
                    elif sh:
                        l1 = sy.dma_start(out=s32[b][:, j, :], in_=src32[j, :, i * 512 - 1: i * 512 + 511])
                        l2 = sy.dma_start(out=s16[b][:, j, :], in_=src16[j, :, i * 512 - 1: i * 512 + 511])
                    else:
                        l1 = sy.dma_start(out=s32[b][:, j, :], in_=src32[j, :, tsl(i, 512)])
                        l2 = sy.dma_start(out=s16[b][:, j, :], in_=src16[j, :, tsl(i, 512)])
                    ld.issue(i, l1)
                    ld.issue(i, l2)
                if has_out and i >= 1:
                    emit_stores(i - 1)
            if has_out:
                emit_stores(NT - 1)
            f = so.final()
            return f if f else ld.final()

        def f_gpsimd(g):
            if shift:
                g.memset(s32[0][:, NB // 2:, 0:1], 0.0)
                g.memset(s16[0][:, NB // 2:, 0:1], 0.0)
            g.sem_inc(ctx.s("gz" + tag), 1)
            ctx.inc("gz" + tag, 1)
            return ("gz" + tag, ctx.val("gz" + tag))

        def f_scalar(s):
            s.wait_ge(ctx.s("gz" + tag), gz0 + 1)

            def lnexp(i):
                s.wait_ge(ctx.s("vs" + tag), vs0 + i + 1)
                s.activation(stat[i % 2][:, 2, :], stat[i % 2][:, 1, :], AF.Ln, bias=consts["scal"][:, 0:1])
                s.activation(stat[i % 2][:, 2, :], stat[i % 2][:, 2, :], AF.Exp,
                             scale=-0.5).then_inc(ctx.s("rs" + tag), 1)
                ctx.inc("rs" + tag, 1)

            for i in range(NT):
                if i >= 1:
                    lnexp(i - 1)
                ld.wait(s, i)
                if i >= 2:
                    s.wait_ge(ctx.s("mm" + tag), mm0 + i - 1)
                s.activation(sqb[i % 2][:].rearrange("p a t -> p (a t)"),
                             s16[i % 2][:].rearrange("p a t -> p (a t)"),
                             AF.Square).then_inc(ctx.s("sq" + tag), 1)
                ctx.inc("sq" + tag, 1)
            lnexp(NT - 1)
            return ("rs" + tag, ctx.val("rs" + tag))

        def _dead_scalar(s):
            for i in range(NT):
                s.wait_ge(ctx.s("vs" + tag), vs0 + i + 1)
                s.activation(stat[i % 2][:, 2, :], stat[i % 2][:, 1, :], AF.Ln, bias=consts["scal"][:, 0:1])
                s.activation(stat[i % 2][:, 2, :], stat[i % 2][:, 2, :], AF.Exp,
                             scale=-0.5).then_inc(ctx.s("rs" + tag), 1)
                ctx.inc("rs" + tag, 1)
            return ("rs" + tag, ctx.val("rs" + tag))

        def f_tensor(t):
            for i in range(NT):
                ld.wait(t, i)
                if i >= 1:
                    t.wait_ge(ctx.s("vs" + tag), vs0 + i)
                for j in range(NB):
                    t.matmul(psA[:], onesb[:], s16[i % 2][:, j, :], start=(j == 0), stop=(j == NB - 1))
                if i == 0:
                    t.wait_ge(ctx.s("gz" + tag), gz0 + 1)
                t.wait_ge(ctx.s("sq" + tag), sq0 + i + 1)
                last = None
                for j in range(NB):
                    last = t.matmul(psB[:], onesb[:], sqb[i % 2][:, j, :], start=(j == 0), stop=(j == NB - 1))
                last.then_inc(ctx.s("mm" + tag), 1)
                ctx.inc("mm" + tag, 1)
            return ("mm" + tag, ctx.val("mm" + tag))

        def f_vector(v):
            inv = 1.0 / D
            for i in range(NT):
                b = i % 2
                v.wait_ge(ctx.s("mm" + tag), mm0 + i + 1)
                v.tensor_scalar(stat[b][:, 0, :], psA[:], inv, None, ALU.mult)
                v.tensor_tensor(stat[b][:, 1, :], stat[b][:, 0, :], stat[b][:, 0, :], ALU.mult)
                v.scalar_tensor_tensor(stat[b][:, 1, :], psB[:], inv, stat[b][:, 1, :],
                                       ALU.mult, ALU.subtract).then_inc(ctx.s("vs" + tag), 1)
                ctx.inc("vs" + tag, 1)
                v.wait_ge(ctx.s("rs" + tag), rs0 + i + 1)
                if i >= 2 and so.n_per > 0:
                    so.wait(v, i - 2)
                last = None
                for j in range(NB):
                    v.tensor_tensor(o32[b][:, j, :], s32[b][:, j, :], stat[b][:, 0, :], ALU.subtract)
                    if gamma is not None:
                        v.scalar_tensor_tensor(o32[b][:, j, :], o32[b][:, j, :], gamma[:, j:j + 1],
                                               stat[b][:, 2, :], ALU.mult, ALU.mult)
                        last = v.tensor_scalar(o32[b][:, j, :], o32[b][:, j, :], beta[:, j:j + 1], None, ALU.add)
                    else:
                        last = v.tensor_tensor(o32[b][:, j, :], o32[b][:, j, :], stat[b][:, 2, :], ALU.mult)
                    dst16 = out16_sb[:, j, tsl(i, 512)] if out16_sb is not None else (
                        o16[b][:, j, :] if out16 is not None else None)
                    if dst16 is not None:
                        last = v.tensor_copy(dst16, o32[b][:, j, :])
                last.then_inc(ctx.s("ap" + tag), 1)
                ctx.inc("ap" + tag, 1)
            return ("ap" + tag, ctx.val("ap" + tag))

        ph.sync(f_sync)
        ph.gpsimd(f_gpsimd)
        ph.scalar(f_scalar)
        ph.tensor(f_tensor)
        ph.vector(f_vector)
        ph.emit()


def p_attn_qkv(ctx, wsrc, bqkv_sb, xn2b, QT, KT, VAC, consts, tag="aq"):
    nc = ctx.nc
    NT = ctx.NT
    NTT = ctx.NTT
    mm = ctx.sem("mm" + tag)
    ev = ctx.sem("ev" + tag)
    vv = ctx.sem("vv" + tag)
    bb = ctx.sem("bb" + tag)
    mm0 = ctx.val("mm" + tag)
    ev0 = ctx.val("ev" + tag)
    vv0 = ctx.val("vv" + tag)
    bb0 = ctx.val("bb" + tag)
    wsrc_r = wsrc.rearrange("(a p) c -> a p c", p=128)
    NQK = 16
    with contextlib.ExitStack() as es:
        wq = es.enter_context(nc.sbuf_tensor("wq" + tag, [128, NB, 3072], BF16))
        qko = [es.enter_context(nc.sbuf_tensor(f"qko{tag}{i}", [128, 512], BF16)) for i in range(4)]
        vb_hi = es.enter_context(nc.sbuf_tensor("vbh" + tag, [1, 1024], BF16))
        vb_lo = es.enter_context(nc.sbuf_tensor("vbl" + tag, [1, 1024], BF16))
        vb_bc = es.enter_context(nc.sbuf_tensor("vbc" + tag, [128, 1024], F32))
        vaug = [es.enter_context(nc.sbuf_tensor(f"va{tag}{i}", [128, 2048], BF16)) for i in range(2)]
        psq = [es.enter_context(nc.psum_tensor(f"pq{tag}{i}", [128, 512], F32)) for i in range(4)]
        psb = es.enter_context(nc.psum_tensor(f"pb{tag}", [128, 1024], F32))
        onesb = consts["onesb"]
        wl = Rot(ctx, "wl" + tag, 1, n_per=NB, swdge=True)
        sq_ = Rot(ctx, "sq" + tag, 4)
        sv = Rot(ctx, "sv" + tag, 2)
        ph = ctx.phase("aqkv" + tag)

        def f_gpsimd(g):
            for j in range(NB):
                wl.issue(0, g.dma_start(out=wq[:, j, :], in_=wsrc_r[j]))
            last = None
            for i in range(2):
                last = g.memset(vaug[i].ap().rearrange("p (h c) -> p h c", c=128)[:, :, 64:128], 1.0)
            last.then_inc(ctx.s("bb" + tag), 1)
            ctx.inc("bb" + tag, 1)
            return ("bb" + tag, ctx.val("bb" + tag))

        def f_vector(v):
            row = consts["qkvb_row"]
            v.tensor_copy(vb_hi[:], row[0:1, 2048:3072])
            v.tensor_tensor(vb_lo[:], row[0:1, 2048:3072], vb_hi[:],
                            ALU.subtract).then_inc(ctx.s("bb" + tag), 1)
            ctx.inc("bb" + tag, 1)
            v.wait_ge(ctx.s("mm" + tag), mm0 + 1)
            v.tensor_copy(vb_bc[:], psb[:]).then_inc(ctx.s("bb" + tag), 1)
            ctx.inc("bb" + tag, 1)
            for i in range(NTT):
                b = i % 2
                if i >= 2:
                    sv.wait(v, i - 2)
                for vc in range(2):
                    v.wait_ge(ctx.s("mm" + tag), mm0 + 1 + NQK * NT + i * 2 + vc + 1)
                    last = v.tensor_tensor(
                        vaug[b].ap().rearrange("p (h c) -> p h c", c=128)[:, tsl(vc, 8), 0:64],
                        psq[2 + vc][:].rearrange("p (h c) -> p h c", c=64),
                        vb_bc[:, tsl(vc, 512)].rearrange("p (h c) -> p h c", c=64),
                        ALU.add)
                last.then_inc(ctx.s("vv" + tag), 1)
                ctx.inc("vv" + tag, 1)
            return ("vv" + tag, ctx.val("vv" + tag))

        def f_tensor(t):
            wl.wait(t, 0)
            t.wait_ge(ctx.s("bb" + tag), bb0 + 2)
            for vc in range(2):
                t.matmul(psb[:, tsl(vc, 512)], onesb[0:1, :], vb_hi[0:1, tsl(vc, 512)], start=True, stop=False)
                last = t.matmul(psb[:, tsl(vc, 512)], onesb[0:1, :], vb_lo[0:1, tsl(vc, 512)], start=False, stop=True)
            last.then_inc(ctx.s("mm" + tag), 1)
            ctx.inc("mm" + tag, 1)
            for ct in range(NQK):
                for tt in range(NT):
                    n = ct * NT + tt
                    k = n % 4
                    if n >= 4:
                        t.wait_ge(ctx.s("ev" + tag), ev0 + n - 3)
                    last = None
                    for j in range(NB):
                        last = t.matmul(psq[k][:], wq[:, j, tsl(ct, 128)], xn2b[:, j, tsl(tt, 512)],
                                        start=(j == 0), stop=(j == NB - 1))
                    last.then_inc(ctx.s("mm" + tag), 1)
                    ctx.inc("mm" + tag, 1)
            t.wait_ge(ctx.s("ev" + tag), ev0 + NQK * NT)
            for i in range(NTT):
                for vc in range(2):
                    if i >= 1:
                        t.wait_ge(ctx.s("vv" + tag), vv0 + i)
                    last = None
                    for j in range(NB):
                        last = t.matmul(psq[2 + vc][:], xn2b[:, j, tsl(i, 128)],
                                        wq[:, j, 2048 + vc * 512: 2560 + vc * 512],
                                        start=(j == 0), stop=(j == NB - 1))
                    last.then_inc(ctx.s("mm" + tag), 1)
                    ctx.inc("mm" + tag, 1)
            return ("mm" + tag, ctx.val("mm" + tag))

        def f_scalar(s):
            for ct in range(NQK):
                for tt in range(NT):
                    n = ct * NT + tt
                    k = n % 4
                    s.wait_ge(ctx.s("mm" + tag), mm0 + 1 + n + 1)
                    if n >= 4:
                        sq_.wait(s, n - 4)
                    s.activation(qko[k][:], psq[k][:], AF.Identity,
                                 bias=bqkv_sb[:, ct:ct + 1]).then_inc(ctx.s("ev" + tag), 1)
                    ctx.inc("ev" + tag, 1)
            return ("ev" + tag, ctx.val("ev" + tag))

        def f_sync(sy):
            for ct in range(NQK):
                for tt in range(NT):
                    n = ct * NT + tt
                    k = n % 4
                    sy.wait_ge(ctx.s("ev" + tag), ev0 + n + 1)
                    dst = QT if ct < 8 else KT
                    sq_.issue(n, sy.dma_start(out=dst[ct % 8, :, tsl(tt, 512)], in_=qko[k][:]))
            for i in range(NTT):
                sy.wait_ge(ctx.s("vv" + tag), vv0 + i + 1)
                sv.issue(i, sy.dma_start(out=VAC[i], in_=vaug[i % 2][:]))
            return sq_.final() + sv.final()

        ph.gpsimd(f_gpsimd)
        ph.vector(f_vector)
        ph.tensor(f_tensor)
        ph.scalar(f_scalar)
        ph.sync(f_sync)
        ph.emit()


def p_causal_attn(ctx, QT, KT, VAC, OTC, consts, tag="ca"):
    nc = ctx.nc
    NT = ctx.NT
    sA = ctx.sem("sA" + tag)
    xp = ctx.sem("xp" + tag)
    pv = ctx.sem("pv" + tag)
    dv = ctx.sem("dv" + tag)
    mk = ctx.sem("mk" + tag)
    sA0 = ctx.val("sA" + tag)
    xp0g = ctx.val("xp" + tag)
    pv0g = ctx.val("pv" + tag)
    dv0 = ctx.val("dv" + tag)
    mk0g = ctx.val("mk" + tag)
    cmask_d = consts["cmask_d"]
    with contextlib.ExitStack() as es:
        cmask = es.enter_context(nc.sbuf_tensor("cmsk" + tag, [128, 4, 512], F32))
        qtile = [es.enter_context(nc.sbuf_tensor(f"qt{tag}{i}", [128, 512], BF16)) for i in range(2)]
        ktile = [es.enter_context(nc.sbuf_tensor(f"kt{tag}{i}", [128, ctx.S], BF16)) for i in range(2)]
        vtile = [es.enter_context(nc.sbuf_tensor(f"vt{tag}{i}", [128, (ctx.S // 128) * 256], BF16)) for i in range(2)]
        pT = [es.enter_context(nc.sbuf_tensor(f"pT{tag}{i}", [128, 512], BF16)) for i in range(4)]
        rl = es.enter_context(nc.sbuf_tensor(f"rl{tag}", [64, 512], F32))
        otile = [es.enter_context(nc.sbuf_tensor(f"ot{tag}{i}", [128, 512], BF16)) for i in range(2)]
        psS = [es.enter_context(nc.psum_tensor(f"pS{tag}{i}", [128, 512], F32)) for i in range(4)]
        psO = [es.enter_context(nc.psum_tensor(f"pO{tag}{i}", [128, 512], F32)) for i in range(2)]
        nks_plan = [4 * qt + 4 for qt in range(NT) for hp in range(NB)]
        nks_plan = [4 * qt + 4 for qt in range(NT) for _ in range(NB)]
        ldq = Rot(ctx, "lq" + tag, 2)
        ldk = Rot(ctx, "lk" + tag, 2)
        ldv = Rot(ctx, "lv" + tag, 2, plan=nks_plan)
        lmk = Rot(ctx, "lm" + tag, 1)
        so = Rot(ctx, "so" + tag, 2)
        ph = ctx.phase("causal" + tag)
        HP = NB
        units = [(hp, qt) for qt in range(NT) for hp in range(HP)]
        sched = []
        sA_c, xp_c, pv_c, mk_c = sA0, xp0g, pv0g, mk0g
        for u, (hp, qt) in enumerate(units):
            nk = 4 * qt + 4
            sched.append({"hp": hp, "qt": qt, "nk": nk, "sA0": sA_c, "xp0": xp_c,
                          "pv0": pv_c, "mk0": mk_c})
            sA_c += nk
            xp_c += 2 * nk
            pv_c += nk
            mk_c += 4

        def f_sync(sy):
            lmk.issue(0, sy.dma_start(out=cmask[:], in_=cmask_d))
            for u, s_ in enumerate(sched):
                hp, qt, nk = s_["hp"], s_["qt"], s_["nk"]
                b = u % 2
                if u >= 2:
                    sy.wait_ge(ctx.s("dv" + tag), dv0 + u - 1)
                ldq.issue(u, sy.dma_start(out=qtile[b][:], in_=QT[hp, :, tsl(qt, 512)]))
                ldk.issue(u, sy.dma_start(out=ktile[b][:, 0:nk * 128], in_=KT[hp, :, 0:nk * 128]))
                for kt in range(nk):
                    ldv.issue(u, sy.dma_start(out=vtile[b][:, tsl(kt, 256)],
                                              in_=VAC[kt, :, hp * 256:(hp + 1) * 256]))
                sy.wait_ge(ctx.s("dv" + tag), dv0 + u + 1)
                so.issue(u, sy.dma_start(out=OTC[hp, :, tsl(qt, 512)], in_=otile[b][:]))
            return so.final()

        def f_tensor(t):
            for u, s_ in enumerate(sched):
                hp, qt, nk = s_["hp"], s_["qt"], s_["nk"]
                b = u % 2
                ldq.wait(t, u)
                ldk.wait(t, u)
                ldv.wait(t, u)
                if u >= 1:
                    t.wait_ge(ctx.s("dv" + tag), dv0 + u)
                    t.wait_ge(ctx.s("xp" + tag), s_["xp0"])

                def S_(kt):
                    kb_ = kt % 2
                    if kt >= 2:
                        t.wait_ge(ctx.s("xp" + tag), s_["xp0"] + 2 * (kt - 1))
                    t.matmul(psS[kb_][:], ktile[b][0:64, tsl(kt, 128)], qtile[b][0:64, :],
                             start=True, stop=True, tile_position=(0, 0))
                    t.matmul(psS[2 + kb_][:], ktile[b][64:128, tsl(kt, 128)], qtile[b][64:128, :],
                             start=True, stop=True, tile_position=(64, 0)).then_inc(ctx.s("sA" + tag), 1)
                    ctx.inc("sA" + tag, 1)

                def PV_(kt):
                    kb_ = kt % 2
                    t.wait_ge(ctx.s("xp" + tag), s_["xp0"] + 2 * (kt + 1))
                    t.matmul(psO[0][:], vtile[b][:, tsl(kt, 256)][:, 0:128], pT[2 * kb_][:],
                             start=(kt == 0), stop=(kt == nk - 1))
                    t.matmul(psO[1][:], vtile[b][:, tsl(kt, 256)][:, 128:256], pT[2 * kb_ + 1][:],
                             start=(kt == 0), stop=(kt == nk - 1)).then_inc(ctx.s("pv" + tag), 1)
                    ctx.inc("pv" + tag, 1)

                S_(0)
                for kt in range(1, nk):
                    S_(kt)
                    PV_(kt - 1)
                PV_(nk - 1)
            return ("pv" + tag, ctx.val("pv" + tag))

        def f_vector(v):
            lmk.wait(v, 0)
            for u, s_ in enumerate(sched):
                hp, qt, nk = s_["hp"], s_["qt"], s_["nk"]
                for kt in range(nk - 4, nk):
                    r = kt - (nk - 4)
                    kb_ = kt % 2
                    v.wait_ge(ctx.s("sA" + tag), s_["sA0"] + kt + 1)
                    v.tensor_tensor(psS[kb_][:], psS[kb_][:], cmask[:, r, :], ALU.add)
                    v.tensor_tensor(psS[2 + kb_][:], psS[2 + kb_][:], cmask[:, r, :],
                                    ALU.add).then_inc(ctx.s("mk" + tag), 1)
                    ctx.inc("mk" + tag, 1)
                if u >= 2:
                    so.wait(v, u - 2)
                v.wait_ge(ctx.s("pv" + tag), s_["pv0"] + nk)
                v.reciprocal(rl[:], psO[0][64:128, :])
                v.tensor_tensor(otile[u % 2][0:64, :], psO[0][0:64, :], rl[:], ALU.mult)
                v.reciprocal(rl[:], psO[1][64:128, :])
                v.tensor_tensor(otile[u % 2][64:128, :], psO[1][0:64, :], rl[:],
                                ALU.mult).then_inc(ctx.s("dv" + tag), 1)
                ctx.inc("dv" + tag, 1)
            return ("dv" + tag, ctx.val("dv" + tag))

        def f_scalar(s):
            for u, s_ in enumerate(sched):
                hp, qt, nk = s_["hp"], s_["qt"], s_["nk"]
                if u >= 1:
                    s.wait_ge(ctx.s("pv" + tag), s_["pv0"])
                for kt in range(nk):
                    kb_ = kt % 2
                    if kt >= nk - 4:
                        r = kt - (nk - 4)
                        s.wait_ge(ctx.s("mk" + tag), s_["mk0"] + r + 1)
                    else:
                        s.wait_ge(ctx.s("sA" + tag), s_["sA0"] + kt + 1)
                    if kt >= 2:
                        s.wait_ge(ctx.s("pv" + tag), s_["pv0"] + kt - 1)
                    s.activation(pT[2 * kb_][:], psS[kb_][:], AF.Exp, scale=0.125)
                    s.activation(pT[2 * kb_ + 1][:], psS[2 + kb_][:], AF.Exp,
                                 scale=0.125).then_inc(ctx.s("xp" + tag), 2)
                    ctx.inc("xp" + tag, 2)
            return ("xp" + tag, ctx.val("xp" + tag))

        ph.sync(f_sync)
        ph.tensor(f_tensor)
        ph.vector(f_vector)
        ph.scalar(f_scalar)
        ph.emit()


def p_proj_t(ctx, wsrc, rhs16, bias_sb, extra16, res32, out32, out16, consts,
             tag, nbin=NB, nbout=NB, act=None):
    nc = ctx.nc
    NT = ctx.NT
    mm = ctx.sem("mm" + tag)
    ev = ctx.sem("ev" + tag)
    vv = ctx.sem("vv" + tag)
    mm0 = ctx.val("mm" + tag)
    ev0 = ctx.val("ev" + tag)
    vv0 = ctx.val("vv" + tag)
    wsrc_r = wsrc.rearrange("(a p) c -> a p c", p=128)
    Dout = nbout * 128
    stream_rhs = nbin > NB
    with contextlib.ExitStack() as es:
        wsb = es.enter_context(nc.sbuf_tensor("w" + tag, [128, nbin, Dout], BF16))
        if stream_rhs:
            rsb = [es.enter_context(nc.sbuf_tensor(f"r{tag}{i}", [128, nbin, 512], BF16)) for i in range(2)]
        else:
            rsb = es.enter_context(nc.sbuf_tensor("r" + tag, [128, nbin, ctx.S], BF16))
        a32 = [es.enter_context(nc.sbuf_tensor(f"a{tag}{i}", [128, 512], F32)) for i in range(2)]
        e16 = [es.enter_context(nc.sbuf_tensor(f"e{tag}{i}", [128, 512], BF16)) for i in range(2)]
        o32 = [es.enter_context(nc.sbuf_tensor(f"o{tag}{i}", [128, 512], F32)) for i in range(2)]
        o16 = [es.enter_context(nc.sbuf_tensor(f"f{tag}{i}", [128, 512], BF16)) for i in range(2)]
        r32 = [es.enter_context(nc.sbuf_tensor(f"g{tag}{i}", [128, 512], F32)) for i in range(2)]
        ps = [es.enter_context(nc.psum_tensor(f"p{tag}{i}", [128, 512], F32)) for i in range(2)]
        wl = Rot(ctx, "wl" + tag, 1, n_per=nbin, swdge=True)
        rl_ = (Rot(ctx, "rl" + tag, 2, n_per=nbin) if stream_rhs
               else Rot(ctx, "rl" + tag, nbin))
        rr = Rot(ctx, "rr" + tag, 2)
        se = Rot(ctx, "se" + tag, 2)
        sv_ = Rot(ctx, "sv" + tag, 2,
                  n_per=(1 if out32 is not None else 0) + (1 if out16 is not None else 0))
        ph = ctx.phase("proj" + tag)
        if stream_rhs:
            units = [(tt, co) for tt in range(NT) for co in range(nbout)]
        else:
            units = [(tt, co) for co in range(nbout) for tt in range(NT)]
        NU = len(units)

        def f_gpsimd(g):
            for j in range(nbin):
                wl.issue(0, g.dma_start(out=(wsb[:, j, :]), in_=wsrc_r[j]))
            return wl.final()[-1]

        def f_sync(sy):
            if not stream_rhs:
                for j in range(nbin):
                    rl_.issue(j, sy.dma_start(out=rsb[:, j, :], in_=rhs16[j]))
            for u, (tt, co) in enumerate(units):
                b = u % 2
                if stream_rhs and co == 0:
                    ti = tt
                    if ti >= 2:
                        sy.wait_ge(ctx.s("mm" + tag), mm0 + (ti - 1) * nbout)
                    for j in range(nbin):
                        rl_.issue(ti, sy.dma_start(out=rsb[ti % 2][:, j, :],
                                                   in_=rhs16[j, :, tsl(ti, 512)]))
                if res32 is not None:
                    if u >= 2:
                        sy.wait_ge(ctx.s("vv" + tag), vv0 + u - 1)
                    rr.issue(u, sy.dma_start(out=r32[b][:], in_=res32[co, :, tsl(tt, 512)]))
                if extra16 is not None:
                    sy.wait_ge(ctx.s("ev" + tag), ev0 + 2 * (u + 1))
                    se.issue(u, sy.dma_start(out=extra16[co, :, tsl(tt, 512)], in_=e16[b][:]))
                if out32 is not None or out16 is not None:
                    sy.wait_ge(ctx.s("vv" + tag), vv0 + u + 1)
                    if out32 is not None:
                        sv_.issue(u, sy.dma_start(out=out32[co, :, tsl(tt, 512)], in_=o32[b][:]))
                    if out16 is not None:
                        sv_.issue(u, sy.dma_start(out=out16[co, :, tsl(tt, 512)], in_=o16[b][:]))
            return sv_.final() + se.final()

        def f_tensor(t):
            wl.wait(t, 0)
            if not stream_rhs:
                for j in range(nbin):
                    rl_.wait(t, j)
            for u, (tt, co) in enumerate(units):
                b = u % 2
                if stream_rhs:
                    rl_.wait(t, tt)
                if u >= 2:
                    t.wait_ge(ctx.s("ev" + tag), ev0 + 2 * (u - 1))
                last = None
                for j in range(nbin):
                    rt = rsb[tt % 2][:, j, :] if stream_rhs else rsb[:, j, tsl(tt, 512)]
                    last = t.matmul(ps[b][:], wsb[:, j, tsl(co, 128)], rt,
                                    start=(j == 0), stop=(j == nbin - 1))
                last.then_inc(ctx.s("mm" + tag), 1)
                ctx.inc("mm" + tag, 1)
            return ("mm" + tag, ctx.val("mm" + tag))

        def f_scalar(s):
            for u, (tt, co) in enumerate(units):
                b = u % 2
                s.wait_ge(ctx.s("mm" + tag), mm0 + u + 1)
                if u >= 2:
                    s.wait_ge(ctx.s("vv" + tag), vv0 + u - 1)
                    if extra16 is not None:
                        se.wait(s, u - 2)
                bias = bias_sb[:, co:co + 1] if bias_sb is not None else consts["scal"][:, 3:4]
                fn = act if act is not None else AF.Identity
                last = s.activation(a32[b][:], ps[b][:], fn, bias=bias)
                if extra16 is not None:
                    last = s.copy(e16[b][:], a32[b][:])
                last.then_inc(ctx.s("ev" + tag), 2)
                ctx.inc("ev" + tag, 2)
            return ("ev" + tag, ctx.val("ev" + tag))

        def f_vector(v):
            for u, (tt, co) in enumerate(units):
                b = u % 2
                v.wait_ge(ctx.s("ev" + tag), ev0 + 2 * u + 2)
                if u >= 2:
                    sv_.wait(v, u - 2)
                if res32 is not None:
                    rr.wait(v, u)
                    last = v.tensor_tensor(o32[b][:], a32[b][:], r32[b][:], ALU.add)
                else:
                    last = v.tensor_copy(o32[b][:], a32[b][:])
                if out16 is not None:
                    last = v.tensor_copy(o16[b][:], o32[b][:])
                last.then_inc(ctx.s("vv" + tag), 1)
                ctx.inc("vv" + tag, 1)
            return ("vv" + tag, ctx.val("vv" + tag))

        ph.gpsimd(f_gpsimd)
        ph.sync(f_sync)
        ph.tensor(f_tensor)
        ph.scalar(f_scalar)
        ph.vector(f_vector)
        ph.emit()


def p_scan(ctx, wctx_d, wx_d, binl_sb, CTXB, XS0_32, XS0_16, Y1_32,
           Y2_32, Y2_16, consts, tag="sc"):
    nc = ctx.nc
    NT = ctx.NT
    NIT = 8
    for n in ["cm", "um", "va", "th", "gp", "vu", "xc", "fi"]:
        ctx.sem(n + tag)
    b0 = {n: ctx.val(n + tag) for n in ["cm", "um", "va", "th", "gp", "vu", "xc", "fi"]}
    PER = {"um": 2 * NIT, "va": 2 * NIT, "th": 2 * NIT, "gp": 1,
           "vu": NIT, "xc": NIT + 1, "cm": 1, "fi": 1}

    def base(n, i):
        return b0[n] + PER[n] * i

    wctx_r = wctx_d.rearrange("(a p) c -> a p c", p=128)
    wx_r = wx_d.rearrange("(a p) c -> a p c", p=128)
    flat = lambda ap: ap.rearrange("p a t -> p (a t)")
    with contextlib.ExitStack() as es:
        wctx = es.enter_context(nc.sbuf_tensor("wc" + tag, [128, NB, D], BF16))
        wx = es.enter_context(nc.sbuf_tensor("wx" + tag, [128, NB, D], BF16))
        ctxb = [es.enter_context(nc.sbuf_tensor(f"cb{tag}{i}", [128, NB, 512], BF16)) for i in range(2)]
        c0 = es.enter_context(nc.sbuf_tensor("c0" + tag, [128, NB, 512], F32))
        xs = es.enter_context(nc.sbuf_tensor("xs" + tag, [128, NB, 512], F32))
        vs = es.enter_context(nc.sbuf_tensor("vs" + tag, [128, NB, 512], F32))
        tmp = es.enter_context(nc.sbuf_tensor("tp" + tag, [128, NB, 512], F32))
        u32 = es.enter_context(nc.sbuf_tensor("u3" + tag, [128, NB, 512], F32))
        xsb = es.enter_context(nc.sbuf_tensor("xb" + tag, [128, NB, 512], BF16))
        up = es.enter_context(nc.psum_tensor("up" + tag, [128, NB, 512], F32))
        wload = Rot(ctx, "wv" + tag, 1, n_per=2 * NB, swdge=True)
        lc = Rot(ctx, "lc" + tag, 2, n_per=NB)
        lx = Rot(ctx, "lx" + tag, 1, n_per=2 * NB)
        ly = Rot(ctx, "ly" + tag, 1, n_per=NB)
        st = Rot(ctx, "st" + tag, 1, n_per=2 * NB)
        ph = ctx.phase("scan" + tag)
        H4 = lambda h: slice(h * 4, h * 4 + 4)

        def f_gpsimd(g):
            for j in range(NB):
                wload.issue(0, g.dma_start(out=wctx[:, j, :], in_=wctx_r[j]))
            for j in range(NB):
                wload.issue(0, g.dma_start(out=wx[:, j, :], in_=wx_r[j]))
            for i in range(NT):
                lx.wait(g, i)
                g.memset(flat(vs[:]), 0.0).then_inc(ctx.s("gp" + tag), 1)
                ctx.inc("gp" + tag, 1)
            return ("gp" + tag, ctx.val("gp" + tag))

        def f_sync(sy):
            for i in range(NT):
                if i >= 1:
                    st.wait(sy, i - 1)
                if i >= 2:
                    sy.wait_ge(ctx.s("cm" + tag), base("cm", i - 1) + 1)
                for j in range(NB):
                    lc.issue(i, sy.dma_start(out=ctxb[i % 2][:, j, :], in_=CTXB[j, :, tsl(i, 512)]))
                for j in range(NB):
                    lx.issue(i, sy.dma_start(out=xs[:, j, :], in_=XS0_32[j, :, tsl(i, 512)]))
                    lx.issue(i, sy.dma_start(out=xsb[:, j, :], in_=XS0_16[j, :, tsl(i, 512)]))
                sy.wait_ge(ctx.s("vu" + tag), base("vu", i) + NIT)
                for j in range(NB):
                    ly.issue(i, sy.dma_start(out=tmp[:, j, :], in_=Y1_32[j, :, tsl(i, 512)]))
                sy.wait_ge(ctx.s("fi" + tag), base("fi", i) + 1)
                sy.wait_ge(ctx.s("xc" + tag), base("xc", i) + NIT + 1)
                for j in range(NB):
                    st.issue(i, sy.dma_start(out=Y2_32[j, :, tsl(i, 512)], in_=u32[:, j, :]))
                    st.issue(i, sy.dma_start(out=Y2_16[j, :, tsl(i, 512)], in_=xsb[:, j, :]))
            return st.final()

        def f_tensor(t):
            wload.wait(t, 0)
            for i in range(NT):
                lx.wait(t, i)
                lc.wait(t, i)
                if i >= 1:
                    t.wait_ge(ctx.s("va" + tag), base("va", i))
                    st.wait(t, i - 1)
                for od in range(NB):
                    for j in range(NB):
                        last = t.matmul(up[:, od, :], wctx[:, j, tsl(od, 128)], ctxb[i % 2][:, j, :],
                                        start=(j == 0), stop=(j == NB - 1))
                last.then_inc(ctx.s("cm" + tag), 1)
                ctx.inc("cm" + tag, 1)
                for it in range(NIT):
                    t.wait_ge(ctx.s("xc" + tag), base("xc", i) + it + 1)
                    for half in range(2):
                        if it >= 1:
                            t.wait_ge(ctx.s("va" + tag), base("va", i) + 2 * (it - 1) + half + 1)
                        last = None
                        for od in range(4 * half, 4 * half + 4):
                            for j in range(NB):
                                last = t.matmul(up[:, od, :], wx[:, j, tsl(od, 128)], xsb[:, j, :],
                                                start=(j == 0), stop=(j == NB - 1))
                        last.then_inc(ctx.s("um" + tag), 1)
                        ctx.inc("um" + tag, 1)
            return ("um" + tag, ctx.val("um" + tag))

        def f_scalar(s):
            for i in range(NT):
                s.wait_ge(ctx.s("cm" + tag), base("cm", i) + 1)
                s.wait_ge(ctx.s("va" + tag), base("va", i))
                last = None
                for od in range(NB):
                    last = s.activation(c0[:, od, :], up[:, od, :], AF.Identity, bias=binl_sb[:, od:od + 1])
                last.then_inc(ctx.s("xc" + tag), 1)
                ctx.inc("xc" + tag, 1)
                for it in range(NIT):
                    for half in range(2):
                        s.wait_ge(ctx.s("va" + tag), base("va", i) + 2 * it + half + 1)
                        s.activation(flat(u32[:, H4(half), :]), flat(u32[:, H4(half), :]),
                                     AF.Tanh).then_inc(ctx.s("th" + tag), 1)
                        ctx.inc("th" + tag, 1)
                    if it < NIT - 1:
                        s.wait_ge(ctx.s("vu" + tag), base("vu", i) + it + 1)
                        s.copy(flat(xsb[:]), flat(xs[:])).then_inc(ctx.s("xc" + tag), 1)
                        ctx.inc("xc" + tag, 1)
                s.wait_ge(ctx.s("fi" + tag), base("fi", i) + 1)
                s.copy(flat(xsb[:]), flat(u32[:])).then_inc(ctx.s("xc" + tag), 1)
                ctx.inc("xc" + tag, 1)
            return ("xc" + tag, ctx.val("xc" + tag))

        def f_vector(v):
            for i in range(NT):
                for it in range(NIT):
                    for half in range(2):
                        v.wait_ge(ctx.s("um" + tag), base("um", i) + 2 * it + half + 1)
                        v.tensor_tensor(flat(u32[:, H4(half), :]), flat(up[:, H4(half), :]),
                                        flat(c0[:, H4(half), :]), ALU.add).then_inc(ctx.s("va" + tag), 1)
                        ctx.inc("va" + tag, 1)
                    if it == 0:
                        v.wait_ge(ctx.s("gp" + tag), base("gp", i) + 1)
                    v.wait_ge(ctx.s("th" + tag), base("th", i) + 2 * (it + 1))
                    v.tensor_tensor(flat(tmp[:]), flat(u32[:]), flat(xs[:]), ALU.subtract)
                    v.scalar_tensor_tensor(flat(vs[:]), flat(tmp[:]), 0.1 / 0.9 ** (it + 1),
                                           flat(vs[:]), ALU.mult, ALU.add)
                    v.scalar_tensor_tensor(flat(xs[:]), flat(vs[:]), 0.1 * 0.9 ** (it + 1),
                                           flat(xs[:]), ALU.mult, ALU.add).then_inc(ctx.s("vu" + tag), 1)
                    ctx.inc("vu" + tag, 1)
                ly.wait(v, i)
                v.tensor_tensor(flat(u32[:]), flat(xs[:]), flat(tmp[:]), ALU.add).then_inc(ctx.s("fi" + tag), 1)
                ctx.inc("fi" + tag, 1)
            return ("fi" + tag, ctx.val("fi" + tag))

        ph.gpsimd(f_gpsimd)
        ph.sync(f_sync)
        ph.tensor(f_tensor)
        ph.scalar(f_scalar)
        ph.vector(f_vector)
        ph.emit()


def p_ffn_qkv(ctx, wsrc, qbc_d, RQC, RQS, RKC, RKS,
              Z16, QTl, KTl, VAUGl, consts, tag):
    nc = ctx.nc
    NTT = ctx.NTT
    for n in ["mm", "s1", "va", "s2", "qe", "tp", "pe", "tb", "ve", "hi"]:
        ctx.sem(n + tag)
    b0 = {n: ctx.val(n + tag) for n in ["mm", "s1", "va", "s2", "qe", "tp", "pe", "tb", "ve", "hi"]}
    onesb = consts["onesb"]
    wsrc_r = wsrc.rearrange("(a p) c -> a p c", p=128)
    with contextlib.ExitStack() as es:
        wq = es.enter_context(nc.sbuf_tensor("wq" + tag, [128, NB, 3072], BF16))
        CSQ = es.enter_context(nc.sbuf_tensor("csq" + tag, [128, NTT, 64], F32))
        SSQ = es.enter_context(nc.sbuf_tensor("ssx" + tag, [128, NTT, 64], F32))
        CSK = es.enter_context(nc.sbuf_tensor("csk" + tag, [128, NTT, 64], F32))
        SSK = es.enter_context(nc.sbuf_tensor("ssk" + tag, [128, NTT, 64], F32))
        bt = [es.enter_context(nc.sbuf_tensor(f"bt{tag}{i}", [128, 64], F32)) for i in range(8)]
        qbc = es.enter_context(nc.sbuf_tensor("qb" + tag, [128, 4, 64], F32))
        zb = [es.enter_context(nc.sbuf_tensor(f"zb{tag}{i}", [128, NB, 130], BF16)) for i in range(2)]
        sqst = [es.enter_context(nc.sbuf_tensor(f"sg{tag}{i}", [128, 512], BF16)) for i in range(2)]
        ssA = [es.enter_context(nc.sbuf_tensor(f"sa{tag}{i}", [128, 8], F32)) for i in range(2)]
        ssB = [es.enter_context(nc.sbuf_tensor(f"sb{tag}{i}", [128, 8], F32)) for i in range(2)]
        rs_t = es.enter_context(nc.sbuf_tensor("rs" + tag, [128, 8], F32))
        st2 = es.enter_context(nc.sbuf_tensor("t2" + tag, [128, 512], F32))
        st3 = es.enter_context(nc.sbuf_tensor("t3" + tag, [128, 512], F32))
        q16 = [es.enter_context(nc.sbuf_tensor(f"q6{tag}{i}", [128, 512], BF16)) for i in range(2)]
        qts = [es.enter_context(nc.sbuf_tensor(f"qt{tag}{i}", [128, 512], BF16)) for i in range(2)]
        vaug = [es.enter_context(nc.sbuf_tensor(f"va{tag}{i}", [128, 2048], BF16)) for i in range(2)]
        pq = [es.enter_context(nc.psum_tensor(f"pq{tag}{i}", [128, 512], F32)) for i in range(3)]
        psT = [es.enter_context(nc.psum_tensor(f"pT{tag}{i}", [128, 512], BF16)) for i in range(2)]
        wl = Rot(ctx, "wl" + tag, 1, n_per=NB, swdge=True)
        lq = Rot(ctx, "lr" + tag, 1, n_per=1)
        lt = Rot(ctx, "lt" + tag, 2, n_per=4)
        lz = Rot(ctx, "lz" + tag, 2, n_per=NB)
        sv = Rot(ctx, "sw" + tag, 2)
        sq_ = Rot(ctx, "sy" + tag, 2, n_per=4)
        ph = ctx.phase("fqkv" + tag)
        idb = consts["idb"]
        hview = lambda ap: ap.rearrange("p (a b) -> p a b", b=64)

        def f_gpsimd(g):
            for j in range(NB):
                wl.issue(0, g.dma_start(out=wq[:, j, :], in_=wsrc_r[j]))
            for i in range(2):
                g.memset(vaug[i].ap().rearrange("p (h c) -> p h c", c=128)[:, :, 64:128], 1.0)
            g.memset(zb[0][:, :, 0:1], 0.0)
            g.memset(zb[1][:, :, 0:1], 0.0)
            g.sem_inc(ctx.s("hi" + tag), 1)
            ctx.inc("hi" + tag, 1)
            return ("hi" + tag, ctx.val("hi" + tag))

        def f_sync(sy):
            lq.issue(0, sy.dma_start(out=qbc[:], in_=qbc_d))
            for i in range(NTT):
                if i >= 2:
                    sy.wait_ge(ctx.s("tb" + tag), b0["tb"] + i - 1)
                for bi, R in enumerate([RQC, RQS, RKC, RKS]):
                    lt.issue(i, sy.dma_start(out=bt[4 * (i % 2) + bi][:], in_=R[i]))
            for i in range(NTT):
                b = i % 2
                if i >= 2:
                    sy.wait_ge(ctx.s("mm" + tag), b0["mm"] + 1 + 6 * (i - 1))
                for j in range(NB):
                    if i == 0:
                        lz.issue(i, sy.dma_start(out=zb[b][:, j, 1:129], in_=Z16[j, :, 0:128]))
                    else:
                        lz.issue(i, sy.dma_start(out=zb[b][:, j, 0:129], in_=Z16[j, :, i * 128 - 1: i * 128 + 128]))
                for cg in range(4):
                    n = i * 4 + cg
                    sy.wait_ge(ctx.s("pe" + tag), b0["pe"] + n + 1)
                    dst = QTl if cg < 2 else KTl
                    for c in range(4):
                        jj = 4 * (cg % 2) + c
                        sq_.issue(n, sy.dma_start(out=dst[jj, :, tsl(i, 128)], in_=qts[n % 2][:, tsl(c, 128)]))
                sy.wait_ge(ctx.s("ve" + tag), b0["ve"] + 2 * (i + 1))
                sv.issue(i, sy.dma_start(out=VAUGl[i], in_=vaug[b][:]))
            return sq_.final() + sv.final()

        def f_tensor(t):
            t.wait_ge(ctx.s("hi" + tag), b0["hi"] + 1)
            wl.wait(t, 0)
            t.wait_ge(ctx.s("tb" + tag), b0["tb"] + NTT)
            for i in range(NTT):
                b = i % 2
                lz.wait(t, i)

                def mm_cg(cg):
                    k = (i * 6 + cg) % 3
                    n_prev = i * 6 + cg - 3
                    if n_prev >= 0:
                        pi, pcg = divmod(n_prev, 6)
                        if pcg >= 4:
                            t.wait_ge(ctx.s("ve" + tag), b0["ve"] + 2 * pi + (pcg - 4) + 1)
                        else:
                            t.wait_ge(ctx.s("qe" + tag), b0["qe"] + pi * 4 + pcg + 1)
                    last = None
                    for j in range(NB):
                        o = 1 if j < 4 else 0
                        last = t.matmul(pq[k][:], zb[b][:, j, o:o + 128], wq[:, j, tsl(cg, 512)],
                                        start=(j == 0), stop=(j == NB - 1))
                    last.then_inc(ctx.s("mm" + tag), 1)
                    ctx.inc("mm" + tag, 1)

                def tr_cg(cg):
                    n = i * 4 + cg
                    t.wait_ge(ctx.s("qe" + tag), b0["qe"] + n + 1)
                    if n >= 2:
                        t.wait_ge(ctx.s("pe" + tag), b0["pe"] + n - 1)
                    last = None
                    for c in range(4):
                        last = t.transpose(psT[n % 2][:, tsl(c, 128)], q16[n % 2][:, tsl(c, 128)], idb[:])
                    last.then_inc(ctx.s("tp" + tag), 1)
                    ctx.inc("tp" + tag, 1)

                mm_cg(0); mm_cg(1); mm_cg(2)
                tr_cg(0); mm_cg(3)
                tr_cg(1); mm_cg(4)
                tr_cg(2); mm_cg(5)
                tr_cg(3)
            return ("tp" + tag, ctx.val("tp" + tag))

        def f_scalar(s):
            for i in range(NTT):
                for cg in range(4):
                    k = (i * 6 + cg) % 3
                    n = i * 4 + cg
                    s.wait_ge(ctx.s("mm" + tag), b0["mm"] + i * 6 + cg + 1)
                    if n >= 2:
                        s.wait_ge(ctx.s("va" + tag), b0["va"] + n - 1)
                    s.activation(sqst[n % 2][:], pq[k][:], AF.Square).then_inc(ctx.s("s1" + tag), 1)
                    ctx.inc("s1" + tag, 1)
                    s.wait_ge(ctx.s("va" + tag), b0["va"] + n + 1)
                    if n >= 2:
                        s.wait_ge(ctx.s("qe" + tag), b0["qe"] + n - 1)
                    s.activation(ssB[n % 2][:], ssA[n % 2][:], AF.Ln, bias=consts["scal"][:, 4:5])
                    s.activation(ssB[n % 2][:], ssB[n % 2][:], AF.Exp, scale=-0.5).then_inc(ctx.s("s2" + tag), 1)
                    ctx.inc("s2" + tag, 1)
            return ("s2" + tag, ctx.val("s2" + tag))

        def f_vector(v):
            lq.wait(v, 0)
            for i in range(NTT):
                lt.wait(v, i)
                o = 4 * (i % 2)
                v.scalar_tensor_tensor(CSQ[:, i, :], bt[o + 0][:], 8.0, qbc[:, 0, :], ALU.mult, ALU.mult)
                v.scalar_tensor_tensor(SSQ[:, i, :], bt[o + 1][:], 8.0, qbc[:, 1, :], ALU.mult, ALU.mult)
                v.scalar_tensor_tensor(CSK[:, i, :], bt[o + 2][:], 1.0, qbc[:, 2, :], ALU.mult, ALU.mult)
                v.scalar_tensor_tensor(SSK[:, i, :], bt[o + 3][:], 1.0, qbc[:, 3, :], ALU.mult,
                                       ALU.mult).then_inc(ctx.s("tb" + tag), 1)
                ctx.inc("tb" + tag, 1)
            for i in range(NTT):
                b = i % 2
                for cg in range(4):
                    k = (i * 6 + cg) % 3
                    n = i * 4 + cg
                    CS = CSQ if cg < 2 else CSK
                    SS = SSQ if cg < 2 else SSK
                    v.wait_ge(ctx.s("s1" + tag), b0["s1"] + n + 1)
                    if n >= 2:
                        v.wait_ge(ctx.s("s2" + tag), b0["s2"] + n - 1)
                    v.reduce_sum(ssA[n % 2][:], hview(sqst[n % 2].ap()), AX.X).then_inc(ctx.s("va" + tag), 1)
                    ctx.inc("va" + tag, 1)
                    v.wait_ge(ctx.s("s2" + tag), b0["s2"] + n + 1)
                    v.tensor_scalar(rs_t[:], ssB[n % 2][:], 1.0e12, None, ALU.min)
                    cst = CS[:, i, :].rearrange("p (o b) -> p o b", o=1).broadcast_to((128, 8, 64))
                    sst = SS[:, i, :].rearrange("p (o b) -> p o b", o=1)
                    pqv = hview(pq[k].ap())
                    v.tensor_tensor(hview(st2.ap()), pqv, cst, ALU.mult)
                    v.tensor_tensor(hview(st3.ap())[:, :, 0:32], pqv[:, :, 32:64],
                                    sst[:, :, 0:32].broadcast_to((128, 8, 32)), ALU.mult)
                    v.tensor_tensor(hview(st3.ap())[:, :, 32:64], pqv[:, :, 0:32],
                                    sst[:, :, 32:64].broadcast_to((128, 8, 32)), ALU.mult)
                    v.tensor_tensor(st2[:], st2[:], st3[:], ALU.add)
                    if n >= 2:
                        v.wait_ge(ctx.s("tp" + tag), b0["tp"] + n - 1)
                    v.tensor_tensor(hview(q16[n % 2].ap()), hview(st2.ap()),
                                    rs_t.ap().rearrange("p (a o) -> p a o", o=1).broadcast_to((128, 8, 64)),
                                    ALU.mult).then_inc(ctx.s("qe" + tag), 1)
                    ctx.inc("qe" + tag, 1)
                    v.wait_ge(ctx.s("tp" + tag), b0["tp"] + n + 1)
                    if n >= 2:
                        sq_.wait(v, n - 2)
                    v.tensor_copy(qts[n % 2][:], psT[n % 2][:]).then_inc(ctx.s("pe" + tag), 1)
                    ctx.inc("pe" + tag, 1)
                for vc in range(2):
                    k = (i * 6 + 4 + vc) % 3
                    v.wait_ge(ctx.s("mm" + tag), b0["mm"] + i * 6 + 4 + vc + 1)
                    if i >= 2:
                        sv.wait(v, i - 2)
                    v.tensor_copy(
                        vaug[b].ap().rearrange("p (h c) -> p h c", c=128)[:, tsl(vc, 8), 0:64],
                        pq[k].ap().rearrange("p (h c) -> p h c", c=64)).then_inc(ctx.s("ve" + tag), 1)
                    ctx.inc("ve" + tag, 1)
            return ("pe" + tag, ctx.val("pe" + tag))

        ph.gpsimd(f_gpsimd)
        ph.sync(f_sync)
        ph.tensor(f_tensor)
        ph.scalar(f_scalar)
        ph.vector(f_vector)
        ph.emit()


def p_ffn_attn(ctx, wo_d, qsr, ksr, QTl, KTl, VAUGl, Z32, U32, U16,
               consts, tag):
    nc = ctx.nc
    NBLK = ctx.NBLK
    NP = 8
    for n in ["mm", "mk", "xp", "pv", "dv", "wm", "ev", "vv", "cb"]:
        ctx.sem(n + tag)
    b0 = {n: ctx.val(n + tag) for n in ["mm", "mk", "xp", "pv", "dv", "wm", "ev", "vv", "cb"]}
    onesb = consts["onesb"]
    lmask_d = consts["lmask_d"]
    wo_r = wo_d.rearrange("(a p) c -> a p c", p=128)
    with contextlib.ExitStack() as es:
        lmask = es.enter_context(nc.sbuf_tensor("lmsk" + tag, [128, 4, 256], F32))
        wo = es.enter_context(nc.sbuf_tensor("wo" + tag, [128, NB, D], BF16))
        qtb = [es.enter_context(nc.sbuf_tensor(f"qt{tag}{i}", [128, NB, 256], BF16)) for i in range(2)]
        ktb = [es.enter_context(nc.sbuf_tensor(f"kt{tag}{i}", [128, NB, 512], BF16)) for i in range(2)]
        vtb = [es.enter_context(nc.sbuf_tensor(f"vt{tag}{i}", [128, 4, 2048], BF16)) for i in range(2)]
        zrb = [es.enter_context(nc.sbuf_tensor(f"zr{tag}{i}", [128, NB, 256], F32)) for i in range(2)]
        pS16 = [[es.enter_context(nc.sbuf_tensor(f"pp{tag}{i}{hi}", [128, 4, 256], BF16))
                 for hi in range(2)] for i in range(2)]
        rl = es.enter_context(nc.sbuf_tensor("rl" + tag, [64, 256], F32))
        osb = [es.enter_context(nc.sbuf_tensor(f"ob{tag}{i}", [128, NB, 256], BF16)) for i in range(2)]
        a32 = [es.enter_context(nc.sbuf_tensor(f"a3{tag}{i}", [128, 256], F32)) for i in range(2)]
        o32 = [es.enter_context(nc.sbuf_tensor(f"o3{tag}{i}", [128, 256], F32)) for i in range(2)]
        o16 = [es.enter_context(nc.sbuf_tensor(f"o6{tag}{i}", [128, 256], BF16)) for i in range(2)]
        crow = es.enter_context(nc.sbuf_tensor("cr" + tag, [1, 2, 64], F32))
        cmx = es.enter_context(nc.sbuf_tensor("cm" + tag, [1, 64], F32))
        cneg = es.enter_context(nc.sbuf_tensor("cn" + tag, [1, 1], BF16))
        cbx = es.enter_context(nc.sbuf_tensor("cx" + tag, [128, 1], F32))
        psS = [es.enter_context(nc.psum_tensor(f"pS{tag}{i}", [128, 4, 256], F32)) for i in range(2)]
        psO = [es.enter_context(nc.psum_tensor(f"pO{tag}{i}", [128, 256], F32)) for i in range(2)]
        psW = [es.enter_context(nc.psum_tensor(f"pW{tag}{i}", [128, 256], F32)) for i in range(2)]
        wl = Rot(ctx, "wl" + tag, 1, n_per=NB, swdge=True)
        lq = Rot(ctx, "lb" + tag, 2, n_per=NB)
        lk = Rot(ctx, "lc" + tag, 2, n_per=NB)
        lv = Rot(ctx, "ld" + tag, 2, n_per=4)
        lz = Rot(ctx, "le" + tag, 2, n_per=NB)
        lcr = Rot(ctx, "lf" + tag, 1, n_per=2)
        lmk = Rot(ctx, "lm" + tag, 1)
        so = Rot(ctx, "sg" + tag, 2, n_per=2)
        ph = ctx.phase("fatt" + tag)

        def f_gpsimd(g):
            for j in range(NB):
                wl.issue(0, g.dma_start(out=wo[:, j, :], in_=wo_r[j]))
            return wl.final()[-1]

        def f_sync(sy):
            lmk.issue(0, sy.dma_start(out=lmask[:], in_=lmask_d))
            lcr.issue(0, sy.dma_start(out=crow[0:1, 0, :], in_=qsr))
            lcr.issue(0, sy.dma_start(out=crow[0:1, 1, :], in_=ksr))
            for blk in range(NBLK):
                b = blk % 2
                if blk >= 2:
                    sy.wait_ge(ctx.s("dv" + tag), b0["dv"] + (blk - 1) * NP)
                    sy.wait_ge(ctx.s("vv" + tag), b0["vv"] + (blk - 1) * NB)
                for j in range(NB):
                    lq.issue(blk, sy.dma_start(out=qtb[b][:, j, :], in_=QTl[j, :, tsl(blk, 256)]))
                if blk == 0:
                    for j in range(NB):
                        lk.issue(blk, sy.dma_start(out=ktb[b][:, j, 256:512], in_=KTl[j, :, 0:256]))
                    for s_ in range(2):
                        lv.issue(blk, sy.dma_start(out=vtb[b][:, 2 + s_, :], in_=VAUGl[s_]))
                    for s_ in range(2):
                        lv.issue(blk, sy.dma_start(out=vtb[b][:, s_, 0:64], in_=VAUGl[0, :, 0:64]))
                else:
                    for j in range(NB):
                        lk.issue(blk, sy.dma_start(out=ktb[b][:, j, :],
                                                   in_=KTl[j, :, (blk - 1) * 256:(blk + 1) * 256]))
                    for s_ in range(4):
                        lv.issue(blk, sy.dma_start(out=vtb[b][:, s_, :], in_=VAUGl[2 * (blk - 1) + s_]))
                for j in range(NB):
                    lz.issue(blk, sy.dma_start(out=zrb[b][:, j, :], in_=Z32[j, :, tsl(blk, 256)]))
                if blk >= 1:
                    for co in range(NB):
                        w = (blk - 1) * NB + co
                        sy.wait_ge(ctx.s("vv" + tag), b0["vv"] + w + 1)
                        so.issue(w, sy.dma_start(out=U32[co, :, tsl(blk - 1, 256)], in_=o32[w % 2][:]))
                        so.issue(w, sy.dma_start(out=U16[co, :, tsl(blk - 1, 256)], in_=o16[w % 2][:]))
            for co in range(NB):
                w = (NBLK - 1) * NB + co
                sy.wait_ge(ctx.s("vv" + tag), b0["vv"] + w + 1)
                so.issue(w, sy.dma_start(out=U32[co, :, tsl(NBLK - 1, 256)], in_=o32[w % 2][:]))
                so.issue(w, sy.dma_start(out=U16[co, :, tsl(NBLK - 1, 256)], in_=o16[w % 2][:]))
            return so.final()

        def f_tensor(t):
            t.wait_ge(ctx.s("cb" + tag), b0["cb"] + 1)
            t.matmul(psW[0][:, 0:1], onesb[0:1, :], cneg[0:1, 0:1],
                     start=True, stop=True).then_inc(ctx.s("cb" + tag), 1)
            ctx.inc("cb" + tag, 1)
            wl.wait(t, 0)

            def PV_(PVp):
                t.wait_ge(ctx.s("xp" + tag), b0["xp"] + PVp + 1)
                if PVp >= 1:
                    t.wait_ge(ctx.s("dv" + tag), b0["dv"] + PVp)
                pb_, php = divmod(PVp, NP)
                pns0 = 2 if pb_ == 0 else 0
                pbb = pb_ % 2
                for hi in range(2):
                    h = 2 * php + hi
                    last = None
                    for s_ in range(pns0, 4):
                        last = t.matmul(psO[hi][:], vtb[pbb][:, s_, h * 128:h * 128 + 128],
                                        pS16[PVp % 2][hi][:, s_, :], start=(s_ == pns0), stop=(s_ == 3))
                last.then_inc(ctx.s("pv" + tag), 1)
                ctx.inc("pv" + tag, 1)

            def Wo_(wb_):
                t.wait_ge(ctx.s("dv" + tag), b0["dv"] + wb_ * NP + NP)
                if wb_ == 0:
                    t.wait_ge(ctx.s("cb" + tag), b0["cb"] + 3)
                for co in range(NB):
                    w = wb_ * NB + co
                    if w >= 2:
                        t.wait_ge(ctx.s("ev" + tag), b0["ev"] + w - 1)
                    last = None
                    for j in range(NB):
                        last = t.matmul(psW[w % 2][:], wo[:, j, tsl(co, 128)], osb[wb_ % 2][:, j, :],
                                        start=(j == 0), stop=(j == NB - 1))
                    last.then_inc(ctx.s("wm" + tag), 1)
                    ctx.inc("wm" + tag, 1)

            for blk in range(NBLK):
                b = blk % 2
                if blk >= 1:
                    PV_(blk * NP - 1)
                lq.wait(t, blk)
                lk.wait(t, blk)
                lv.wait(t, blk)
                ns0 = 2 if blk == 0 else 0
                for hp in range(NP):
                    p = blk * NP + hp
                    if p >= 1:
                        t.wait_ge(ctx.s("xp" + tag), b0["xp"] + p)
                    for hi in range(2):
                        h = 2 * hp + hi
                        hr = slice(hi * 64, hi * 64 + 64)
                        last = None
                        for s_ in range(ns0, 4):
                            last = t.matmul(psS[hi][:, s_, :], ktb[b][hr, hp, tsl(s_, 128)],
                                            qtb[b][hr, hp, :], start=True, stop=True,
                                            tile_position=(hi * 64, 0))
                        last.then_inc(ctx.s("mm" + tag), 1)
                        ctx.inc("mm" + tag, 1)
                    if hp >= 1:
                        PV_(p - 1)
                if blk >= 1:
                    Wo_(blk - 1)
            PV_(NBLK * NP - 1)
            Wo_(NBLK - 1)
            return ("wm" + tag, ctx.val("wm" + tag))

        def f_vector(v):
            lcr.wait(v, 0)
            v.tensor_tensor(cmx[:], crow[0:1, 0, :], crow[0:1, 1, :], ALU.mult)
            v.reduce_max(cmx[:, 0:1], cmx[:], AX.X)
            v.tensor_scalar(cneg[:], cmx[:, 0:1], -8.0, None, ALU.mult).then_inc(ctx.s("cb" + tag), 1)
            ctx.inc("cb" + tag, 1)
            v.wait_ge(ctx.s("cb" + tag), b0["cb"] + 2)
            v.tensor_copy(cbx[:], psW[0][:, 0:1]).then_inc(ctx.s("cb" + tag), 1)
            ctx.inc("cb" + tag, 1)
            lmk.wait(v, 0)

            def div_(p):
                db_, dhp = divmod(p, NP)
                v.wait_ge(ctx.s("pv" + tag), b0["pv"] + p + 1)
                if db_ >= 2 and dhp == 0:
                    v.wait_ge(ctx.s("wm" + tag), b0["wm"] + (db_ - 1) * NB)
                for hi in range(2):
                    v.reciprocal(rl[:], psO[hi][64:128, :])
                    v.tensor_tensor(osb[db_ % 2][hi * 64:hi * 64 + 64, dhp, :],
                                    psO[hi][0:64, :], rl[:], ALU.mult)
                vfence(v, consts).then_inc(ctx.s("dv" + tag), 1)
                ctx.inc("dv" + tag, 1)

            def WoAdd_(wb_):
                for co in range(NB):
                    w = wb_ * NB + co
                    v.wait_ge(ctx.s("ev" + tag), b0["ev"] + w + 1)
                    lz.wait(v, wb_)
                    if w >= 2:
                        so.wait(v, w - 2)
                    v.tensor_tensor(o32[w % 2][:], a32[w % 2][:], zrb[wb_ % 2][:, co, :], ALU.add)
                    v.tensor_copy(o16[w % 2][:], o32[w % 2][:])
                    vfence(v, consts).then_inc(ctx.s("vv" + tag), 1)
                    ctx.inc("vv" + tag, 1)

            for blk in range(NBLK):
                ns0 = 2 if blk == 0 else 0
                if blk >= 1:
                    div_(blk * NP - 1)
                for hp in range(NP):
                    p = blk * NP + hp
                    v.wait_ge(ctx.s("mm" + tag), b0["mm"] + 2 * (p + 1))
                    for hi in range(2):
                        v.tensor_tensor(psS[hi][:, ns0:4, :], psS[hi][:, ns0:4, :],
                                        lmask[:, ns0:4, :], ALU.add)
                    vfence(v, consts).then_inc(ctx.s("mk" + tag), 1)
                    ctx.inc("mk" + tag, 1)
                    if hp >= 1:
                        div_(p - 1)
                if blk >= 1:
                    WoAdd_(blk - 1)
            div_(NBLK * NP - 1)
            WoAdd_(NBLK - 1)
            return ("vv" + tag, ctx.val("vv" + tag))

        def f_scalar(s):
            s.wait_ge(ctx.s("cb" + tag), b0["cb"] + 3)

            def Evac_(wb_):
                for co in range(NB):
                    w = wb_ * NB + co
                    s.wait_ge(ctx.s("wm" + tag), b0["wm"] + w + 1)
                    if w >= 2:
                        s.wait_ge(ctx.s("vv" + tag), b0["vv"] + w - 1)
                    s.activation(a32[w % 2][:], psW[w % 2][:], AF.Identity,
                                 bias=consts["scal"][:, 3:4])
                    sfence(s, consts).then_inc(ctx.s("ev" + tag), 1)
                    ctx.inc("ev" + tag, 1)

            for blk in range(NBLK):
                ns0 = 2 if blk == 0 else 0
                for hp in range(NP):
                    p = blk * NP + hp
                    s.wait_ge(ctx.s("mk" + tag), b0["mk"] + p + 1)
                    if p >= 2:
                        s.wait_ge(ctx.s("pv" + tag), b0["pv"] + p - 1)
                    for hi in range(2):
                        s.activation(pS16[p % 2][hi][:, ns0:4, :], psS[hi][:, ns0:4, :], AF.Exp,
                                     bias=cbx[:, 0:1])
                    sfence(s, consts).then_inc(ctx.s("xp" + tag), 1)
                    ctx.inc("xp" + tag, 1)
                if blk >= 1:
                    Evac_(blk - 1)
            Evac_(NBLK - 1)
            return ("ev" + tag, ctx.val("ev" + tag))

        ph.gpsimd(f_gpsimd)
        ph.sync(f_sync)
        ph.tensor(f_tensor)
        ph.vector(f_vector)
        ph.scalar(f_scalar)
        ph.emit()


def p_ffn_geglu(ctx, w1_d, w2_d, b1_sb, b2_sb, USH16, U32, Z32n, Z16n,
                consts, tag):
    nc = ctx.nc
    NT = ctx.NT
    NK = 22
    CW = [128] * 21 + [42]
    for n in ["ma", "ev", "hh", "mw", "e2", "vv"]:
        ctx.sem(n + tag)
    b0 = {n: ctx.val(n + tag) for n in ["ma", "ev", "hh", "mw", "e2", "vv"]}
    w1_r = w1_d.rearrange("(a p) c -> a p c", p=128)
    with contextlib.ExitStack() as es:
        w1 = es.enter_context(nc.sbuf_tensor("w1" + tag, [128, NB, 5460], BF16))
        w2 = es.enter_context(nc.sbuf_tensor("w2" + tag, [128, NK, D], BF16))
        ush = [es.enter_context(nc.sbuf_tensor(f"us{tag}{i}", [128, NB, 512], BF16)) for i in range(2)]
        ur = [es.enter_context(nc.sbuf_tensor(f"ur{tag}{i}", [128, 512], F32)) for i in range(2)]
        h16 = es.enter_context(nc.sbuf_tensor("h6" + tag, [128, NK, 512], BF16))
        ga32 = [es.enter_context(nc.sbuf_tensor(f"ga{tag}{i}", [128, 512], F32)) for i in range(2)]
        gg32 = [es.enter_context(nc.sbuf_tensor(f"gg{tag}{i}", [128, 512], F32)) for i in range(2)]
        z32 = [es.enter_context(nc.sbuf_tensor(f"z3{tag}{i}", [128, 512], F32)) for i in range(2)]
        z16 = [es.enter_context(nc.sbuf_tensor(f"z6{tag}{i}", [128, 512], BF16)) for i in range(2)]
        a2 = [es.enter_context(nc.sbuf_tensor(f"a2{tag}{i}", [128, 512], F32)) for i in range(2)]
        psA = [es.enter_context(nc.psum_tensor(f"pA{tag}{i}", [128, 512], F32)) for i in range(2)]
        psG = [es.enter_context(nc.psum_tensor(f"pG{tag}{i}", [128, 512], F32)) for i in range(2)]
        psW = [es.enter_context(nc.psum_tensor(f"pw{tag}{i}", [128, 512], F32)) for i in range(2)]
        wl = Rot(ctx, "wl" + tag, 1, n_per=NB + NK, swdge=True)
        lu = Rot(ctx, "lu" + tag, 2, n_per=NB)
        rr = Rot(ctx, "rr" + tag, 2, n_per=1)
        so = Rot(ctx, "sh" + tag, 2, n_per=2)
        ph = ctx.phase("geglu" + tag)

        def f_gpsimd(g):
            for j in range(NB):
                wl.issue(0, g.dma_start(out=w1[:, j, :], in_=w1_r[j]))
            for k in range(NK):
                wl.issue(0, g.dma_start(out=w2[0:CW[k], k, :], in_=w2_d[128 * k:128 * k + CW[k], :]))
            return wl.final()[-1]

        def f_sync2(sy):
            for tt in range(NT):
                b = tt % 2
                if tt >= 2:
                    sy.wait_ge(ctx.s("ma" + tag), b0["ma"] + (tt - 1) * 2 * NK)
                for j in range(NB):
                    lu.issue(tt, sy.dma_start(out=ush[b][:, j, :], in_=USH16[j, :, tsl(tt, 512)]))
                for co in range(NB):
                    w = tt * NB + co
                    if w >= 2:
                        sy.wait_ge(ctx.s("vv" + tag), b0["vv"] + w - 1)
                    rr.issue(w, sy.dma_start(out=ur[w % 2][:], in_=U32[co, :, tsl(tt, 512)]))
                    sy.wait_ge(ctx.s("vv" + tag), b0["vv"] + w + 1)
                    so.issue(w, sy.dma_start(out=Z32n[co, :, tsl(tt, 512)], in_=z32[w % 2][:]))
                    so.issue(w, sy.dma_start(out=Z16n[co, :, tsl(tt, 512)], in_=z16[w % 2][:]))
            return so.final()

        def f_tensor(t):
            wl.wait(t, 0)
            for tt in range(NT):
                b = tt % 2
                lu.wait(t, tt)
                for k in range(NK):
                    n = tt * NK + k
                    if n >= 2:
                        t.wait_ge(ctx.s("ev" + tag), b0["ev"] + n - 1)
                    last = None
                    for j in range(NB):
                        last = t.matmul(psA[n % 2][0:CW[k], :], w1[:, j, 128 * k:128 * k + CW[k]],
                                        ush[b][:, j, :], start=(j == 0), stop=(j == NB - 1))
                    last.then_inc(ctx.s("ma" + tag), 1)
                    ctx.inc("ma" + tag, 1)
                    last = None
                    for j in range(NB):
                        last = t.matmul(psG[n % 2][0:CW[k], :], w1[:, j, 2730 + 128 * k:2730 + 128 * k + CW[k]],
                                        ush[b][:, j, :], start=(j == 0), stop=(j == NB - 1))
                    last.then_inc(ctx.s("ma" + tag), 1)
                    ctx.inc("ma" + tag, 1)
                t.wait_ge(ctx.s("hh" + tag), b0["hh"] + (tt + 1) * NK)
                for co in range(NB):
                    w = tt * NB + co
                    if w >= 2:
                        t.wait_ge(ctx.s("e2" + tag), b0["e2"] + w - 1)
                    last = None
                    for k in range(NK):
                        last = t.matmul(psW[w % 2][:], w2[0:CW[k], k, tsl(co, 128)], h16[0:CW[k], k, :],
                                        start=(k == 0), stop=(k == NK - 1))
                    last.then_inc(ctx.s("mw" + tag), 1)
                    ctx.inc("mw" + tag, 1)
            return ("mw" + tag, ctx.val("mw" + tag))

        def f_scalar(s):
            for tt in range(NT):
                for k in range(NK):
                    n = tt * NK + k
                    s.wait_ge(ctx.s("ma" + tag), b0["ma"] + 2 * n + 2)
                    if n >= 2:
                        s.wait_ge(ctx.s("hh" + tag), b0["hh"] + n - 1)
                    s.activation(ga32[n % 2][0:CW[k], :], psA[n % 2][0:CW[k], :], AF.Identity,
                                 bias=b1_sb[0:CW[k], k:k + 1])
                    s.activation(gg32[n % 2][0:CW[k], :], psG[n % 2][0:CW[k], :], AF.Gelu,
                                 bias=b1_sb[0:CW[k], 22 + k:23 + k]).then_inc(ctx.s("ev" + tag), 1)
                    ctx.inc("ev" + tag, 1)
                for co in range(NB):
                    w = tt * NB + co
                    s.wait_ge(ctx.s("mw" + tag), b0["mw"] + w + 1)
                    if w >= 2:
                        s.wait_ge(ctx.s("vv" + tag), b0["vv"] + w - 1)
                    s.activation(a2[w % 2][:], psW[w % 2][:], AF.Identity,
                                 bias=b2_sb[:, co:co + 1]).then_inc(ctx.s("e2" + tag), 1)
                    ctx.inc("e2" + tag, 1)
            return ("e2" + tag, ctx.val("e2" + tag))

        def f_vector(v):
            for tt in range(NT):
                for k in range(NK):
                    n = tt * NK + k
                    v.wait_ge(ctx.s("ev" + tag), b0["ev"] + n + 1)
                    if tt >= 1 and k == 0:
                        v.wait_ge(ctx.s("mw" + tag), b0["mw"] + tt * NB)
                    v.tensor_tensor(h16[0:CW[k], k, :], ga32[n % 2][0:CW[k], :], gg32[n % 2][0:CW[k], :],
                                    ALU.mult).then_inc(ctx.s("hh" + tag), 1)
                    ctx.inc("hh" + tag, 1)
                for co in range(NB):
                    w = tt * NB + co
                    v.wait_ge(ctx.s("e2" + tag), b0["e2"] + w + 1)
                    rr.wait(v, w)
                    if w >= 2:
                        so.wait(v, w - 2)
                    v.tensor_tensor(z32[w % 2][:], a2[w % 2][:], ur[w % 2][:], ALU.add)
                    v.tensor_copy(z16[w % 2][:], z32[w % 2][:]).then_inc(ctx.s("vv" + tag), 1)
                    ctx.inc("vv" + tag, 1)
            return ("vv" + tag, ctx.val("vv" + tag))

        ph.gpsimd(f_gpsimd)
        ph.sync(f_sync2)
        ph.tensor(f_tensor)
        ph.scalar(f_scalar)
        ph.vector(f_vector)
        ph.emit()


def p_final(ctx, x_nat, S32, Z32, OUT, consts, tag="fz"):
    nc = ctx.nc
    NT = ctx.NT
    for n in ["sg", "gm", "tp", "ev", "ad"]:
        ctx.sem(n + tag)
    b0 = {n: ctx.val(n + tag) for n in ["sg", "gm", "tp", "ev", "ad"]}
    x_r = x_nat.rearrange("(n p) d -> n p d", p=128)
    out_r = OUT.rearrange("(n p) d -> n p d", p=128)
    idf = consts["idf"]
    with contextlib.ExitStack() as es:
        sb = [es.enter_context(nc.sbuf_tensor(f"sb{tag}{i}", [128, NB, 512], F32)) for i in range(2)]
        zb = [es.enter_context(nc.sbuf_tensor(f"zb{tag}{i}", [128, NB, 512], F32)) for i in range(2)]
        gb = es.enter_context(nc.sbuf_tensor("gb" + tag, [128, NB, 512], F32))
        xb = [es.enter_context(nc.sbuf_tensor(f"xb{tag}{i}", [128, D], F32)) for i in range(2)]
        ob = [es.enter_context(nc.sbuf_tensor(f"ob{tag}{i}", [128, D], F32)) for i in range(2)]
        pst = es.enter_context(nc.psum_tensor("pt" + tag, [128, D], F32))
        ls = Rot(ctx, "ls" + tag, 2, n_per=2 * NB)
        lx = Rot(ctx, "lx" + tag, 2, n_per=1)
        st = Rot(ctx, "st" + tag, 2, n_per=1)
        ph = ctx.phase("final" + tag)

        def f_sync(sy):
            for tt in range(NT):
                b = tt % 2
                if tt >= 2:
                    sy.wait_ge(ctx.s("tp" + tag), b0["tp"] + (tt - 1) * 4)
                    sy.wait_ge(ctx.s("sg" + tag), b0["sg"] + tt - 1)
                for j in range(NB):
                    ls.issue(tt, sy.dma_start(out=sb[b][:, j, :], in_=S32[j, :, tsl(tt, 512)]))
                    ls.issue(tt, sy.dma_start(out=zb[b][:, j, :], in_=Z32[j, :, tsl(tt, 512)]))
                for c in range(4):
                    i = tt * 4 + c
                    if i >= 2:
                        sy.wait_ge(ctx.s("ad" + tag), b0["ad"] + i - 1)
                    lx.issue(i, sy.dma_start(out=xb[i % 2][:], in_=x_r[i]))
                    sy.wait_ge(ctx.s("ad" + tag), b0["ad"] + i + 1)
                    st.issue(i, sy.dma_start(out=out_r[i], in_=ob[i % 2][:]))
            return st.final()

        def f_scalar(s):
            for tt in range(NT):
                b = tt % 2
                ls.wait(s, tt)
                if tt >= 1:
                    s.wait_ge(ctx.s("gm" + tag), b0["gm"] + tt)
                s.activation(gb[:].rearrange("p a t -> p (a t)"), zb[b][:].rearrange("p a t -> p (a t)"),
                             AF.Silu).then_inc(ctx.s("sg" + tag), 1)
                ctx.inc("sg" + tag, 1)
            return ("sg" + tag, ctx.val("sg" + tag))

        def f_gpsimd(g):
            for tt in range(NT):
                b = tt % 2
                g.wait_ge(ctx.s("sg" + tag), b0["sg"] + tt + 1)
                g.tensor_mul(sb[b][:].rearrange("p a t -> p (a t)"),
                             sb[b][:].rearrange("p a t -> p (a t)"),
                             gb[:].rearrange("p a t -> p (a t)")).then_inc(ctx.s("gm" + tag), 1)
                ctx.inc("gm" + tag, 1)
            return ("gm" + tag, ctx.val("gm" + tag))

        def f_tensor(t):
            for tt in range(NT):
                b = tt % 2
                t.wait_ge(ctx.s("gm" + tag), b0["gm"] + tt + 1)
                for c in range(4):
                    i = tt * 4 + c
                    if i >= 1:
                        t.wait_ge(ctx.s("ev" + tag), b0["ev"] + i)
                    last = None
                    for j in range(NB):
                        last = t.transpose(pst[:, tsl(j, 128)], sb[b][:, j, tsl(c, 128)], idf[:])
                    last.then_inc(ctx.s("tp" + tag), 1)
                    ctx.inc("tp" + tag, 1)
            return ("tp" + tag, ctx.val("tp" + tag))

        def f_vector(v):
            for tt in range(NT):
                for c in range(4):
                    i = tt * 4 + c
                    v.wait_ge(ctx.s("tp" + tag), b0["tp"] + i + 1)
                    v.tensor_copy(ob[i % 2][:], pst[:]).then_inc(ctx.s("ev" + tag), 1)
                    ctx.inc("ev" + tag, 1)
                    lx.wait(v, i)
                    if i >= 2:
                        st.wait(v, i - 2)
                    v.tensor_tensor(ob[i % 2][:], ob[i % 2][:], xb[i % 2][:], ALU.add).then_inc(ctx.s("ad" + tag), 1)
                    ctx.inc("ad" + tag, 1)
            return ("ad" + tag, ctx.val("ad" + tag))

        ph.sync(f_sync)
        ph.scalar(f_scalar)
        ph.gpsimd(f_gpsimd)
        ph.tensor(f_tensor)
        ph.vector(f_vector)
        ph.emit()


import ml_dtypes


def build_full(S=4096, race=False, dbg=False):
    nc = bass.Bass("TRN2", target_bir_lowering=False, detect_race_conditions=race)
    NTT = S // 128
    with contextlib.ExitStack() as stack:
        ctx = Ctx(nc, stack, S)
        DT = nc.dram_tensor
        x = DT("x", [S, 1024], F32, kind="ExternalInput").ap()
        qkvw = DT("qkv_w", [1024, 3072], F32, kind="ExternalInput").ap()
        outw = DT("out_w", [1024, 1024], F32, kind="ExternalInput").ap()
        wctx_d = DT("w_ctx", [1024, 1024], F32, kind="ExternalInput").ap()
        wx_d = DT("w_x", [1024, 1024], F32, kind="ExternalInput").ap()
        ffw1_d = DT("ff_w1", [1024, 4096], F32, kind="ExternalInput").ap()
        ffw2_d = DT("ff_w2", [4096, 1024], F32, kind="ExternalInput").ap()
        Wqkv = DT("fWqkv", [6, 1024, 3072], F32, kind="ExternalInput").ap()
        Wo = DT("fWo", [6, 1024, 1024], F32, kind="ExternalInput").ap()
        W1 = DT("fW1", [6, 1024, 5460], F32, kind="ExternalInput").ap()
        W2 = DT("fW2", [6, 2730, 1024], F32, kind="ExternalInput").ap()
        ins_small = {}
        for nm, shp in [("P_qkvb", [128, 24]), ("P_qkvb_row", [1, 3072]),
                        ("P_outb", [128, 8]), ("P_gattn", [128, 8]), ("P_battn", [128, 8]),
                        ("P_g1", [128, 8]), ("P_b1", [128, 8]), ("P_g2", [128, 8]),
                        ("P_b2", [128, 8]), ("P_binl", [128, 8]),
                        ("P_ffb1", [128, 32]), ("P_ffb2", [128, 8]),
                        ("P_qs", [6, 64]), ("P_ks", [6, 64]), ("P_qbc", [6, 128, 4, 64]),
                        ("P_lng", [6, 128, 8]), ("P_lnb", [6, 128, 8]),
                        ("P_fb1", [6, 128, 44]), ("P_fb2", [6, 128, 8]),
                        ("C_idf", [128, 128]), ("C_scal", [128, 8]),
                        ("C_cmask", [128, 4, 512]), ("C_lmask", [128, 4, 256]),
                        ("C_rqc", [NTT, 128, 64]), ("C_rqs", [NTT, 128, 64]),
                        ("C_rkc", [NTT, 128, 64]), ("C_rks", [NTT, 128, 64])]:
            ins_small[nm] = DT(nm, shp, F32, kind="ExternalInput").ap()
        for nm, shp in [("C_idb", [128, 128]), ("C_onesb", [128, 128])]:
            ins_small[nm] = DT(nm, shp, BF16, kind="ExternalInput").ap()
        OUT = DT("OUT", [S, 1024], F32, kind="ExternalOutput").ap()

        ST = lambda nm, f32=True: DT(nm, [NB, 128, S], F32 if f32 else BF16,
                                     kind=("ExternalOutput" if dbg else "Internal")).ap()
        XT32, XT16 = ST("XT32"), ST("XT16", False)
        XN32, XN16 = ST("XN32"), ST("XN16", False)
        QT, KT = ST("QT", False), ST("KT", False)
        VAC = DT("VAC", [NTT, 128, 2048], BF16, kind="Internal").ap()
        OTC = ST("OTC", False)
        CTXB = ST("CTXB", False)
        Y1, Y1B = ST("Y1"), ST("Y1B", False)
        XS0A, XS0B = ST("XS0A"), ST("XS0B", False)
        Y2, Y2B = ST("Y2"), ST("Y2B", False)
        HN16 = ST("HN16", False)
        H16 = DT("H16", [32, 128, S], BF16, kind="Internal").ap()
        S32 = ST("S32")
        ZA32, ZA16 = ST("ZA32"), ST("ZA16", False)
        ZB32, ZB16 = ST("ZB32"), ST("ZB16", False)
        U32, U16 = ST("U32"), ST("U16", False)
        USH16 = ST("USH16", False)
        QTl, KTl = ST("QTl", False), ST("KTl", False)
        VAUGl = DT("VAUGl", [NTT, 128, 2048], BF16, kind="Internal").ap()

        SB = {}
        for nm, shp, dt_ in [("idf", [128, 128], F32), ("idb", [128, 128], BF16),
                             ("onesb", [128, 128], BF16), ("scal", [128, 8], F32),
                             ("qkvb", [128, 24], F32), ("qkvbr", [1, 3072], F32),
                             ("outb", [128, 8], F32), ("ga", [128, 8], F32), ("ba", [128, 8], F32),
                             ("g1", [128, 8], F32), ("b1", [128, 8], F32),
                             ("g2", [128, 8], F32), ("b2", [128, 8], F32), ("binl", [128, 8], F32),
                             ("ffb1", [128, 32], F32), ("ffb2", [128, 8], F32),
                             ("lng", [128, 6, 8], F32), ("lnb", [128, 6, 8], F32),
                             ("fb1", [128, 6, 44], F32), ("fb2", [128, 6, 8], F32),
                             ("fence", [1, 4], F32)]:
            SB[nm] = stack.enter_context(nc.sbuf_tensor(nm, shp, dt_))
        consts = {"idf": SB["idf"], "idb": SB["idb"], "onesb": SB["onesb"],
                  "scal": SB["scal"], "cmask_d": ins_small["C_cmask"],
                  "lmask_d": ins_small["C_lmask"], "qkvb_row": SB["qkvbr"],
                  "fence": SB["fence"]}

        ctx.sem("cl")
        ph = ctx.phase("consts")
        def f_sync(sy):
            pairs = [(SB["idf"][:], ins_small["C_idf"]), (SB["idb"][:], ins_small["C_idb"]),
                     (SB["onesb"][:], ins_small["C_onesb"]), (SB["scal"][:], ins_small["C_scal"]),
                     (SB["qkvb"][:], ins_small["P_qkvb"]), (SB["qkvbr"][:], ins_small["P_qkvb_row"]),
                     (SB["outb"][:], ins_small["P_outb"]), (SB["ga"][:], ins_small["P_gattn"]),
                     (SB["ba"][:], ins_small["P_battn"]), (SB["g1"][:], ins_small["P_g1"]),
                     (SB["b1"][:], ins_small["P_b1"]), (SB["g2"][:], ins_small["P_g2"]),
                     (SB["b2"][:], ins_small["P_b2"]), (SB["binl"][:], ins_small["P_binl"]),
                     (SB["ffb1"][:], ins_small["P_ffb1"]), (SB["ffb2"][:], ins_small["P_ffb2"])]
            for li in range(6):
                pairs += [(SB["lng"][:, li, :], ins_small["P_lng"][li]),
                          (SB["lnb"][:, li, :], ins_small["P_lnb"][li]),
                          (SB["fb1"][:, li, :], ins_small["P_fb1"][li]),
                          (SB["fb2"][:, li, :], ins_small["P_fb2"][li])]
            for dst, src in pairs:
                sy.dma_start(out=dst, in_=src).then_inc(ctx.s("cl"), 16)
                ctx.inc("cl", 16)
            return ("cl", ctx.val("cl"))
        ph.sync(f_sync)
        ph.emit()

        p_nat_to_t(ctx, x, XT32, XT16, consts, "A")
        p_ln_t(ctx, XT32, XT16, None, None, XN32, XN16, consts, "B")
        with contextlib.ExitStack() as s2:
            xn2b = s2.enter_context(nc.sbuf_tensor("xn2b", [128, NB, S], BF16))
            p_ln_t(ctx, XN32, XN16, SB["ga"], SB["ba"], None, None, consts, "C", out16_sb=xn2b)
            p_attn_qkv(ctx, qkvw, SB["qkvb"], xn2b, QT, KT, VAC, consts, "D")
        p_causal_attn(ctx, QT, KT, VAC, OTC, consts, "E")
        p_proj_t(ctx, outw, OTC, SB["outb"], CTXB, XN32, Y1, Y1B, consts, "F")
        p_ln_t(ctx, Y1, Y1B, SB["g1"], SB["b1"], XS0A, XS0B, consts, "G")
        p_scan(ctx, wctx_d, wx_d, SB["binl"], CTXB, XS0A, XS0B, Y1, Y2, Y2B, consts, "H")
        p_ln_t(ctx, Y2, Y2B, SB["g2"], SB["b2"], None, HN16, consts, "I")
        p_proj_t(ctx, ffw1_d, HN16, SB["ffb1"], H16, None, None, None, consts, "J",
                 nbout=32, act=AF.Gelu)
        p_proj_t(ctx, ffw2_d, H16, SB["ffb2"], None, Y2, S32, None, consts, "K", nbin=32)

        if dbg:
            ZDBG = [ST("ZD%d" % li) for li in range(6)]
            UDBG = [ST("UD%d" % li) for li in range(6)]
            QDBG = [ST("QD%d" % li, False) for li in range(6)]
            KDBG = [ST("KD%d" % li, False) for li in range(6)]
            VDBG = [DT("VD%d" % li, [NTT, 128, 2048], BF16, kind="ExternalOutput").ap() for li in range(6)]
        zin32, zin16 = XN32, XN16
        for li in range(6):
            zo32 = (ZDBG[li] if dbg else (ZA32 if li % 2 == 0 else ZB32))
            zo16 = ZA16 if li % 2 == 0 else ZB16
            p_ffn_qkv(ctx, Wqkv[li], ins_small["P_qbc"][li],
                      ins_small["C_rqc"], ins_small["C_rqs"], ins_small["C_rkc"], ins_small["C_rks"],
                      zin16, (QDBG[li] if dbg else QTl), (KDBG[li] if dbg else KTl),
                      (VDBG[li] if dbg else VAUGl), consts, "P%d" % li)
            p_ffn_attn(ctx, Wo[li], ins_small["P_qs"][li:li+1, :], ins_small["P_ks"][li:li+1, :],
                       (QDBG[li] if dbg else QTl), (KDBG[li] if dbg else KTl),
                       (VDBG[li] if dbg else VAUGl), zin32, (UDBG[li] if dbg else U32), U16, consts, "Q%d" % li)
            uu = UDBG[li] if dbg else U32
            p_ln_t(ctx, uu, U16, SB["lng"][:, li, :], SB["lnb"][:, li, :], None, USH16,
                   consts, "R%d" % li, shift=True)
            p_ffn_geglu(ctx, W1[li], W2[li], SB["fb1"][:, li, :], SB["fb2"][:, li, :],
                        USH16, uu, zo32, zo16, consts, "S%d" % li)
            zin32, zin16 = zo32, zo16

        p_final(ctx, x, S32, zin32, OUT, consts, "Z")
    return nc


def _host_inputs(S=4096):
    NTT = S // 128
    HDh = np.arange(0, 64, 2, dtype=np.float64)
    inv_freq = 1.0 / 10000.0 ** (HDh / 64)
    t = np.arange(S, dtype=np.float64)
    freqs = np.concatenate([t[:, None] * inv_freq[None, :]] * 2, 1)
    base_scale = (HDh + 0.4 * 64) / (1.4 * 64)
    power = (t - S // 2) / 128.0
    scale = np.concatenate([base_scale[None, :] ** power[:, None]] * 2, 1)
    sgn = np.where(np.arange(64) < 32, -1.0, 1.0)[None, :]
    C = {}
    C["C_rqc"] = (np.cos(freqs) * scale).astype(np.float32).reshape(NTT, 128, 64)
    C["C_rqs"] = (sgn * np.sin(freqs) * scale).astype(np.float32).reshape(NTT, 128, 64)
    C["C_rkc"] = (np.cos(freqs) / scale).astype(np.float32).reshape(NTT, 128, 64)
    C["C_rks"] = (sgn * np.sin(freqs) / scale).astype(np.float32).reshape(NTT, 128, 64)
    lm = np.zeros((128, 4, 256), np.float32)
    kk = np.arange(128)[:, None]; qq = np.arange(256)[None, :]
    for s_ in range(4):
        d = qq - kk + 256 - 128 * s_
        lm[:, s_, :] = np.where((d >= 0) & (d <= 256), 0.0, NEG)
    C["C_lmask"] = lm
    cm = np.zeros((128, 4, 512), np.float32)
    qq = np.arange(512)[None, :]
    for r in range(4):
        cm[:, r, :] = np.where(kk <= qq - 128 * r, 0.0, NEG)
    C["C_cmask"] = cm
    C["C_idf"] = np.eye(128, dtype=np.float32)
    C["C_idb"] = np.eye(128, dtype=ml_dtypes.bfloat16)
    C["C_onesb"] = np.ones((128, 128), dtype=ml_dtypes.bfloat16)
    sc = np.zeros((128, 8), np.float32)
    sc[:, 0] = 1e-5; sc[:, 1] = 1.0; sc[:, 4] = 1e-38
    C["C_scal"] = sc
    return C


def _layout_params(I):
    P = {}
    r8 = lambda a: np.ascontiguousarray(np.asarray(a, np.float32).reshape(8, 128).T)
    P["P_qkvb"] = np.ascontiguousarray(np.asarray(I["L0_attn_qkv_b"], np.float32).reshape(24, 128).T)
    P["P_qkvb_row"] = np.asarray(I["L0_attn_qkv_b"], np.float32).reshape(1, 3072)
    P["P_outb"] = r8(I["L0_attn_out_b"])
    P["P_gattn"] = r8(I["L0_attn_g_attn"]); P["P_battn"] = r8(I["L0_attn_b_attn"])
    P["P_g1"] = r8(I["L0_attn_g1"]); P["P_b1"] = r8(I["L0_attn_b1"])
    P["P_g2"] = r8(I["L0_attn_g2"]); P["P_b2"] = r8(I["L0_attn_b2"])
    P["P_binl"] = r8(I["L0_attn_b_inl"])
    P["P_ffb1"] = np.ascontiguousarray(np.asarray(I["L0_attn_ff_b1"], np.float32).reshape(32, 128).T)
    P["P_ffb2"] = r8(I["L0_attn_ff_b2"])
    qs = np.asarray(I["L0_ffn_q_scale"], np.float32)
    ks = np.asarray(I["L0_ffn_k_scale"], np.float32)
    perm = (np.arange(64) + 32) % 64
    P["P_qs"] = np.ascontiguousarray(qs); P["P_ks"] = np.ascontiguousarray(ks)
    rows = np.stack([qs, qs[:, perm], ks, ks[:, perm]], 1)
    P["P_qbc"] = np.ascontiguousarray(
        np.broadcast_to(rows[:, None, :, :], (6, 128, 4, 64)).astype(np.float32))
    P["P_lng"] = np.ascontiguousarray(np.asarray(I["L0_ffn_ln_g"], np.float32).reshape(6, 8, 128).transpose(0, 2, 1))
    P["P_lnb"] = np.ascontiguousarray(np.asarray(I["L0_ffn_ln_b"], np.float32).reshape(6, 8, 128).transpose(0, 2, 1))
    fb1 = np.zeros((6, 128, 44), np.float32)
    b1v = np.asarray(I["L0_ffn_b1"], np.float32)
    for li in range(6):
        for k in range(22):
            w = min(128, 2730 - 128 * k)
            fb1[li, 0:w, k] = b1v[li, 128 * k:128 * k + w]
            fb1[li, 0:w, 22 + k] = b1v[li, 2730 + 128 * k:2730 + 128 * k + w]
    P["P_fb1"] = fb1
    P["P_fb2"] = np.ascontiguousarray(np.asarray(I["L0_ffn_b2"], np.float32).reshape(6, 8, 128).transpose(0, 2, 1))
    return P


def kernel(**inputs):
    from concourse.bass_utils import run_bass_kernel_spmd
    S = 4096
    nc = build_full(S=S)
    C = _host_inputs(S)
    P = _layout_params(inputs)
    f32c = lambda a: np.ascontiguousarray(np.asarray(a, np.float32))
    shared = dict(C)
    shared.update(P)
    shared["qkv_w"] = f32c(inputs["L0_attn_qkv_w"])
    shared["out_w"] = f32c(inputs["L0_attn_out_w"])
    shared["w_ctx"] = f32c(inputs["L0_attn_w_ctx"])
    shared["w_x"] = f32c(inputs["L0_attn_w_x"])
    shared["ff_w1"] = f32c(inputs["L0_attn_ff_w1"])
    shared["ff_w2"] = f32c(inputs["L0_attn_ff_w2"])
    shared["fWqkv"] = f32c(inputs["L0_ffn_Wqkv"])
    shared["fWo"] = f32c(inputs["L0_ffn_Wo"])
    shared["fW1"] = f32c(inputs["L0_ffn_W1"])
    shared["fW2"] = f32c(inputs["L0_ffn_W2"])
    xfull = np.asarray(inputs["L0_attn_x"], np.float32)
    in_maps = []
    for b in range(8):
        m = dict(shared)
        m["x"] = np.ascontiguousarray(xfull[b])
        in_maps.append(m)
    res = run_bass_kernel_spmd(nc, in_maps, core_ids=list(range(8)))
    out = np.stack([np.asarray(res.results[b]["OUT"], np.float32) for b in range(8)], axis=0)
    return out
